# version 27
# baseline (speedup 1.0000x reference)
import numpy as np
import ml_dtypes

import concourse.bass as bass
import concourse.mybir as mybir
from concourse.bass_utils import run_bass_kernel_spmd

F32 = mybir.dt.float32
BF16 = mybir.dt.bfloat16
U8 = mybir.dt.uint8
AF = mybir.ActivationFunctionType
ALU = mybir.AluOpType
AXX = mybir.AxisListType.X
BF = ml_dtypes.bfloat16


class Cfg:
    def __init__(self, S=8192, D=2048, L=512, IS=8192, IR=1024, NE=8, C=8,
                 eps=1e-5, base=10000.0):
        self.S, self.D, self.L, self.IS, self.IR, self.NE, self.C = S, D, L, IS, IR, NE, C
        self.eps, self.base = eps, base
        self.T = S // C
        self.QT = self.T // 128
        self.DC = D // 128
        self.LC = L // 128
        self.NH = D // 128
        self.NC2 = self.DC // 2
        self.JH = (D // 4) // 128
        self.KC = S // 128
        self.ISC = IS // 128
        self.IRC = IR // 128
        self.IG = 4
        self.NF = max(1, self.T // 512)
        self.NW = min(512, self.T)


CFG = Cfg()


def _groups(nchunks, gsz):
    out, i = [], 0
    while i < nchunks:
        out.append((i, min(gsz, nchunks - i)))
        i += gsz
    return out


def build(cfg):
    S, D, L, T = cfg.S, cfg.D, cfg.L, cfg.T
    DC, LC, QT, NH, KC = cfg.DC, cfg.LC, cfg.QT, cfg.NH, cfg.KC
    JH, NC2 = cfg.JH, cfg.NC2
    NE, NF, NW = cfg.NE, cfg.NF, cfg.NW
    C = cfg.C
    scale = float(np.float32(1.0 / np.sqrt(np.float32(128.0))))

    nc = bass.Bass("TRN2", target_bir_lowering=False)

    x_d = nc.dram_tensor("x", [T, D], F32, kind="ExternalInput")
    cos_d = nc.dram_tensor("cosT", [JH * 128, T], F32, kind="ExternalInput")
    sin_d = nc.dram_tensor("sinT", [JH * 128, T], F32, kind="ExternalInput")
    wqd_d = nc.dram_tensor("wqdT", [D, L], F32, kind="ExternalInput")
    wkvd_d = nc.dram_tensor("wkvdT", [D, L], F32, kind="ExternalInput")
    wqu_d = nc.dram_tensor("wquT", [L, D], F32, kind="ExternalInput")
    wkn_d = nc.dram_tensor("wknT", [L, D // 2], F32, kind="ExternalInput")
    wkr_d = nc.dram_tensor("wkrT", [L, D // 2], F32, kind="ExternalInput")
    wvu_d = nc.dram_tensor("wvuT", [L, D], F32, kind="ExternalInput")
    wo_d = nc.dram_tensor("woT", [D, D], F32, kind="ExternalInput")
    rt_d = nc.dram_tensor("routerT", [D, NE], F32, kind="ExternalInput")
    wg_d = nc.dram_tensor("wgT", [D, cfg.IS], F32, kind="ExternalInput")
    wu_d = nc.dram_tensor("wuT", [D, cfg.IS], F32, kind="ExternalInput")
    wd_d = nc.dram_tensor("wdT", [cfg.IS, D], F32, kind="ExternalInput")
    rg_d = nc.dram_tensor("rgT", [D, cfg.IR], F32, kind="ExternalInput")
    ru_d = nc.dram_tensor("ruT", [D, cfg.IR], F32, kind="ExternalInput")
    rd_d = nc.dram_tensor("rdT", [cfg.IR, D], F32, kind="ExternalInput")
    bqd_d = nc.dram_tensor("bqd2", [128, LC], F32, kind="ExternalInput")
    bkv_d = nc.dram_tensor("bkv2", [128, LC], F32, kind="ExternalInput")
    bqu_d = nc.dram_tensor("bqu2", [128, DC], F32, kind="ExternalInput")
    bkn_d = nc.dram_tensor("bkn2", [128, NC2], F32, kind="ExternalInput")
    bkr_d = nc.dram_tensor("bkr2", [128, 2 * JH], F32, kind="ExternalInput")
    bvu_d = nc.dram_tensor("bvu2", [128, DC], F32, kind="ExternalInput")
    bo_d = nc.dram_tensor("bo2", [128, DC], F32, kind="ExternalInput")
    xT_d = nc.dram_tensor("xT", [D, T], F32, kind="ExternalInput")
    out_d = nc.dram_tensor("out", [D, T], F32, kind="ExternalOutput")

    id_bf_d = nc.inline_tensor(np.eye(128, dtype=BF), "id_bf")
    id_f32_d = nc.inline_tensor(np.eye(128, dtype=np.float32), "id_f32")

    kT_locA = nc.dram_tensor("kT_locA", [D // 2, T], BF16)
    kT_locB = nc.dram_tensor("kT_locB", [D // 2, T], BF16)
    v_locA = nc.dram_tensor("v_locA", [T, D // 2], BF16)
    v_locB = nc.dram_tensor("v_locB", [T, D // 2], BF16)
    kT_gatA = nc.dram_tensor("kT_gatA", [C * (D // 2), T], BF16)
    kT_gatB = nc.dram_tensor("kT_gatB", [C * (D // 2), T], BF16)
    v_gatA = nc.dram_tensor("v_gatA", [C * T, D // 2], BF16)
    v_gatB = nc.dram_tensor("v_gatB", [C * T, D // 2], BF16)
    kn_loc = nc.dram_tensor("kn_loc", [1, NH], F32)
    kn_gat = nc.dram_tensor("kn_gat", [1, NH], F32)
    comb_dr = nc.dram_tensor("comb_dr", [QT, 128], F32)
    cneg_dr = nc.dram_tensor("cneg_dr", [1, NH], F32)
    yffn_dr = nc.dram_tensor("yffn_dr", [128, DC * T], F32)

    o = 0

    def take(nbytes, align=256):
        nonlocal o
        o = (o + align - 1) // align * align
        r = o
        o += nbytes
        return r

    O_IDBF = take(256)
    O_IDF32 = take(512)
    O_ONES = take(256)
    O_BQD = take(4 * LC); O_BKV = take(4 * LC); O_BQU = take(4 * DC)
    O_BKN = take(4 * NC2); O_BKR = take(4 * 2 * JH); O_BVU = take(4 * DC)
    O_BO = take(4 * DC)
    O_COMB = take(2 * T)
    O_CNEG = take(4 * NH)
    O_KN2 = take(4 * NH)
    O_QN2 = take(4 * NH)
    O_KNG = take(4 * NH)
    O_CROW = take(4 * NH)
    O_CROW2 = take(4 * NH)
    O_LGS = take(4 * NE * QT)
    O_COMBT = take(4 * 128)
    O_MSK = take(4 * NE * QT)
    O_M1 = take(4 * QT)
    O_M2 = take(4 * QT)
    O_RTR = take(2 * DC * NE)
    O_EPS = take(64)
    O_SMALL = take(64 * 16)
    FIXED_END = take(0, 512)

    def small(i, n=1):
        return (O_SMALL + i * 64, n)

    O_XNT = FIXED_END
    O_QTT = O_XNT + 2 * DC * T
    BIG = O_QTT + 2 * DC * T

    D4 = D * 4
    A_X = [BIG, BIG + D4, BIG + 2 * D4]
    A_XSQ = BIG + 3 * D4
    A_XNB = [A_XSQ + D4, A_XSQ + D4 + 2 * D]
    A_END = A_XSQ + D4 + 4 * D
    B_COS = BIG
    B_SIN = B_COS + 4 * JH * T
    B_KVL = B_SIN + 4 * JH * T
    B_W = B_KVL + 2 * LC * T
    B_WSZ = max(2 * DC * L, 2 * LC * D)
    B_KT = B_W + B_WSZ
    B_KR = B_KT + 2 * DC * T
    B_VSB = B_KR
    B_RT1 = B_KR + 4 * 2 * JH * T
    B_RT2 = B_RT1 + 4 * T
    B_SQ = B_RT2 + 4 * T
    B_VT = [B_SQ + 2 * T, B_SQ + 4 * T]
    B_QL = B_KVL
    B_END = B_VT[1] + 2 * T
    C_WG = BIG
    C_WU = C_WG + 2 * DC * cfg.IG * 128
    C_WD = C_WU + 2 * DC * cfg.IG * 128
    C_Y = C_WD + 2 * cfg.IG * D
    C_H = [C_Y + 4 * DC * T, C_Y + 4 * DC * T + 2 * cfg.IG * T]
    C_SG = C_H[1] + 2 * cfg.IG * T
    C_T2 = C_SG + 2 * T
    C_END = C_T2 + 2 * T
    E_KH = [BIG, BIG + 2 * S]
    E_VH = [BIG + 4 * S, BIG + 6 * S]
    E_E = [BIG + 8 * S + i * 2 * T for i in range(4)]
    E_PB = [BIG + 8 * S + 8 * T, BIG + 8 * S + 10 * T]
    E_EA = BIG + 8 * S + 12 * T
    E_EA2 = E_EA + 4 * T
    E_AO = E_EA2 + 4 * T
    E_END = E_AO + 2 * DC * T
    F_YB = [BIG, BIG + 4 * T]
    F_XR = [BIG + 8 * T, BIG + 16 * T]
    F_OS = [BIG + 24 * T, BIG + 32 * T]
    F_WO = [E_END, E_END + 2 * DC * 256]
    F_END = F_WO[1] + 2 * DC * 256

    ARENA = max(A_END, B_END, C_END, E_END, F_END)
    ARENA = (ARENA + 511) // 512 * 512
    O_QL = B_RT1
    O_ONESF = B_VT[1]
    O_RCPA = B_RT2
    O_RCPB = C_SG
    O_NSQ = B_VSB
    arena = nc.alloc_sbuf_tensor("arena", [128, ARENA], U8)

    def bb(off, n, parts=128, p0=0):
        return arena[p0:p0 + parts, off:off + 2 * n].bitcast(BF16)

    def bf(off, n, parts=128, p0=0):
        return arena[p0:p0 + parts, off:off + 4 * n].bitcast(F32)

    PT = max(512, T)
    P = [nc.alloc_psum_tensor(f"P{i}", [128, PT], F32) for i in range(4)]

    class Sem:
        def __init__(self, name):
            self.name = name
            self.h = nc.semaphore(name).__enter__()
            self.n = 0

        def bump(self, k=1):
            self.n += k
            return self.n

    s_xa = [Sem("s_xa0"), Sem("s_xa1"), Sem("s_xa2")]
    g_cst = Sem("g_cst")
    g_cs = Sem("g_cs")
    g_wB = Sem("g_wB")
    g_slab = [Sem("g_sg"), Sem("g_su"), Sem("g_sd")]
    g_wo = [Sem("g_wo0"), Sem("g_wo1")]
    g_kn = Sem("g_kn")
    g_cb = Sem("g_cb")
    g_ms = Sem("g_ms")
    g_cc = Sem("g_cc")
    s_kst = Sem("s_kst")
    s_kn2 = Sem("s_kn2")
    s_vst = Sem("s_vst")
    s_cd1 = Sem("s_cd1")
    s_cd2 = Sem("s_cd2")
    s_kv = [Sem("s_kv0"), Sem("s_kv1")]
    s_ysp = [Sem("s_yspa"), Sem("s_yspb"), Sem("s_yspc")]
    s_yb = [Sem("s_yb0"), Sem("s_yb1")]
    s_xr = [Sem("s_xr0"), Sem("s_xr1")]
    s_out = [Sem("s_out0"), Sem("s_out1"), Sem("s_out2"), Sem("s_out3")]
    scsem = Sem("scsem")
    expsem = Sem("expsem")
    pvsem = Sem("pvsem")
    dacc = Sem("dacc")
    pacc = Sem("pacc")
    vfin = Sem("vfin")
    TE = Sem("TE")
    V = Sem("V")
    SC = Sem("SC")

    ops = {k: [] for k in ("sync", "gpsimd", "tensor", "scalar", "vector")}
    max_waited = {}

    def em(eng, fn):
        ops[eng].append(fn)

    def wait(eng, sem, v):
        if v > 0:
            max_waited[sem] = max(max_waited.get(sem, 0), v)
            em(eng, lambda e, h=sem.h, vv=v: e.wait_ge(h, vv))

    def dma(eng, sem, out, in_):
        tgt = sem.bump(16)
        em(eng, lambda e, oo=out, ii=in_, h=sem.h: e.dma_start(out=oo, in_=ii).then_inc(h, 16))
        return tgt

    def mm(pp, w, r, st, sp, inc=None, inc_amt=1):
        if inc is None:
            em("tensor", lambda e, o=pp, ww=w, rr=r, s1=st, s2=sp:
               e.matmul(o, ww, rr, start=s1, stop=s2, skip_group_check=True))
        else:
            em("tensor", lambda e, o=pp, ww=w, rr=r, s1=st, s2=sp, h=inc.h, a=inc_amt:
               e.matmul(o, ww, rr, start=s1, stop=s2, skip_group_check=True).then_inc(h, a))

    ident_bf = bb(O_IDBF, 128)
    ident_f32 = bf(O_IDF32, 128)
    ones_bf = bb(O_ONES, 128)
    ones_f32 = bf(O_ONESF, 128)

    dma("gpsimd", g_cst, bb(O_IDBF, 128), id_bf_d[:, :])
    dma("gpsimd", g_cst, bf(O_IDF32, 128), id_f32_d[:, :])
    for od, dd in ((O_BQD, bqd_d), (O_BKV, bkv_d), (O_BQU, bqu_d),
                   (O_BKN, bkn_d), (O_BKR, bkr_d), (O_BVU, bvu_d), (O_BO, bo_d)):
        dma("gpsimd", g_cst, bf(od, dd.shape[1]), dd[:, :])
    dma("gpsimd", g_cst, bb(O_RTR, DC * NE).rearrange("p (dc e) -> p dc e", dc=DC),
        rt_d.ap().rearrange("(dc p) e -> p dc e", p=128))
    CST_ALL = g_cst.n
    em("gpsimd", lambda e: e.memset(bb(O_ONES, 128), 1.0).then_inc(g_ms.h, 1))
    ONES_OK = g_ms.bump()
    em("gpsimd", lambda e, v=float(cfg.eps): e.memset(bf(O_EPS, 1), v).then_inc(g_ms.h, 1))
    EPS_OK = g_ms.bump()

    xnT = bb(O_XNT, DC * T)
    A_VDONE = {}
    A_TCOPY = {}
    A_TTR = {}
    A_SCEND = {}
    for qt in range(QT):
        pr = qt % 3
        p2 = qt % 2
        sb = 8 * p2
        if qt >= 3:
            wait("sync", V, A_VDONE[qt - 3])
        xld = dma("sync", s_xa[pr], bf(A_X[pr], D), x_d[qt * 128:(qt + 1) * 128, :])
        xap = bf(A_X[pr], D)

        wait("vector", s_xa[pr], xld)
        if qt >= 2:
            wait("vector", V, A_VDONE[qt - 2])
        em("vector", lambda e, x=xap, oo=bf(*small(sb + 0)): e.reduce_sum(oo, x, axis=AXX).then_inc(V.h, 1))
        v_sum = V.bump()
        wait("vector", V, v_sum)
        em("vector", lambda e, i=bf(*small(sb + 0)), oo=bf(*small(sb + 1)):
           e.tensor_scalar_mul(oo, i, 1.0 / D).then_inc(V.h, 1))
        v_mean = V.bump()
        wait("scalar", s_xa[pr], xld)
        if qt >= 2:
            wait("scalar", SC, A_SCEND[qt - 2])
            wait("scalar", V, A_VDONE[qt - 2])
        em("scalar", lambda e, x=xap, oo=bf(A_XSQ, D), a=bf(*small(sb + 2)):
           e.activation(oo, x, AF.Square, accum_out=a).then_inc(SC.h, 1))
        sc_sq = SC.bump()
        wait("vector", V, v_mean)
        em("vector", lambda e, m=bf(*small(sb + 1)), oo=bf(*small(sb + 3)):
           e.tensor_mul(oo, m, m).then_inc(V.h, 1))
        v_m2 = V.bump()
        wait("vector", SC, sc_sq)
        wait("vector", V, v_m2)
        em("vector", lambda e, sq=bf(*small(sb + 2)), m2=bf(*small(sb + 3)), oo=bf(*small(sb + 4)):
           e.scalar_tensor_tensor(oo, sq, 1.0 / D, m2, ALU.mult, ALU.subtract).then_inc(V.h, 1))
        v_var = V.bump()
        wait("scalar", V, v_var)
        if qt == 0:
            wait("scalar", g_ms, EPS_OK)
        em("scalar", lambda e, i=bf(*small(sb + 4)), oo=bf(*small(sb + 5)), b=bf(O_EPS, 1):
           e.activation(oo, i, AF.Ln, bias=b).then_inc(SC.h, 1))
        sc_ln = SC.bump()
        wait("scalar", SC, sc_ln)
        em("scalar", lambda e, i=bf(*small(sb + 5)), oo=bf(*small(sb + 6)):
           e.activation(oo, i, AF.Exp, scale=-0.5).then_inc(SC.h, 1))
        sc_rs = SC.bump()
        wait("vector", SC, sc_rs)
        if qt >= 2:
            wait("vector", TE, A_TTR[qt - 2])
        em("vector", lambda e, x=xap, m=bf(*small(sb + 1)), r=bf(*small(sb + 6)), oo=bb(A_XNB[p2], D):
           e.tensor_scalar(oo, x, m, r, ALU.subtract, ALU.mult).then_inc(V.h, 1))
        v_xn = V.bump()
        A_VDONE[qt] = v_xn
        A_SCEND[qt] = SC.n

        ppr = P[(qt % 2) + (2 if qt >= 4 else 0)]
        pview = ppr[:, 0:(DC * 128) // 2].bitcast(BF16)
        if qt == 0:
            wait("tensor", g_cst, CST_ALL)
        wait("tensor", V, v_xn)
        if qt not in (0, 1, 4, 5):
            wait("tensor", V, A_TCOPY[qt - 2])
        for dc in range(DC):
            if dc == DC - 1:
                em("tensor", lambda e, oo=pview[:, dc * 128:(dc + 1) * 128],
                   i=bb(A_XNB[p2] + dc * 256, 128), idt=ident_bf:
                   e.transpose(oo, i, idt).then_inc(TE.h, 1))
            else:
                em("tensor", lambda e, oo=pview[:, dc * 128:(dc + 1) * 128],
                   i=bb(A_XNB[p2] + dc * 256, 128), idt=ident_bf:
                   e.transpose(oo, i, idt))
        A_TTR[qt] = TE.bump()
        wait("vector", TE, A_TTR[qt])
        dst = xnT.rearrange("p (dc t) -> p dc t", dc=DC)[:, :, qt * 128:(qt + 1) * 128]
        src = pview.rearrange("p (dc i) -> p dc i", dc=DC)
        em("vector", lambda e, oo=dst, i=src: e.tensor_copy(oo, i).then_inc(V.h, 1))
        A_TCOPY[qt] = V.bump()

    A_DONE_V = V.n
    A_DONE_SC = SC.n

    def mm_group(pp, tiles, inc_sem):
        n = len(tiles)
        for i, (po, w, r, st, sp) in enumerate(tiles):
            if i == n - 1:
                em("tensor", lambda e, o=po, ww=w, rr=r, s1=st, s2=sp, h=inc_sem.h:
                   e.matmul(o, ww, rr, start=s1, stop=s2, skip_group_check=True).then_inc(h, 1))
            else:
                em("tensor", lambda e, o=po, ww=w, rr=r, s1=st, s2=sp:
                   e.matmul(o, ww, rr, start=s1, stop=s2, skip_group_check=True))
        return inc_sem.bump()

    def mm_proj2(rhs_off, rhs_nchunks, w_off, nout, dst_writer,
                 pre_tensor_wait=None, psel=None, first_wait=None, f_waits=None):
        first_v = V.n if first_wait is None else first_wait
        psel = psel or [P[0], P[1]]
        evac_v = {}
        for oc in range(nout):
            pp = psel[oc % len(psel)]
            if oc == 0 and pre_tensor_wait is not None:
                pre_tensor_wait()
            if oc < len(psel):
                wait("tensor", V, first_v)
            else:
                wait("tensor", V, evac_v[oc - len(psel)])
            if f_waits is None:
                tiles = []
                for c in range(rhs_nchunks):
                    w = bb(w_off + 2 * (c * nout + oc) * 128, 128)
                    for f in range(NF):
                        tiles.append((pp[:, f * NW:(f + 1) * NW], w,
                                      bb(rhs_off + 2 * (c * T + f * NW), NW),
                                      c == 0, c == rhs_nchunks - 1))
                t_mm = mm_group(pp, tiles, TE)
            else:
                for f in range(NF):
                    if oc == 0 and f in f_waits:
                        wait("tensor", f_waits[f][0], f_waits[f][1])
                    tiles = [(pp[:, f * NW:(f + 1) * NW],
                              bb(w_off + 2 * (c * nout + oc) * 128, 128),
                              bb(rhs_off + 2 * (c * T + f * NW), NW),
                              c == 0, c == rhs_nchunks - 1)
                             for c in range(rhs_nchunks)]
                    t_mm = mm_group(pp, tiles, TE)
            wait("vector", TE, t_mm)
            evac_v[oc] = dst_writer(oc, pp)
        return evac_v

    def evac_bias(dst_off, bias_off, f32out=False, first_wait=None):
        def w(oc, pp):
            if oc == 0 and first_wait is not None:
                wait("vector", first_wait[0], first_wait[1])
            bias = bf(bias_off + 4 * oc, 1)
            if f32out:
                oap = bf(dst_off + 4 * oc * T, T)
            else:
                oap = bb(dst_off + 2 * oc * T, T)
            em("vector", lambda e, oo=oap, i=pp[:, 0:T], b=bias:
               e.tensor_scalar(oo, i, b, None, ALU.add).then_inc(V.h, 1))
            return V.bump()
        return w

    wait("sync", V, A_DONE_V)
    wait("sync", SC, A_DONE_SC)
    dma("sync", g_cs, bf(B_COS, JH * T).rearrange("p (j t) -> p j t", j=JH),
        cos_d.ap().rearrange("(j p) t -> p j t", p=128))
    dma("sync", g_cs, bf(B_SIN, JH * T).rearrange("p (j t) -> p j t", j=JH),
        sin_d.ap().rearrange("(j p) t -> p j t", p=128))
    CS_ALL = g_cs.n

    def loadB(src_ap, te_wait=None):
        wait("gpsimd", TE, TE.n if te_wait is None else te_wait)
        wait("gpsimd", g_wB, g_wB.n)
        nch = src_ap.shape[0] // 128
        return dma("gpsimd", g_wB,
                   bb(B_W, nch * src_ap.shape[1]).rearrange("p (c n) -> p c n", c=nch),
                   src_ap.rearrange("(c p) n -> p c n", p=128))

    if B_W >= A_END:
        ld = loadB(wkvd_d.ap(), te_wait=0)
    else:
        wait("gpsimd", V, A_DONE_V)
        wait("gpsimd", SC, A_DONE_SC)
        ld = loadB(wkvd_d.ap())

    def pre_kv():
        wait("tensor", g_wB, ld)
    mm_proj2(O_XNT, DC, B_W, LC, evac_bias(B_KVL, O_BKV), pre_tensor_wait=pre_kv,
             first_wait=A_TCOPY[QT // NF - 1],
             f_waits={f: (V, A_TCOPY[(f + 1) * (QT // NF) - 1]) for f in range(NF)})

    ldn = loadB(wkn_d.ap())
    mm_proj2(B_KVL, LC, B_W, NC2, evac_bias(B_KT, O_BKN),
             pre_tensor_wait=lambda: wait("tensor", g_wB, ldn))

    ldr = loadB(wkr_d.ap())
    KR_EV = mm_proj2(B_KVL, LC, B_W, 2 * JH, evac_bias(B_KR, O_BKR, f32out=True),
                     pre_tensor_wait=lambda: wait("tensor", g_wB, ldr))

    wait("vector", g_cs, CS_ALL)
    wait("vector", V, KR_EV[2 * JH - 1])
    for jc in range(JH):
        x1 = bf(B_KR + 4 * jc * T, T)
        x2 = bf(B_KR + 4 * (JH + jc) * T, T)
        co = bf(B_COS + 4 * jc * T, T)
        si = bf(B_SIN + 4 * jc * T, T)
        em("vector", lambda e, oo=bf(B_RT1, T), a=x1, b=co: e.tensor_mul(oo, a, b).then_inc(V.h, 1))
        V.bump()
        em("vector", lambda e, oo=bf(B_RT2, T), a=x2, b=si: e.tensor_mul(oo, a, b).then_inc(V.h, 1))
        v2 = V.bump()
        wait("vector", V, v2)
        em("vector", lambda e, oo=bb(B_KT + 2 * (NC2 + jc) * T, T), a=bf(B_RT1, T), b=bf(B_RT2, T):
           e.tensor_sub(oo, a, b).then_inc(V.h, 1))
        v3 = V.bump()
        wait("vector", V, v3)
        em("vector", lambda e, oo=bf(B_RT1, T), a=x1, b=si: e.tensor_mul(oo, a, b).then_inc(V.h, 1))
        V.bump()
        em("vector", lambda e, oo=bf(B_RT2, T), a=x2, b=co: e.tensor_mul(oo, a, b).then_inc(V.h, 1))
        v5 = V.bump()
        wait("vector", V, v5)
        em("vector", lambda e, oo=bb(B_KT + 2 * (NC2 + JH + jc) * T, T), a=bf(B_RT1, T), b=bf(B_RT2, T):
           e.tensor_add(oo, a, b).then_inc(V.h, 1))
        v6 = V.bump()
        if jc < JH - 1:
            wait("vector", V, v6)
    KT_DONE_V = V.n

    wait("sync", V, KT_DONE_V)
    kstA = dma("sync", s_kst, kT_locA.ap().rearrange("(dc p) t -> p dc t", p=128),
               bb(B_KT, (DC // 2) * T).rearrange("p (dc t) -> p dc t", dc=DC // 2))
    kstB = dma("sync", s_kst, kT_locB.ap().rearrange("(dc p) t -> p dc t", p=128),
               bb(B_KT + DC * T, (DC // 2) * T).rearrange("p (dc t) -> p dc t", dc=DC // 2))

    ldq = loadB(wqd_d.ap())
    QD_EV = mm_proj2(O_XNT, DC, B_W, LC,
                     evac_bias(O_QL, O_BQD, first_wait=(V, KT_DONE_V)),
                     pre_tensor_wait=lambda: wait("tensor", g_wB, ldq),
                     first_wait=KR_EV[2 * JH - 1])

    wait("gpsimd", V, KT_DONE_V)
    wait("gpsimd", g_wB, g_wB.n)
    ldv = dma("gpsimd", g_wB, bb(C_WG, LC * D).rearrange("p (c n) -> p c n", c=LC),
              wvu_d.ap().rearrange("(c p) n -> p c n", p=128))
    wait("gpsimd", g_wB, g_wB.n)
    ldqu = dma("gpsimd", g_wB, bb(C_WU, LC * D).rearrange("p (c n) -> p c n", c=LC),
               wqu_d.ap().rearrange("(c p) n -> p c n", p=128))

    KSQ_SC = {}
    KN_MM = {}
    KN_V = {}
    for h in range(NH):
        half = h // 8
        if h % 8 == 0:
            wait("scalar", V, KT_DONE_V)
            if half == 1:
                wait("scalar", TE, KN_MM[7])
            em("scalar", lambda e, oo=bb(O_NSQ, 8 * T), i=bb(B_KT + 2 * half * 8 * T, 8 * T):
               e.activation(oo, i, AF.Square).then_inc(SC.h, 1))
            KSQ_SC[half] = SC.bump()
        if h == 0:
            wait("tensor", g_ms, ONES_OK)
        wait("tensor", SC, KSQ_SC[half])
        if h >= 2:
            wait("tensor", V, KN_V[h - 2])
        pp = P[2 + (h % 2)]
        tiles = [(pp[:, f * NW:(f + 1) * NW], ones_bf,
                  bb(O_NSQ + 2 * ((h % 8) * T + f * NW), NW), True, True)
                 for f in range(NF)]
        KN_MM[h] = mm_group(pp, tiles, TE)
        wait("vector", TE, KN_MM[h])
        em("vector", lambda e, oo=bf(O_KN2 + 4 * h, 1), i=pp[:, 0:T]:
           e.reduce_max(oo, i, axis=AXX).then_inc(V.h, 1))
        KN_V[h] = V.bump()
    KN_DONE_V = V.n

    wait("sync", V, KN_DONE_V)
    knst = dma("sync", s_kn2, kn_loc[0:1, :], bf(O_KN2, NH, parts=1))

    vsb = bb(B_VSB, QT * D)
    VT_COPY = {}
    VT_TR = {}

    def evac_v_writer(dc, pp):
        pr = dc % 2
        bias = bf(O_BVU + 4 * dc, 1)
        if dc >= 2:
            wait("vector", TE, VT_TR[dc - 2])
        em("vector", lambda e, oo=bb(B_VT[pr], T), i=pp[:, 0:T], b=bias:
           e.tensor_scalar(oo, i, b, None, ALU.add).then_inc(V.h, 1))
        v_e = V.bump()
        wait("tensor", V, v_e)
        ptr = P[2 + pr]
        pview = ptr[:, 0:(QT * 128) // 2].bitcast(BF16)
        if dc < 2:
            wait("tensor", V, KN_V[NH - 2 + dc])
        else:
            wait("tensor", V, VT_COPY[dc - 2])
        for qt in range(QT):
            if qt == QT - 1:
                em("tensor", lambda e, oo=pview[:, qt * 128:(qt + 1) * 128],
                   i=bb(B_VT[pr] + 2 * qt * 128, 128), idt=ident_bf:
                   e.transpose(oo, i, idt).then_inc(TE.h, 1))
            else:
                em("tensor", lambda e, oo=pview[:, qt * 128:(qt + 1) * 128],
                   i=bb(B_VT[pr] + 2 * qt * 128, 128), idt=ident_bf:
                   e.transpose(oo, i, idt))
        VT_TR[dc] = TE.bump()
        wait("vector", TE, VT_TR[dc])
        dst = vsb.rearrange("p (qt d) -> p qt d", qt=QT)[:, :, dc * 128:(dc + 1) * 128]
        src = pview.rearrange("p (qt i) -> p qt i", qt=QT)
        em("vector", lambda e, oo=dst, i=src: e.tensor_copy(oo, i).then_inc(V.h, 1))
        VT_COPY[dc] = V.bump()
        return VT_COPY[dc]

    def pre_v():
        wait("tensor", g_wB, ldv)
        wait("tensor", V, KT_DONE_V)
    mm_proj2(B_KVL, LC, C_WG, DC, evac_v_writer, pre_tensor_wait=pre_v,
             first_wait=QD_EV[LC - 1])
    V_TE = TE.n

    wait("sync", V, VT_COPY[DC - 1])
    vstA = dma("sync", s_vst, v_locA.ap().rearrange("(qt p) d -> p qt d", p=128),
               vsb.rearrange("p (qt d) -> p qt d", qt=QT)[:, :, 0:D // 2])
    vstB = dma("sync", s_vst, v_locB.ap().rearrange("(qt p) d -> p qt d", p=128),
               vsb.rearrange("p (qt d) -> p qt d", qt=QT)[:, :, D // 2:D])

    QT_EV = mm_proj2(O_QL, LC, C_WU, DC, evac_bias(O_QTT, O_BQU),
                     pre_tensor_wait=lambda: wait("tensor", g_wB, ldqu))
    QU_TE = TE.n

    sgroups = [(0, 1), (1, 1), (2, 2)] + [(g0 + 4, gs) for g0, gs in _groups(cfg.ISC - 4, cfg.IG)]
    rgroups = _groups(cfg.IRC, cfg.IG)
    allgroups = [("s", g0, gs) for g0, gs in sgroups] + [("r", g0, gs) for g0, gs in rgroups]
    g0_gs = allgroups[0][2]
    wait("gpsimd", TE, V_TE)
    lgld0 = dma("gpsimd", g_slab[0],
                bb(C_WG, DC * g0_gs * 128).rearrange("p (dc i) -> p dc i", dc=DC),
                wg_d.ap()[:, 0:g0_gs * 128].rearrange("(dc p) i -> p dc i", p=128))
    wait("gpsimd", TE, QU_TE)
    luld0 = dma("gpsimd", g_slab[1],
                bb(C_WU, DC * g0_gs * 128).rearrange("p (dc i) -> p dc i", dc=DC),
                wu_d.ap()[:, 0:g0_gs * 128].rearrange("(dc p) i -> p dc i", p=128))
    ldld0 = dma("gpsimd", g_slab[2],
                bb(C_WD, g0_gs * D).rearrange("p (il c) -> p il c", il=g0_gs),
                wd_d.ap()[0:g0_gs * 128, :].rearrange("(il p) c -> p il c", p=128))

    wait("gpsimd", s_kst, kstB)
    em("gpsimd", lambda e: e.collective_compute(
        "AllGather", ALU.bypass, replica_groups=[list(range(C))],
        ins=[kT_locA.ap().opt()], outs=[kT_gatA.ap().opt()]).then_inc(g_cc.h, 1))
    CC_KA = g_cc.bump()
    wait("gpsimd", s_kn2, knst)
    em("gpsimd", lambda e: e.collective_compute(
        "AllReduce", ALU.max, replica_groups=[list(range(C))],
        ins=[kn_loc.ap().opt()], outs=[kn_gat.ap().opt()]).then_inc(g_cc.h, 1))
    CC_KN = g_cc.bump()

    QSQ_SC = {}
    QN_MM = {}
    QN_V = {}
    for h in range(NH):
        half = h // 8
        if h % 8 == 0:
            wait("scalar", V, QT_EV[DC - 1])
            wait("scalar", s_vst, vstB)
            wait("scalar", TE, KN_MM[15] if half == 0 else QN_MM[7])
            em("scalar", lambda e, oo=bb(O_NSQ, 8 * T), i=bb(O_QTT + 2 * half * 8 * T, 8 * T):
               e.activation(oo, i, AF.Square).then_inc(SC.h, 1))
            QSQ_SC[half] = SC.bump()
        wait("tensor", SC, QSQ_SC[half])
        if h < 2:
            wait("tensor", V, VT_COPY[DC - 2 + h])
        else:
            wait("tensor", V, QN_V[h - 2])
        pp = P[2 + (h % 2)]
        tiles = [(pp[:, f * NW:(f + 1) * NW], ones_bf,
                  bb(O_NSQ + 2 * ((h % 8) * T + f * NW), NW), True, True)
                 for f in range(NF)]
        QN_MM[h] = mm_group(pp, tiles, TE)
        wait("vector", TE, QN_MM[h])
        em("vector", lambda e, oo=bf(O_QN2 + 4 * h, 1), i=pp[:, 0:T]:
           e.reduce_max(oo, i, axis=AXX).then_inc(V.h, 1))
        QN_V[h] = V.bump()
    QN_DONE_V = V.n

    wait("tensor", V, QN_V[NH - 1])
    lg_tiles = []
    for qt in range(QT):
        for dc in range(DC):
            lg_tiles.append((P[3][:, qt * NE:(qt + 1) * NE],
                             bb(O_XNT + 2 * (dc * T + qt * 128), 128),
                             bb(O_RTR + 2 * dc * NE, NE),
                             dc == 0, dc == DC - 1))
    t_lg = mm_group(None, lg_tiles, TE)
    wait("vector", TE, t_lg)
    em("vector", lambda e, oo=bf(O_LGS, NE * QT), i=P[3][:, 0:NE * QT]:
       e.tensor_copy(oo, i).then_inc(V.h, 1))
    v_lg = V.bump()
    wait("vector", V, v_lg)
    em("vector", lambda e, oo=bf(O_M1, QT),
       i=bf(O_LGS, NE * QT).rearrange("p (q n) -> p q n", q=QT):
       e.reduce_max(oo, i, axis=AXX).then_inc(V.h, 1))
    v_m1 = V.bump()
    wait("vector", V, v_m1)
    for qt in range(QT):
        em("vector", lambda e, oo=bf(O_MSK + 4 * qt * NE, NE), i=bf(O_LGS + 4 * qt * NE, NE),
           m=bf(O_M1 + 4 * qt, 1):
           e.tensor_scalar(oo, i, m, None, ALU.is_ge).then_inc(V.h, 1))
        V.bump()
    v_eq = V.n
    wait("vector", V, v_eq)
    for qt in range(QT):
        em("vector", lambda e, oo=bf(O_MSK + 4 * qt * NE, NE), i=bf(O_MSK + 4 * qt * NE, NE),
           l=bf(O_LGS + 4 * qt * NE, NE):
           e.scalar_tensor_tensor(oo, i, -1e30, l, ALU.mult, ALU.add).then_inc(V.h, 1))
        V.bump()
    v_mk = V.n
    wait("vector", V, v_mk)
    em("vector", lambda e, oo=bf(O_M2, QT),
       i=bf(O_MSK, NE * QT).rearrange("p (q n) -> p q n", q=QT):
       e.reduce_max(oo, i, axis=AXX).then_inc(V.h, 1))
    v_m2 = V.bump()
    wait("vector", V, v_m2)
    em("vector", lambda e, oo=bf(O_M2, QT), a=bf(O_M2, QT), b=bf(O_M1, QT):
       e.tensor_sub(oo, a, b).then_inc(V.h, 1))
    v_d = V.bump()
    wait("scalar", V, v_d)
    em("scalar", lambda e, oo=bf(O_M2, QT), i=bf(O_M2, QT):
       e.activation(oo, i, AF.Exp).then_inc(SC.h, 1))
    sc_t = SC.bump()
    wait("vector", SC, sc_t)
    em("vector", lambda e, oo=bf(O_M1, QT), i=bf(O_M2, QT):
       e.tensor_scalar_add(oo, i, 1.0).then_inc(V.h, 1))
    v_o = V.bump()
    wait("vector", V, v_o)
    em("vector", lambda e, oo=bf(O_M2, QT), i=bf(O_M1, QT):
       e.reciprocal(oo, i).then_inc(V.h, 1))
    v_r = V.bump()
    wait("vector", V, v_r)
    em("vector", lambda e, oo=bf(O_COMBT, QT), a=bf(O_M1, QT), b=bf(O_M2, QT):
       e.tensor_mul(oo, a, b).then_inc(V.h, 1))
    v_combtok = V.bump()
    wait("tensor", V, v_combtok)
    em("tensor", lambda e, oo=P[2][0:max(QT, 1), 0:128], i=bf(O_COMBT, QT), idt=ident_f32:
       e.transpose(oo, i, idt).then_inc(TE.h, 1))
    t_ct = TE.bump()
    wait("vector", TE, t_ct)
    em("vector", lambda e, oo=bf(O_COMBT, 128, parts=QT), i=P[2][0:QT, 0:128]:
       e.tensor_copy(oo, i).then_inc(V.h, 1))
    v_ctc = V.bump()
    wait("sync", V, v_ctc)
    cbst = dma("sync", s_cd2, comb_dr[:, :], bf(O_COMBT, 128, parts=QT))

    yacc = bf(C_Y, DC * T)
    NSL = (DC + 1) // 2

    def load_wo(sidx):
        pr = sidx % 2
        ncc = min(2, DC - sidx * 2)
        wait("gpsimd", g_wo[pr], g_wo[pr].n)
        dma("gpsimd", g_wo[pr],
            bb(F_WO[pr], DC * ncc * 128).rearrange("p (dc c) -> p dc c", dc=DC),
            wo_d.ap()[:, sidx * 256:sidx * 256 + ncc * 128]
            .rearrange("(dc p) c -> p dc c", p=128))

    C_SILU = {}
    C_HM = {}
    C_YACC_V = {}
    C_TDOWN = {}
    C_GLAST = {}
    C_ULAST = {}

    for gi, (kind, g0, gs) in enumerate(allgroups):
        gsrc = (wg_d, wu_d, wd_d) if kind == "s" else (rg_d, ru_d, rd_d)
        if gi == 0:
            lgld, luld, ldld = lgld0, luld0, ldld0
        else:
            wait("gpsimd", TE, C_GLAST[gi - 1])
            wait("gpsimd", g_slab[0], g_slab[0].n)
            lgld = dma("gpsimd", g_slab[0],
                       bb(C_WG, DC * gs * 128).rearrange("p (dc i) -> p dc i", dc=DC),
                       gsrc[0].ap()[:, g0 * 128:(g0 + gs) * 128].rearrange("(dc p) i -> p dc i", p=128))
            wait("gpsimd", TE, C_ULAST[gi - 1])
            wait("gpsimd", g_slab[1], g_slab[1].n)
            luld = dma("gpsimd", g_slab[1],
                       bb(C_WU, DC * gs * 128).rearrange("p (dc i) -> p dc i", dc=DC),
                       gsrc[1].ap()[:, g0 * 128:(g0 + gs) * 128].rearrange("(dc p) i -> p dc i", p=128))
            wait("gpsimd", TE, C_TDOWN[gi - 1])
            wait("gpsimd", g_slab[2], g_slab[2].n)
            ldld = dma("gpsimd", g_slab[2],
                       bb(C_WD, gs * D).rearrange("p (il c) -> p il c", il=gs),
                       gsrc[2].ap()[g0 * 128:(g0 + gs) * 128, :].rearrange("(il p) c -> p il c", p=128))
        if gi == 4:
            wait("gpsimd", s_kst, kstB)
            em("gpsimd", lambda e: e.collective_compute(
                "AllGather", ALU.bypass, replica_groups=[list(range(C))],
                ins=[kT_locB.ap().opt()], outs=[kT_gatB.ap().opt()]).then_inc(g_cc.h, 1))
            CC_KB = g_cc.bump()
        if gi == 7:
            wait("gpsimd", s_vst, vstB)
            em("gpsimd", lambda e: e.collective_compute(
                "AllGather", ALU.bypass, replica_groups=[list(range(C))],
                ins=[v_locA.ap().opt()], outs=[v_gatA.ap().opt()]).then_inc(g_cc.h, 1))
            CC_VA = g_cc.bump()
        if gi == 11:
            wait("gpsimd", s_vst, vstB)
            em("gpsimd", lambda e: e.collective_compute(
                "AllGather", ALU.bypass, replica_groups=[list(range(C))],
                ins=[v_locB.ap().opt()], outs=[v_gatB.ap().opt()]).then_inc(g_cc.h, 1))
            CC_VB = g_cc.bump()
        if gi == 14:
            em("gpsimd", lambda e: e.memset(bf(O_ONESF, 128), 1.0).then_inc(g_ms.h, 1))
            ONESF_OK = g_ms.bump()
        if gi == 10:
            wait("gpsimd", g_cc, CC_KN)
            knld = dma("gpsimd", g_kn, bf(O_KNG, NH, parts=1), kn_gat[0:1, :])
        if gi == 8:
            wait("gpsimd", s_cd2, cbst)
            cbld = dma("gpsimd", g_cb, bb(O_COMB, T), bass.AP(comb_dr, 0, [[0, 128], [1, T]]))
        if gi == 12:
            wait("vector", g_kn, knld)
            em("vector", lambda e, oo=bf(O_CROW, NH, parts=1), a=bf(O_QN2, NH, parts=1),
               b=bf(O_KNG, NH, parts=1):
               e.tensor_mul(oo, a, b).then_inc(V.h, 1))
            v_c1 = V.bump()
            wait("scalar", V, v_c1)
            em("scalar", lambda e, oo=bf(O_CROW2, NH, parts=1), i=bf(O_CROW, NH, parts=1):
               e.activation(oo, i, AF.Sqrt).then_inc(SC.h, 1))
            sc_cs = SC.bump()
            wait("vector", SC, sc_cs)
            em("vector", lambda e, oo=bf(O_CROW, NH, parts=1), i=bf(O_CROW2, NH, parts=1):
               e.tensor_scalar_mul(oo, i, -1.05 * scale).then_inc(V.h, 1))
            v_cn = V.bump()
            wait("sync", V, v_cn)
            cst2 = dma("sync", s_cd1, cneg_dr[0:1, :], bf(O_CROW, NH, parts=1))
        for il in range(gs):
            if gi == 0 and il == 0:
                wait("tensor", V, QT_EV[DC - 1])
            wait("tensor", g_slab[0], lgld)
            prev_silu = C_SILU.get((gi, il - 1)) or (C_SILU.get((gi - 1, allgroups[gi - 1][2] - 1)) if gi > 0 else 0)
            wait("tensor", SC, prev_silu or 0)
            tiles = []
            for dc in range(DC):
                w = bb(C_WG + 2 * (dc * gs * 128 + il * 128), 128)
                for f in range(NF):
                    tiles.append((P[0][:, f * NW:(f + 1) * NW], w,
                                  bb(O_XNT + 2 * (dc * T + f * NW), NW),
                                  dc == 0, dc == DC - 1))
            t_g = mm_group(None, tiles, TE)
            wait("tensor", g_slab[1], luld)
            prev_hm = C_HM.get((gi, il - 1)) or (C_HM.get((gi - 1, allgroups[gi - 1][2] - 1)) if gi > 0 else 0)
            wait("tensor", V, prev_hm or 0)
            tiles = []
            for dc in range(DC):
                w = bb(C_WU + 2 * (dc * gs * 128 + il * 128), 128)
                for f in range(NF):
                    tiles.append((P[1][:, f * NW:(f + 1) * NW], w,
                                  bb(O_XNT + 2 * (dc * T + f * NW), NW),
                                  dc == 0, dc == DC - 1))
            t_u = mm_group(None, tiles, TE)
            C_GLAST[gi] = t_g
            C_ULAST[gi] = t_u
            wait("scalar", TE, t_g)
            wait("scalar", V, prev_hm or 0)
            em("scalar", lambda e, oo=bb(C_SG, T), i=P[0][:, 0:T]:
               e.activation(oo, i, AF.Silu).then_inc(SC.h, 1))
            C_SILU[gi, il] = SC.bump()
            wait("vector", SC, C_SILU[gi, il])
            wait("vector", TE, t_u)
            if gi >= 2:
                wait("vector", TE, C_TDOWN[gi - 2])
            if kind == "s":
                em("vector", lambda e, oo=bb(C_H[gi % 2] + 2 * il * T, T), a=bb(C_SG, T),
                   b=P[1][:, 0:T]: e.tensor_mul(oo, a, b).then_inc(V.h, 1))
                C_HM[gi, il] = V.bump()
            else:
                em("vector", lambda e, oo=bb(C_T2, T), a=bb(C_SG, T), b=P[1][:, 0:T]:
                   e.tensor_mul(oo, a, b).then_inc(V.h, 1))
                v_t2 = V.bump()
                wait("vector", V, v_t2)
                wait("vector", g_cb, cbld)
                em("vector", lambda e, oo=bb(C_H[gi % 2] + 2 * il * T, T), a=bb(C_T2, T),
                   b=bb(O_COMB, T): e.tensor_mul(oo, a, b).then_inc(V.h, 1))
                C_HM[gi, il] = V.bump()
        wait("tensor", g_slab[2], ldld)
        wait("tensor", V, C_HM[gi, gs - 1])
        for cc in range(DC):
            pp = P[2 + (cc % 2)]
            if cc >= 2:
                wait("tensor", V, C_YACC_V[gi, cc - 2])
            elif gi > 0:
                wait("tensor", V, C_YACC_V[gi - 1, DC - 2 + cc])
            else:
                wait("tensor", V, v_ctc)
            tiles = []
            for il in range(gs):
                w = bb(C_WD + 2 * (il * D + cc * 128), 128)
                for f in range(NF):
                    tiles.append((pp[:, f * NW:(f + 1) * NW], w,
                                  bb(C_H[gi % 2] + 2 * (il * T + f * NW), NW),
                                  il == 0, il == gs - 1))
            t_d = mm_group(None, tiles, TE)
            wait("vector", TE, t_d)
            if gi == 0:
                if cc == 0:
                    wait("vector", s_kst, kstB)
                    wait("vector", s_vst, vstB)
                em("vector", lambda e, oo=bf(C_Y + 4 * cc * T, T), i=pp[:, 0:T]:
                   e.tensor_copy(oo, i).then_inc(V.h, 1))
            else:
                wait("vector", V, C_YACC_V[gi - 1, cc])
                em("vector", lambda e, oo=bf(C_Y + 4 * cc * T, T), i=pp[:, 0:T],
                   a=bf(C_Y + 4 * cc * T, T):
                   e.tensor_add(oo, a, i).then_inc(V.h, 1))
            C_YACC_V[gi, cc] = V.bump()
        C_TDOWN[gi] = TE.n

    FFN_LAST_V = C_YACC_V[len(allgroups) - 1, DC - 1]
    C_END_SC = SC.n

    wait("gpsimd", s_cd1, cst2)
    cngld = dma("gpsimd", g_kn, bf(O_CNEG, NH),
                bass.AP(cneg_dr, 0, [[0, 128], [1, NH]]))

    wait("gpsimd", TE, C_TDOWN[len(allgroups) - 1])
    load_wo(0)

    LG = len(allgroups) - 1
    wait("scalar", V, C_YACC_V[LG, 3])
    ysp_a = dma("scalar", s_ysp[0], yffn_dr[:, 0:4 * T], bf(C_Y, 4 * T))
    wait("scalar", V, C_YACC_V[LG, 9])
    ysp_b = dma("scalar", s_ysp[1], yffn_dr[:, 4 * T:10 * T], bf(C_Y + 16 * T, 6 * T))
    wait("scalar", V, C_YACC_V[LG, 15])
    ysp_c = dma("scalar", s_ysp[2], yffn_dr[:, 10 * T:16 * T], bf(C_Y + 40 * T, 6 * T))

    for h in range(NH):
        pr = h % 2
        vpr = (h + 1) % 2
        if h == 0:
            wait("sync", TE, C_GLAST[LG])
            wait("sync", g_cc, CC_VA)
        if h == 8:
            wait("sync", g_cc, CC_VB)
        if h >= 2:
            wait("sync", pvsem, (h - 1) * KC)
        ktg = kT_gatA if h < 8 else kT_gatB
        vg = v_gatA if h < 8 else v_gatB
        hh = h % 8
        if h == 1:
            wait("sync", TE, C_ULAST[LG])
        for r in range(C):
            dma("sync", s_kv[pr], bb(E_KH[pr] + 2 * r * T, T),
                ktg[r * (D // 2) + hh * 128:r * (D // 2) + (hh + 1) * 128, :])
        if h == 0:
            wait("sync", s_ysp[0], ysp_a)
        if h == 1:
            wait("sync", TE, C_TDOWN[LG])
        for r in range(C):
            dma("sync", s_kv[pr],
                bb(E_VH[vpr] + 2 * r * T, T).rearrange("p (kc d) -> p kc d", d=128),
                vg[r * T:(r + 1) * T, hh * 128:(hh + 1) * 128]
                .rearrange("(kc p) d -> p kc d", p=128))

    slot_wait = [(expsem, 0), (expsem, 0), (expsem, 0)]
    eslot = [(dacc, 0)] * 4
    pb_rd = [(dacc, 0), (pacc, 0)]
    FIN_V = {}
    COMB_V = {}
    PACC_AT = {}
    DACC_AT = {}

    def emit_combine(fh):
        wait("vector", pacc, PACC_AT[fh])
        wait("vector", dacc, DACC_AT[fh])
        em("vector", lambda e, oo=bf(E_EA, T), a=bf(E_EA, T), b=bf(E_EA2, T):
           e.tensor_tensor(oo, a, b, ALU.add).then_inc(V.h, 1))
        COMB_V[fh] = V.bump()

    DEN_T = {}
    DEN_SLOT = {}

    def emit_den(fh, slot):
        wsem, wval = slot_wait[slot]
        wait("tensor", wsem, wval)
        wait("tensor", V, COMB_V[fh])
        ppD = P[slot]
        tiles = [(ppD[:, f * NW:(f + 1) * NW], ones_f32,
                  bf(E_EA + 4 * f * NW, NW), True, True) for f in range(NF)]
        DEN_T[fh] = mm_group(None, tiles, TE)
        DEN_SLOT[fh] = slot

    def emit_rcp(fh):
        slot = DEN_SLOT[fh]
        wait("scalar", TE, DEN_T[fh])
        if fh >= 1:
            wait("scalar", V, FIN_V[fh - 1])
        em("scalar", lambda e, oo=bf(O_RCPA, T), i=P[slot][:, 0:T]:
           e.activation(oo, i, AF.Ln).then_inc(SC.h, 1))
        sc_ln = SC.bump()
        slot_wait[slot] = (SC, sc_ln)
        wait("scalar", SC, sc_ln)
        em("scalar", lambda e, oo=bf(O_RCPB, T), i=bf(O_RCPA, T):
           e.activation(oo, i, AF.Exp, scale=-1.0).then_inc(SC.h, 1))
        sc_rcp = SC.bump()
        wait("vector", SC, sc_rcp)
        wait("vector", vfin, fh + 1)
        em("vector", lambda e, oo=bb(E_AO + 2 * fh * T, T), a=bb(E_AO + 2 * fh * T, T),
           b=bf(O_RCPB, T): e.tensor_mul(oo, a, b).then_inc(V.h, 1))
        FIN_V[fh] = V.bump()

    for h in range(NH):
        pr = h % 2
        g_end_prev = h * KC
        kv_target = 16 * (2 * C) * (h // 2 + 1)
        wait("tensor", s_kv[pr], kv_target)
        if h == 0:
            wait("tensor", V, FFN_LAST_V)
            wait("tensor", SC, C_END_SC)
            wait("tensor", g_ms, ONESF_OK)
            wait("scalar", g_kn, cngld)
            wait("scalar", s_ysp[1], ysp_b)
            wait("vector", s_ysp[1], ysp_b)
            wait("gpsimd", s_ysp[1], ysp_b)
        for kc in range(KC):
            g = h * KC + kc
            wsem, wval = slot_wait[g % 3]
            wait("tensor", wsem, wval)
            pp = P[g % 3]
            tiles = [(pp[:, f * NW:(f + 1) * NW],
                      bb(E_KH[pr] + 2 * kc * 128, 128),
                      bb(O_QTT + 2 * (h * T + f * NW), NW), True, True)
                     for f in range(NF)]
            mm_group(None, tiles, scsem)
            slot_wait[g % 3] = (expsem, g + 1)
            wait("scalar", scsem, g + 1)
            wait("scalar", pvsem, g - 3)
            if g >= 4:
                esem, ev = eslot[g % 4]
                wait("scalar", esem, ev)
            em("scalar", lambda e, oo=bb(E_E[g % 4], T), i=pp[:, 0:T],
               b=bf(O_CNEG + 4 * h, 1), ssc=scale:
               e.activation(oo, i, AF.Exp, bias=b, scale=ssc).then_inc(expsem.h, 1))
            expsem.bump()
            if h >= 1 and kc == 1:
                emit_combine(h - 1)
            if h >= 1 and kc == 3:
                emit_den(h - 1, g_end_prev % 3)
            if h >= 1 and kc == 5:
                emit_rcp(h - 1)
            if kc % 2 == 1:
                j = kc // 2
                on_pool = j % 4 == 0
                wait("vector", expsem, g + 1)
                if on_pool and j >= 4:
                    wait("vector", pacc, pb_rd[0][1])
                if not on_pool and j >= 2:
                    wait("vector", dacc, pb_rd[1][1])
                em("vector", lambda e, oo=bb(E_PB[0 if on_pool else 1], T),
                   a=bb(E_E[(g - 1) % 4], T),
                   bbb=bb(E_E[g % 4], T), hh=dacc.h:
                   e.tensor_tensor(oo, a, bbb, ALU.add).then_inc(hh, 1))
                pj = dacc.bump()
                eslot[(g - 1) % 4] = (dacc, pj)
                eslot[g % 4] = (dacc, pj)
                if not on_pool:
                    wait("vector", dacc, pj)
                    if j == 1:
                        if h > 0:
                            wait("vector", TE, DEN_T[h - 1])
                        em("vector", lambda e, oo=bf(E_EA, T), i=bb(E_PB[1], T), hh=dacc.h:
                           e.tensor_copy(oo, i).then_inc(hh, 1))
                    else:
                        em("vector", lambda e, oo=bf(E_EA, T), a=bf(E_EA, T),
                           i=bb(E_PB[1], T), hh=dacc.h:
                           e.tensor_tensor(oo, a, i, ALU.add).then_inc(hh, 1))
                    pb_rd[1] = (dacc, dacc.bump())
                else:
                    wait("gpsimd", dacc, pj)
                    if j == 0:
                        if h > 0:
                            wait("gpsimd", V, COMB_V[h - 1])
                        em("gpsimd", lambda e, oo=bf(E_EA2, T), i=bb(E_PB[0], T), hh=pacc.h:
                           e.tensor_copy(oo, i).then_inc(hh, 1))
                    else:
                        em("gpsimd", lambda e, oo=bf(E_EA2, T), a=bf(E_EA2, T),
                           i=bb(E_PB[0], T), hh=pacc.h:
                           e.tensor_tensor(oo, a, i, ALU.add).then_inc(hh, 1))
                    pb_rd[0] = (pacc, pacc.bump())
            def emit_pv(gg, kk):
                wait("tensor", expsem, gg + 1)
                if kk == 0:
                    wait("tensor", vfin, h)
                tiles = [(P[3][:, f * NW:(f + 1) * NW],
                          bb(E_VH[(h + 1) % 2] + 2 * kk * 128, 128),
                          bb(E_E[gg % 4] + 2 * f * NW, NW),
                          kk == 0, kk == KC - 1) for f in range(NF)]
                mm_group(None, tiles, pvsem)
            if kc >= 2:
                emit_pv(g - 2, kc - 2)
        emit_pv(h * KC + KC - 2, KC - 2)
        emit_pv(h * KC + KC - 1, KC - 1)
        PACC_AT[h] = pacc.n
        DACC_AT[h] = dacc.n
        wait("vector", pvsem, (h + 1) * KC)
        if h == 0:
            wait("vector", s_ysp[2], ysp_c)
        em("vector", lambda e, oo=bb(E_AO + 2 * h * T, T), i=P[3][:, 0:T]:
           e.tensor_copy(oo, i).then_inc(vfin.h, 1))
        vfin.bump()
        if h == NH - 1:
            emit_combine(h)
            emit_den(h, ((h + 1) * KC) % 3)
            emit_rcp(h)
    ATT_DONE = NH * KC

    wait("gpsimd", V, FIN_V[NH - 1])
    load_wo(1)

    F_EV = {}
    XLD = {}
    PADD_V = {}
    for cc in range(DC):
        sidx = cc // 2
        pr = sidx % 2
        if cc == 0:
            wait("sync", pvsem, ATT_DONE)
            wait("sync", V, FIN_V[NH - 1])
            wait("sync", s_ysp[2], ysp_c)
        if cc >= 2:
            wait("sync", V, F_EV[cc - 2])
        ybl = dma("sync", s_yb[cc % 2], bf(F_YB[cc % 2], T), yffn_dr[:, cc * T:(cc + 1) * T])
        if cc >= 2:
            wait("scalar", V, PADD_V[cc - 2])
        if cc >= 4:
            wait("scalar", V, F_EV[cc - 4])
        XLD[cc] = dma("scalar", s_xr[cc % 2], bf(F_XR[cc % 2] + 4 * ((cc // 2) % 2) * T, T),
                      xT_d[cc * 128:(cc + 1) * 128, :])
        wait("tensor", g_wo[pr], 16 * (sidx // 2 + 1))
        if cc == 0:
            wait("tensor", V, FIN_V[NH - 1])
        if cc >= 2:
            wait("tensor", V, F_EV[cc - 2])
        pp = P[2 + (cc % 2)]
        tiles = []
        for dc in range(DC):
            w = bb(F_WO[pr] + 2 * (dc * 2 * 128 + (cc - sidx * 2) * 128), 128)
            for f in range(NF):
                tiles.append((pp[:, f * NW:(f + 1) * NW], w,
                              bb(E_AO + 2 * (dc * T + f * NW), NW),
                              dc == 0, dc == DC - 1))
        t_wo = mm_group(None, tiles, TE)
        if (cc % 2 == 1 or cc == DC - 1) and sidx + 2 < NSL:
            wait("gpsimd", TE, t_wo)
            load_wo(sidx + 2)
        wait("vector", s_yb[cc % 2], ybl)
        wait("vector", s_xr[cc % 2], XLD[cc])
        em("vector", lambda e, oo=bf(F_YB[cc % 2], T), a=bf(F_YB[cc % 2], T),
           b=bf(F_XR[cc % 2] + 4 * ((cc // 2) % 2) * T, T):
           e.tensor_add(oo, a, b).then_inc(V.h, 1))
        PADD_V[cc] = V.bump()
        wait("vector", TE, t_wo)
        wait("vector", V, PADD_V[cc])
        if cc >= 4:
            wait("vector", s_out[cc % 4], 16 * (cc // 4))
        em("vector", lambda e, oo=bf(F_OS[cc % 2] + 4 * ((cc // 2) % 2) * T, T),
           i=pp[:, 0:T], b=bf(O_BO + 4 * cc, 1), y=bf(F_YB[cc % 2], T):
           e.scalar_tensor_tensor(oo, i, b, y, ALU.add, ALU.add).then_inc(V.h, 1))
        F_EV[cc] = V.bump()
        wait("sync", V, F_EV[cc])
        dma("sync", s_out[cc % 4], out_d[cc * 128:(cc + 1) * 128, :],
            bf(F_OS[cc % 2] + 4 * ((cc // 2) % 2) * T, T))

    for so in s_out:
        wait("sync", so, so.n)

    for s, v in max_waited.items():
        assert v <= s.n, f"wait beyond final count on sem {s.name}: waited {v}, final {s.n}"

    with nc.Block() as block:
        @block.sync
        def _(e):
            for f in ops["sync"]:
                f(e)

        @block.gpsimd
        def _(e):
            for f in ops["gpsimd"]:
                f(e)

        @block.tensor
        def _(e):
            for f in ops["tensor"]:
                f(e)

        @block.scalar
        def _(e):
            for f in ops["scalar"]:
                f(e)

        @block.vector
        def _(e):
            for f in ops["vector"]:
                f(e)

    return nc


def _host_inputs(cfg, inputs):
    S, D, T, C = cfg.S, cfg.D, cfg.T, cfg.C
    f32 = np.float32

    def t2(a):
        return np.ascontiguousarray(np.asarray(a, dtype=f32).T)

    def b2(b, nch):
        return np.ascontiguousarray(np.asarray(b, dtype=f32).reshape(nch, 128).T)

    x = np.ascontiguousarray(np.asarray(inputs["L0_attn_hidden_states"], dtype=f32).reshape(S, D))

    shared = dict(
        wqdT=t2(inputs["L0_attn_Wq_down"]), bqd2=b2(inputs["L0_attn_bq_down"], cfg.LC),
        wkvdT=t2(inputs["L0_attn_Wkv_down"]), bkv2=b2(inputs["L0_attn_bkv_down"], cfg.LC),
        wquT=t2(inputs["L0_attn_Wq_up"]), bqu2=b2(inputs["L0_attn_bq_up"], cfg.DC),
        wknT=t2(inputs["L0_attn_Wk_nope"]), bkn2=b2(inputs["L0_attn_bk_nope"], cfg.NC2),
        wkrT=t2(inputs["L0_attn_Wk_rope"]), bkr2=b2(inputs["L0_attn_bk_rope"], 2 * cfg.JH),
        wvuT=t2(inputs["L0_attn_Wv_up"]), bvu2=b2(inputs["L0_attn_bv_up"], cfg.DC),
        woT=t2(inputs["L0_attn_Wo"]), bo2=b2(inputs["L0_attn_bo"], cfg.DC),
        routerT=t2(inputs["L0_ffn_router_weight"]),
        wgT=t2(inputs["L0_ffn_sh_gate"]), wuT=t2(inputs["L0_ffn_sh_up"]),
        wdT=t2(inputs["L0_ffn_sh_down"]),
        rgT=t2(inputs["L0_ffn_r_gate"]), ruT=t2(inputs["L0_ffn_r_up"]),
        rdT=t2(inputs["L0_ffn_r_down"]),
    )

    ROT = D // 2
    inv = (1.0 / (np.float32(cfg.base) ** (np.arange(0, ROT, 2, dtype=f32) / f32(ROT)))).astype(f32)
    tpos = np.arange(S, dtype=f32)
    freqs = np.outer(tpos, inv).astype(f32)
    cosT = np.ascontiguousarray(np.cos(freqs).astype(f32).T)
    sinT = np.ascontiguousarray(np.sin(freqs).astype(f32).T)

    in_maps = []
    for c in range(C):
        m = dict(shared)
        m["x"] = np.ascontiguousarray(x[c * T:(c + 1) * T])
        m["xT"] = np.ascontiguousarray(x[c * T:(c + 1) * T].T)
        m["cosT"] = np.ascontiguousarray(cosT[:, c * T:(c + 1) * T])
        m["sinT"] = np.ascontiguousarray(sinT[:, c * T:(c + 1) * T])
        in_maps.append(m)
    return in_maps


def kernel(**inputs):
    cfg = CFG
    nc = build(cfg)
    in_maps = _host_inputs(cfg, inputs)
    res = run_bass_kernel_spmd(nc, in_maps, core_ids=list(range(cfg.C)))
    out = np.concatenate(
        [np.ascontiguousarray(np.asarray(r["out"], dtype=np.float32).T) for r in res.results],
        axis=0)
    return out.reshape(1, cfg.S, cfg.D)


# version 28
# speedup vs baseline: 1.0042x; 1.0042x over previous
import numpy as np
import ml_dtypes

import concourse.bass as bass
import concourse.mybir as mybir
from concourse.bass_utils import run_bass_kernel_spmd

F32 = mybir.dt.float32
BF16 = mybir.dt.bfloat16
U8 = mybir.dt.uint8
AF = mybir.ActivationFunctionType
ALU = mybir.AluOpType
AXX = mybir.AxisListType.X
BF = ml_dtypes.bfloat16


class Cfg:
    def __init__(self, S=8192, D=2048, L=512, IS=8192, IR=1024, NE=8, C=8,
                 eps=1e-5, base=10000.0):
        self.S, self.D, self.L, self.IS, self.IR, self.NE, self.C = S, D, L, IS, IR, NE, C
        self.eps, self.base = eps, base
        self.T = S // C
        self.QT = self.T // 128
        self.DC = D // 128
        self.LC = L // 128
        self.NH = D // 128
        self.NC2 = self.DC // 2
        self.JH = (D // 4) // 128
        self.KC = S // 128
        self.ISC = IS // 128
        self.IRC = IR // 128
        self.IG = 4
        self.NF = max(1, self.T // 512)
        self.NW = min(512, self.T)


CFG = Cfg()


def _groups(nchunks, gsz):
    out, i = [], 0
    while i < nchunks:
        out.append((i, min(gsz, nchunks - i)))
        i += gsz
    return out


def build(cfg):
    S, D, L, T = cfg.S, cfg.D, cfg.L, cfg.T
    DC, LC, QT, NH, KC = cfg.DC, cfg.LC, cfg.QT, cfg.NH, cfg.KC
    JH, NC2 = cfg.JH, cfg.NC2
    NE, NF, NW = cfg.NE, cfg.NF, cfg.NW
    C = cfg.C
    scale = float(np.float32(1.0 / np.sqrt(np.float32(128.0))))

    nc = bass.Bass("TRN2", target_bir_lowering=False)

    x_d = nc.dram_tensor("x", [T, D], F32, kind="ExternalInput")
    cos_d = nc.dram_tensor("cosT", [JH * 128, T], F32, kind="ExternalInput")
    sin_d = nc.dram_tensor("sinT", [JH * 128, T], F32, kind="ExternalInput")
    wqd_d = nc.dram_tensor("wqdT", [D, L], F32, kind="ExternalInput")
    wkvd_d = nc.dram_tensor("wkvdT", [D, L], F32, kind="ExternalInput")
    wqu_d = nc.dram_tensor("wquT", [L, D], F32, kind="ExternalInput")
    wkn_d = nc.dram_tensor("wknT", [L, D // 2], F32, kind="ExternalInput")
    wkr_d = nc.dram_tensor("wkrT", [L, D // 2], F32, kind="ExternalInput")
    wvu_d = nc.dram_tensor("wvuT", [L, D], F32, kind="ExternalInput")
    wo_d = nc.dram_tensor("woT", [D, D], F32, kind="ExternalInput")
    rt_d = nc.dram_tensor("routerT", [D, NE], F32, kind="ExternalInput")
    wg_d = nc.dram_tensor("wgT", [D, cfg.IS], F32, kind="ExternalInput")
    wu_d = nc.dram_tensor("wuT", [D, cfg.IS], F32, kind="ExternalInput")
    wd_d = nc.dram_tensor("wdT", [cfg.IS, D], F32, kind="ExternalInput")
    rg_d = nc.dram_tensor("rgT", [D, cfg.IR], F32, kind="ExternalInput")
    ru_d = nc.dram_tensor("ruT", [D, cfg.IR], F32, kind="ExternalInput")
    rd_d = nc.dram_tensor("rdT", [cfg.IR, D], F32, kind="ExternalInput")
    bqd_d = nc.dram_tensor("bqd2", [128, LC], F32, kind="ExternalInput")
    bkv_d = nc.dram_tensor("bkv2", [128, LC], F32, kind="ExternalInput")
    bqu_d = nc.dram_tensor("bqu2", [128, DC], F32, kind="ExternalInput")
    bkn_d = nc.dram_tensor("bkn2", [128, NC2], F32, kind="ExternalInput")
    bkr_d = nc.dram_tensor("bkr2", [128, 2 * JH], F32, kind="ExternalInput")
    bvu_d = nc.dram_tensor("bvu2", [128, DC], F32, kind="ExternalInput")
    bo_d = nc.dram_tensor("bo2", [128, DC], F32, kind="ExternalInput")
    xT_d = nc.dram_tensor("xT", [D, T], F32, kind="ExternalInput")
    out_d = nc.dram_tensor("out", [D, T], F32, kind="ExternalOutput")

    id_bf_d = nc.inline_tensor(np.eye(128, dtype=BF), "id_bf")
    id_f32_d = nc.inline_tensor(np.eye(128, dtype=np.float32), "id_f32")

    NQ = 4
    kT_locQ = [nc.dram_tensor(f"kT_loc{q}", [D // NQ, T], BF16) for q in range(NQ)]
    v_locQ = [nc.dram_tensor(f"v_loc{q}", [T, D // NQ], BF16) for q in range(NQ)]
    kT_gatQ = [nc.dram_tensor(f"kT_gat{q}", [C * (D // NQ), T], BF16) for q in range(NQ)]
    v_gatQ = [nc.dram_tensor(f"v_gat{q}", [C * T, D // NQ], BF16) for q in range(NQ)]
    kn_loc = nc.dram_tensor("kn_loc", [1, NH], F32)
    kn_gat = nc.dram_tensor("kn_gat", [1, NH], F32)
    comb_dr = nc.dram_tensor("comb_dr", [QT, 128], F32)
    cneg_dr = nc.dram_tensor("cneg_dr", [1, NH], F32)
    yffn_dr = nc.dram_tensor("yffn_dr", [128, DC * T], F32)

    o = 0

    def take(nbytes, align=256):
        nonlocal o
        o = (o + align - 1) // align * align
        r = o
        o += nbytes
        return r

    O_IDBF = take(256)
    O_IDF32 = take(512)
    O_ONES = take(256)
    O_BQD = take(4 * LC); O_BKV = take(4 * LC); O_BQU = take(4 * DC)
    O_BKN = take(4 * NC2); O_BKR = take(4 * 2 * JH); O_BVU = take(4 * DC)
    O_BO = take(4 * DC)
    O_COMB = take(2 * T)
    O_CNEG = take(4 * NH)
    O_KN2 = take(4 * NH)
    O_QN2 = take(4 * NH)
    O_KNG = take(4 * NH)
    O_CROW = take(4 * NH)
    O_CROW2 = take(4 * NH)
    O_LGS = take(4 * NE * QT)
    O_COMBT = take(4 * 128)
    O_MSK = take(4 * NE * QT)
    O_M1 = take(4 * QT)
    O_M2 = take(4 * QT)
    O_RTR = take(2 * DC * NE)
    O_EPS = take(64)
    O_SMALL = take(64 * 16)
    FIXED_END = take(0, 512)

    def small(i, n=1):
        return (O_SMALL + i * 64, n)

    O_XNT = FIXED_END
    O_QTT = O_XNT + 2 * DC * T
    BIG = O_QTT + 2 * DC * T

    D4 = D * 4
    A_X = [BIG, BIG + D4, BIG + 2 * D4]
    A_XSQ = BIG + 3 * D4
    A_XNB = [A_XSQ + D4, A_XSQ + D4 + 2 * D]
    A_END = A_XSQ + D4 + 4 * D
    B_COS = BIG
    B_SIN = B_COS + 4 * JH * T
    B_KVL = B_SIN + 4 * JH * T
    B_W = B_KVL + 2 * LC * T
    B_WSZ = max(2 * DC * L, 2 * LC * D)
    B_KT = B_W + B_WSZ
    B_KR = B_KT + 2 * DC * T
    B_VSB = B_KR
    B_RT1 = B_KR + 4 * 2 * JH * T
    B_RT2 = B_RT1 + 4 * T
    B_SQ = B_RT2 + 4 * T
    B_VT = [B_SQ + 2 * T, B_SQ + 4 * T]
    B_QL = B_KVL
    B_END = B_VT[1] + 2 * T
    C_WG = BIG
    C_WU = C_WG + 2 * DC * cfg.IG * 128
    C_WD = C_WU + 2 * DC * cfg.IG * 128
    C_Y = C_WD + 2 * cfg.IG * D
    C_H = [C_Y + 4 * DC * T, C_Y + 4 * DC * T + 2 * cfg.IG * T]
    C_SG = C_H[1] + 2 * cfg.IG * T
    C_T2 = C_SG + 2 * T
    C_END = C_T2 + 2 * T
    E_KH = [BIG, BIG + 2 * S]
    E_VH = [BIG + 4 * S, BIG + 6 * S]
    E_E = [BIG + 8 * S + i * 2 * T for i in range(4)]
    E_PB = [BIG + 8 * S + 8 * T, BIG + 8 * S + 10 * T]
    E_EA = BIG + 8 * S + 12 * T
    E_EA2 = E_EA + 4 * T
    E_AO = E_EA2 + 4 * T
    E_END = E_AO + 2 * DC * T
    F_YB = [BIG, BIG + 4 * T]
    F_XR = [BIG + 8 * T, BIG + 16 * T]
    F_OS = [BIG + 24 * T, BIG + 32 * T]
    F_WO = [E_END, E_END + 2 * DC * 256]
    F_END = F_WO[1] + 2 * DC * 256

    ARENA = max(A_END, B_END, C_END, E_END, F_END)
    ARENA = (ARENA + 511) // 512 * 512
    O_QL = B_RT1
    O_ONESF = B_VT[1]
    O_RCPA = B_RT2
    O_RCPB = C_SG
    O_NSQ = B_VSB
    arena = nc.alloc_sbuf_tensor("arena", [128, ARENA], U8)

    def bb(off, n, parts=128, p0=0):
        return arena[p0:p0 + parts, off:off + 2 * n].bitcast(BF16)

    def bf(off, n, parts=128, p0=0):
        return arena[p0:p0 + parts, off:off + 4 * n].bitcast(F32)

    PT = max(512, T)
    P = [nc.alloc_psum_tensor(f"P{i}", [128, PT], F32) for i in range(4)]

    class Sem:
        def __init__(self, name):
            self.name = name
            self.h = nc.semaphore(name).__enter__()
            self.n = 0

        def bump(self, k=1):
            self.n += k
            return self.n

    s_xa = [Sem("s_xa0"), Sem("s_xa1"), Sem("s_xa2")]
    g_cst = Sem("g_cst")
    g_cs = Sem("g_cs")
    g_wB = Sem("g_wB")
    g_slab = [Sem("g_sg"), Sem("g_su"), Sem("g_sd")]
    g_wo = [Sem("g_wo0"), Sem("g_wo1")]
    g_kn = Sem("g_kn")
    g_cb = Sem("g_cb")
    g_ms = Sem("g_ms")
    g_cc = Sem("g_cc")
    s_kst = Sem("s_kst")
    s_kn2 = Sem("s_kn2")
    s_vst = Sem("s_vst")
    s_cd1 = Sem("s_cd1")
    s_cd2 = Sem("s_cd2")
    s_kv = [Sem("s_kv0"), Sem("s_kv1")]
    s_ysp = [Sem("s_yspa"), Sem("s_yspb"), Sem("s_yspc")]
    s_yb = [Sem("s_yb0"), Sem("s_yb1")]
    s_xr = [Sem("s_xr0"), Sem("s_xr1")]
    s_out = [Sem("s_out0"), Sem("s_out1"), Sem("s_out2"), Sem("s_out3")]
    scsem = Sem("scsem")
    expsem = Sem("expsem")
    pvsem = Sem("pvsem")
    dacc = Sem("dacc")
    pacc = Sem("pacc")
    vfin = Sem("vfin")
    TE = Sem("TE")
    V = Sem("V")
    SC = Sem("SC")

    ops = {k: [] for k in ("sync", "gpsimd", "tensor", "scalar", "vector")}
    max_waited = {}

    def em(eng, fn):
        ops[eng].append(fn)

    def wait(eng, sem, v):
        if v > 0:
            max_waited[sem] = max(max_waited.get(sem, 0), v)
            em(eng, lambda e, h=sem.h, vv=v: e.wait_ge(h, vv))

    def dma(eng, sem, out, in_):
        tgt = sem.bump(16)
        em(eng, lambda e, oo=out, ii=in_, h=sem.h: e.dma_start(out=oo, in_=ii).then_inc(h, 16))
        return tgt

    def mm(pp, w, r, st, sp, inc=None, inc_amt=1):
        if inc is None:
            em("tensor", lambda e, o=pp, ww=w, rr=r, s1=st, s2=sp:
               e.matmul(o, ww, rr, start=s1, stop=s2, skip_group_check=True))
        else:
            em("tensor", lambda e, o=pp, ww=w, rr=r, s1=st, s2=sp, h=inc.h, a=inc_amt:
               e.matmul(o, ww, rr, start=s1, stop=s2, skip_group_check=True).then_inc(h, a))

    ident_bf = bb(O_IDBF, 128)
    ident_f32 = bf(O_IDF32, 128)
    ones_bf = bb(O_ONES, 128)
    ones_f32 = bf(O_ONESF, 128)

    dma("gpsimd", g_cst, bb(O_IDBF, 128), id_bf_d[:, :])
    dma("gpsimd", g_cst, bf(O_IDF32, 128), id_f32_d[:, :])
    for od, dd in ((O_BQD, bqd_d), (O_BKV, bkv_d), (O_BQU, bqu_d),
                   (O_BKN, bkn_d), (O_BKR, bkr_d), (O_BVU, bvu_d), (O_BO, bo_d)):
        dma("gpsimd", g_cst, bf(od, dd.shape[1]), dd[:, :])
    dma("gpsimd", g_cst, bb(O_RTR, DC * NE).rearrange("p (dc e) -> p dc e", dc=DC),
        rt_d.ap().rearrange("(dc p) e -> p dc e", p=128))
    CST_ALL = g_cst.n
    em("gpsimd", lambda e: e.memset(bb(O_ONES, 128), 1.0).then_inc(g_ms.h, 1))
    ONES_OK = g_ms.bump()
    em("gpsimd", lambda e, v=float(cfg.eps): e.memset(bf(O_EPS, 1), v).then_inc(g_ms.h, 1))
    EPS_OK = g_ms.bump()

    xnT = bb(O_XNT, DC * T)
    A_VDONE = {}
    A_TCOPY = {}
    A_TTR = {}
    A_SCEND = {}
    for qt in range(QT):
        pr = qt % 3
        p2 = qt % 2
        sb = 8 * p2
        if qt >= 3:
            wait("sync", V, A_VDONE[qt - 3])
        xld = dma("sync", s_xa[pr], bf(A_X[pr], D), x_d[qt * 128:(qt + 1) * 128, :])
        xap = bf(A_X[pr], D)

        wait("vector", s_xa[pr], xld)
        if qt >= 2:
            wait("vector", V, A_VDONE[qt - 2])
        em("vector", lambda e, x=xap, oo=bf(*small(sb + 0)): e.reduce_sum(oo, x, axis=AXX).then_inc(V.h, 1))
        v_sum = V.bump()
        wait("vector", V, v_sum)
        em("vector", lambda e, i=bf(*small(sb + 0)), oo=bf(*small(sb + 1)):
           e.tensor_scalar_mul(oo, i, 1.0 / D).then_inc(V.h, 1))
        v_mean = V.bump()
        wait("scalar", s_xa[pr], xld)
        if qt >= 2:
            wait("scalar", SC, A_SCEND[qt - 2])
            wait("scalar", V, A_VDONE[qt - 2])
        em("scalar", lambda e, x=xap, oo=bf(A_XSQ, D), a=bf(*small(sb + 2)):
           e.activation(oo, x, AF.Square, accum_out=a).then_inc(SC.h, 1))
        sc_sq = SC.bump()
        wait("vector", V, v_mean)
        em("vector", lambda e, m=bf(*small(sb + 1)), oo=bf(*small(sb + 3)):
           e.tensor_mul(oo, m, m).then_inc(V.h, 1))
        v_m2 = V.bump()
        wait("vector", SC, sc_sq)
        wait("vector", V, v_m2)
        em("vector", lambda e, sq=bf(*small(sb + 2)), m2=bf(*small(sb + 3)), oo=bf(*small(sb + 4)):
           e.scalar_tensor_tensor(oo, sq, 1.0 / D, m2, ALU.mult, ALU.subtract).then_inc(V.h, 1))
        v_var = V.bump()
        wait("scalar", V, v_var)
        if qt == 0:
            wait("scalar", g_ms, EPS_OK)
        em("scalar", lambda e, i=bf(*small(sb + 4)), oo=bf(*small(sb + 5)), b=bf(O_EPS, 1):
           e.activation(oo, i, AF.Ln, bias=b).then_inc(SC.h, 1))
        sc_ln = SC.bump()
        wait("scalar", SC, sc_ln)
        em("scalar", lambda e, i=bf(*small(sb + 5)), oo=bf(*small(sb + 6)):
           e.activation(oo, i, AF.Exp, scale=-0.5).then_inc(SC.h, 1))
        sc_rs = SC.bump()
        wait("vector", SC, sc_rs)
        if qt >= 2:
            wait("vector", TE, A_TTR[qt - 2])
        em("vector", lambda e, x=xap, m=bf(*small(sb + 1)), r=bf(*small(sb + 6)), oo=bb(A_XNB[p2], D):
           e.tensor_scalar(oo, x, m, r, ALU.subtract, ALU.mult).then_inc(V.h, 1))
        v_xn = V.bump()
        A_VDONE[qt] = v_xn
        A_SCEND[qt] = SC.n

        ppr = P[(qt % 2) + (2 if qt >= 4 else 0)]
        pview = ppr[:, 0:(DC * 128) // 2].bitcast(BF16)
        if qt == 0:
            wait("tensor", g_cst, CST_ALL)
        wait("tensor", V, v_xn)
        if qt not in (0, 1, 4, 5):
            wait("tensor", V, A_TCOPY[qt - 2])
        for dc in range(DC):
            if dc == DC - 1:
                em("tensor", lambda e, oo=pview[:, dc * 128:(dc + 1) * 128],
                   i=bb(A_XNB[p2] + dc * 256, 128), idt=ident_bf:
                   e.transpose(oo, i, idt).then_inc(TE.h, 1))
            else:
                em("tensor", lambda e, oo=pview[:, dc * 128:(dc + 1) * 128],
                   i=bb(A_XNB[p2] + dc * 256, 128), idt=ident_bf:
                   e.transpose(oo, i, idt))
        A_TTR[qt] = TE.bump()
        wait("vector", TE, A_TTR[qt])
        dst = xnT.rearrange("p (dc t) -> p dc t", dc=DC)[:, :, qt * 128:(qt + 1) * 128]
        src = pview.rearrange("p (dc i) -> p dc i", dc=DC)
        em("vector", lambda e, oo=dst, i=src: e.tensor_copy(oo, i).then_inc(V.h, 1))
        A_TCOPY[qt] = V.bump()

    A_DONE_V = V.n
    A_DONE_SC = SC.n

    def mm_group(pp, tiles, inc_sem):
        n = len(tiles)
        for i, (po, w, r, st, sp) in enumerate(tiles):
            if i == n - 1:
                em("tensor", lambda e, o=po, ww=w, rr=r, s1=st, s2=sp, h=inc_sem.h:
                   e.matmul(o, ww, rr, start=s1, stop=s2, skip_group_check=True).then_inc(h, 1))
            else:
                em("tensor", lambda e, o=po, ww=w, rr=r, s1=st, s2=sp:
                   e.matmul(o, ww, rr, start=s1, stop=s2, skip_group_check=True))
        return inc_sem.bump()

    def mm_proj2(rhs_off, rhs_nchunks, w_off, nout, dst_writer,
                 pre_tensor_wait=None, psel=None, first_wait=None, f_waits=None):
        first_v = V.n if first_wait is None else first_wait
        psel = psel or [P[0], P[1]]
        evac_v = {}
        for oc in range(nout):
            pp = psel[oc % len(psel)]
            if oc == 0 and pre_tensor_wait is not None:
                pre_tensor_wait()
            if oc < len(psel):
                wait("tensor", V, first_v)
            else:
                wait("tensor", V, evac_v[oc - len(psel)])
            if f_waits is None:
                tiles = []
                for c in range(rhs_nchunks):
                    w = bb(w_off + 2 * (c * nout + oc) * 128, 128)
                    for f in range(NF):
                        tiles.append((pp[:, f * NW:(f + 1) * NW], w,
                                      bb(rhs_off + 2 * (c * T + f * NW), NW),
                                      c == 0, c == rhs_nchunks - 1))
                t_mm = mm_group(pp, tiles, TE)
            else:
                for f in range(NF):
                    if oc == 0 and f in f_waits:
                        wait("tensor", f_waits[f][0], f_waits[f][1])
                    tiles = [(pp[:, f * NW:(f + 1) * NW],
                              bb(w_off + 2 * (c * nout + oc) * 128, 128),
                              bb(rhs_off + 2 * (c * T + f * NW), NW),
                              c == 0, c == rhs_nchunks - 1)
                             for c in range(rhs_nchunks)]
                    t_mm = mm_group(pp, tiles, TE)
            wait("vector", TE, t_mm)
            evac_v[oc] = dst_writer(oc, pp)
        return evac_v

    def evac_bias(dst_off, bias_off, f32out=False, first_wait=None):
        def w(oc, pp):
            if oc == 0 and first_wait is not None:
                wait("vector", first_wait[0], first_wait[1])
            bias = bf(bias_off + 4 * oc, 1)
            if f32out:
                oap = bf(dst_off + 4 * oc * T, T)
            else:
                oap = bb(dst_off + 2 * oc * T, T)
            em("vector", lambda e, oo=oap, i=pp[:, 0:T], b=bias:
               e.tensor_scalar(oo, i, b, None, ALU.add).then_inc(V.h, 1))
            return V.bump()
        return w

    wait("sync", V, A_DONE_V)
    wait("sync", SC, A_DONE_SC)
    dma("sync", g_cs, bf(B_COS, JH * T).rearrange("p (j t) -> p j t", j=JH),
        cos_d.ap().rearrange("(j p) t -> p j t", p=128))
    dma("sync", g_cs, bf(B_SIN, JH * T).rearrange("p (j t) -> p j t", j=JH),
        sin_d.ap().rearrange("(j p) t -> p j t", p=128))
    CS_ALL = g_cs.n

    def loadB(src_ap, te_wait=None):
        wait("gpsimd", TE, TE.n if te_wait is None else te_wait)
        wait("gpsimd", g_wB, g_wB.n)
        nch = src_ap.shape[0] // 128
        return dma("gpsimd", g_wB,
                   bb(B_W, nch * src_ap.shape[1]).rearrange("p (c n) -> p c n", c=nch),
                   src_ap.rearrange("(c p) n -> p c n", p=128))

    if B_W >= A_END:
        ld = loadB(wkvd_d.ap(), te_wait=0)
    else:
        wait("gpsimd", V, A_DONE_V)
        wait("gpsimd", SC, A_DONE_SC)
        ld = loadB(wkvd_d.ap())

    def pre_kv():
        wait("tensor", g_wB, ld)
    mm_proj2(O_XNT, DC, B_W, LC, evac_bias(B_KVL, O_BKV), pre_tensor_wait=pre_kv,
             first_wait=A_TCOPY[QT // NF - 1],
             f_waits={f: (V, A_TCOPY[(f + 1) * (QT // NF) - 1]) for f in range(NF)})

    ldn = loadB(wkn_d.ap())
    mm_proj2(B_KVL, LC, B_W, NC2, evac_bias(B_KT, O_BKN),
             pre_tensor_wait=lambda: wait("tensor", g_wB, ldn))

    ldr = loadB(wkr_d.ap())
    KR_EV = mm_proj2(B_KVL, LC, B_W, 2 * JH, evac_bias(B_KR, O_BKR, f32out=True),
                     pre_tensor_wait=lambda: wait("tensor", g_wB, ldr))

    wait("vector", g_cs, CS_ALL)
    wait("vector", V, KR_EV[2 * JH - 1])
    for jc in range(JH):
        x1 = bf(B_KR + 4 * jc * T, T)
        x2 = bf(B_KR + 4 * (JH + jc) * T, T)
        co = bf(B_COS + 4 * jc * T, T)
        si = bf(B_SIN + 4 * jc * T, T)
        em("vector", lambda e, oo=bf(B_RT1, T), a=x1, b=co: e.tensor_mul(oo, a, b).then_inc(V.h, 1))
        V.bump()
        em("vector", lambda e, oo=bf(B_RT2, T), a=x2, b=si: e.tensor_mul(oo, a, b).then_inc(V.h, 1))
        v2 = V.bump()
        wait("vector", V, v2)
        em("vector", lambda e, oo=bb(B_KT + 2 * (NC2 + jc) * T, T), a=bf(B_RT1, T), b=bf(B_RT2, T):
           e.tensor_sub(oo, a, b).then_inc(V.h, 1))
        v3 = V.bump()
        wait("vector", V, v3)
        em("vector", lambda e, oo=bf(B_RT1, T), a=x1, b=si: e.tensor_mul(oo, a, b).then_inc(V.h, 1))
        V.bump()
        em("vector", lambda e, oo=bf(B_RT2, T), a=x2, b=co: e.tensor_mul(oo, a, b).then_inc(V.h, 1))
        v5 = V.bump()
        wait("vector", V, v5)
        em("vector", lambda e, oo=bb(B_KT + 2 * (NC2 + JH + jc) * T, T), a=bf(B_RT1, T), b=bf(B_RT2, T):
           e.tensor_add(oo, a, b).then_inc(V.h, 1))
        v6 = V.bump()
        if jc < JH - 1:
            wait("vector", V, v6)
    KT_DONE_V = V.n

    wait("sync", V, KT_DONE_V)
    for q in range(NQ):
        kst = dma("sync", s_kst, kT_locQ[q].ap().rearrange("(dc p) t -> p dc t", p=128),
                  bb(B_KT + q * (DC // NQ) * 2 * T, (DC // NQ) * T)
                  .rearrange("p (dc t) -> p dc t", dc=DC // NQ))
    KST_ALL = kst

    ldq = loadB(wqd_d.ap())
    QD_EV = mm_proj2(O_XNT, DC, B_W, LC,
                     evac_bias(O_QL, O_BQD, first_wait=(V, KT_DONE_V)),
                     pre_tensor_wait=lambda: wait("tensor", g_wB, ldq),
                     first_wait=KR_EV[2 * JH - 1])

    wait("gpsimd", V, KT_DONE_V)
    wait("gpsimd", g_wB, g_wB.n)
    ldv = dma("gpsimd", g_wB, bb(C_WG, LC * D).rearrange("p (c n) -> p c n", c=LC),
              wvu_d.ap().rearrange("(c p) n -> p c n", p=128))
    wait("gpsimd", g_wB, g_wB.n)
    ldqu = dma("gpsimd", g_wB, bb(C_WU, LC * D).rearrange("p (c n) -> p c n", c=LC),
               wqu_d.ap().rearrange("(c p) n -> p c n", p=128))

    KSQ_SC = {}
    KN_MM = {}
    KN_V = {}
    for h in range(NH):
        half = h // 8
        if h % 8 == 0:
            wait("scalar", V, KT_DONE_V)
            if half == 1:
                wait("scalar", TE, KN_MM[7])
            em("scalar", lambda e, oo=bb(O_NSQ, 8 * T), i=bb(B_KT + 2 * half * 8 * T, 8 * T):
               e.activation(oo, i, AF.Square).then_inc(SC.h, 1))
            KSQ_SC[half] = SC.bump()
        if h == 0:
            wait("tensor", g_ms, ONES_OK)
        wait("tensor", SC, KSQ_SC[half])
        if h >= 2:
            wait("tensor", V, KN_V[h - 2])
        pp = P[2 + (h % 2)]
        tiles = [(pp[:, f * NW:(f + 1) * NW], ones_bf,
                  bb(O_NSQ + 2 * ((h % 8) * T + f * NW), NW), True, True)
                 for f in range(NF)]
        KN_MM[h] = mm_group(pp, tiles, TE)
        wait("vector", TE, KN_MM[h])
        em("vector", lambda e, oo=bf(O_KN2 + 4 * h, 1), i=pp[:, 0:T]:
           e.reduce_max(oo, i, axis=AXX).then_inc(V.h, 1))
        KN_V[h] = V.bump()
    KN_DONE_V = V.n

    wait("sync", V, KN_DONE_V)
    knst = dma("sync", s_kn2, kn_loc[0:1, :], bf(O_KN2, NH, parts=1))

    vsb = bb(B_VSB, QT * D)
    VT_COPY = {}
    VT_TR = {}

    def evac_v_writer(dc, pp):
        pr = dc % 2
        bias = bf(O_BVU + 4 * dc, 1)
        if dc >= 2:
            wait("vector", TE, VT_TR[dc - 2])
        em("vector", lambda e, oo=bb(B_VT[pr], T), i=pp[:, 0:T], b=bias:
           e.tensor_scalar(oo, i, b, None, ALU.add).then_inc(V.h, 1))
        v_e = V.bump()
        wait("tensor", V, v_e)
        ptr = P[2 + pr]
        pview = ptr[:, 0:(QT * 128) // 2].bitcast(BF16)
        if dc < 2:
            wait("tensor", V, KN_V[NH - 2 + dc])
        else:
            wait("tensor", V, VT_COPY[dc - 2])
        for qt in range(QT):
            if qt == QT - 1:
                em("tensor", lambda e, oo=pview[:, qt * 128:(qt + 1) * 128],
                   i=bb(B_VT[pr] + 2 * qt * 128, 128), idt=ident_bf:
                   e.transpose(oo, i, idt).then_inc(TE.h, 1))
            else:
                em("tensor", lambda e, oo=pview[:, qt * 128:(qt + 1) * 128],
                   i=bb(B_VT[pr] + 2 * qt * 128, 128), idt=ident_bf:
                   e.transpose(oo, i, idt))
        VT_TR[dc] = TE.bump()
        wait("vector", TE, VT_TR[dc])
        dst = vsb.rearrange("p (qt d) -> p qt d", qt=QT)[:, :, dc * 128:(dc + 1) * 128]
        src = pview.rearrange("p (qt i) -> p qt i", qt=QT)
        em("vector", lambda e, oo=dst, i=src: e.tensor_copy(oo, i).then_inc(V.h, 1))
        VT_COPY[dc] = V.bump()
        return VT_COPY[dc]

    def pre_v():
        wait("tensor", g_wB, ldv)
        wait("tensor", V, KT_DONE_V)
    mm_proj2(B_KVL, LC, C_WG, DC, evac_v_writer, pre_tensor_wait=pre_v,
             first_wait=QD_EV[LC - 1])
    V_TE = TE.n

    wait("sync", V, VT_COPY[DC - 1])
    for q in range(NQ):
        vst = dma("sync", s_vst, v_locQ[q].ap().rearrange("(qt p) d -> p qt d", p=128),
                  vsb.rearrange("p (qt d) -> p qt d", qt=QT)[:, :, q * (D // NQ):(q + 1) * (D // NQ)])
    VST_ALL = vst

    QT_EV = mm_proj2(O_QL, LC, C_WU, DC, evac_bias(O_QTT, O_BQU),
                     pre_tensor_wait=lambda: wait("tensor", g_wB, ldqu))
    QU_TE = TE.n

    sgroups = [(0, 1), (1, 1), (2, 2)] + [(g0 + 4, gs) for g0, gs in _groups(cfg.ISC - 4, cfg.IG)]
    rgroups = _groups(cfg.IRC, cfg.IG)
    allgroups = [("s", g0, gs) for g0, gs in sgroups] + [("r", g0, gs) for g0, gs in rgroups]
    g0_gs = allgroups[0][2]
    wait("gpsimd", TE, V_TE)
    lgld0 = dma("gpsimd", g_slab[0],
                bb(C_WG, DC * g0_gs * 128).rearrange("p (dc i) -> p dc i", dc=DC),
                wg_d.ap()[:, 0:g0_gs * 128].rearrange("(dc p) i -> p dc i", p=128))
    wait("gpsimd", TE, QU_TE)
    luld0 = dma("gpsimd", g_slab[1],
                bb(C_WU, DC * g0_gs * 128).rearrange("p (dc i) -> p dc i", dc=DC),
                wu_d.ap()[:, 0:g0_gs * 128].rearrange("(dc p) i -> p dc i", p=128))
    ldld0 = dma("gpsimd", g_slab[2],
                bb(C_WD, g0_gs * D).rearrange("p (il c) -> p il c", il=g0_gs),
                wd_d.ap()[0:g0_gs * 128, :].rearrange("(il p) c -> p il c", p=128))

    CC_K = {}
    CC_V = {}

    def emit_cc_k(q):
        wait("gpsimd", s_kst, KST_ALL)
        em("gpsimd", lambda e: e.collective_compute(
            "AllGather", ALU.bypass, replica_groups=[list(range(C))],
            ins=[kT_locQ[q].ap().opt()], outs=[kT_gatQ[q].ap().opt()]).then_inc(g_cc.h, 1))
        CC_K[q] = g_cc.bump()

    def emit_cc_v(q):
        wait("gpsimd", s_vst, VST_ALL)
        em("gpsimd", lambda e: e.collective_compute(
            "AllGather", ALU.bypass, replica_groups=[list(range(C))],
            ins=[v_locQ[q].ap().opt()], outs=[v_gatQ[q].ap().opt()]).then_inc(g_cc.h, 1))
        CC_V[q] = g_cc.bump()

    emit_cc_k(0)
    wait("gpsimd", s_kn2, knst)
    em("gpsimd", lambda e: e.collective_compute(
        "AllReduce", ALU.max, replica_groups=[list(range(C))],
        ins=[kn_loc.ap().opt()], outs=[kn_gat.ap().opt()]).then_inc(g_cc.h, 1))
    CC_KN = g_cc.bump()

    QSQ_SC = {}
    QN_MM = {}
    QN_V = {}
    for h in range(NH):
        half = h // 8
        if h % 8 == 0:
            wait("scalar", V, QT_EV[DC - 1])
            wait("scalar", s_vst, VST_ALL)
            wait("scalar", TE, KN_MM[15] if half == 0 else QN_MM[7])
            em("scalar", lambda e, oo=bb(O_NSQ, 8 * T), i=bb(O_QTT + 2 * half * 8 * T, 8 * T):
               e.activation(oo, i, AF.Square).then_inc(SC.h, 1))
            QSQ_SC[half] = SC.bump()
        wait("tensor", SC, QSQ_SC[half])
        if h < 2:
            wait("tensor", V, VT_COPY[DC - 2 + h])
        else:
            wait("tensor", V, QN_V[h - 2])
        pp = P[2 + (h % 2)]
        tiles = [(pp[:, f * NW:(f + 1) * NW], ones_bf,
                  bb(O_NSQ + 2 * ((h % 8) * T + f * NW), NW), True, True)
                 for f in range(NF)]
        QN_MM[h] = mm_group(pp, tiles, TE)
        wait("vector", TE, QN_MM[h])
        em("vector", lambda e, oo=bf(O_QN2 + 4 * h, 1), i=pp[:, 0:T]:
           e.reduce_max(oo, i, axis=AXX).then_inc(V.h, 1))
        QN_V[h] = V.bump()
    QN_DONE_V = V.n

    wait("tensor", V, QN_V[NH - 1])
    lg_tiles = []
    for qt in range(QT):
        for dc in range(DC):
            lg_tiles.append((P[3][:, qt * NE:(qt + 1) * NE],
                             bb(O_XNT + 2 * (dc * T + qt * 128), 128),
                             bb(O_RTR + 2 * dc * NE, NE),
                             dc == 0, dc == DC - 1))
    t_lg = mm_group(None, lg_tiles, TE)
    wait("vector", TE, t_lg)
    em("vector", lambda e, oo=bf(O_LGS, NE * QT), i=P[3][:, 0:NE * QT]:
       e.tensor_copy(oo, i).then_inc(V.h, 1))
    v_lg = V.bump()
    wait("vector", V, v_lg)
    em("vector", lambda e, oo=bf(O_M1, QT),
       i=bf(O_LGS, NE * QT).rearrange("p (q n) -> p q n", q=QT):
       e.reduce_max(oo, i, axis=AXX).then_inc(V.h, 1))
    v_m1 = V.bump()
    wait("vector", V, v_m1)
    for qt in range(QT):
        em("vector", lambda e, oo=bf(O_MSK + 4 * qt * NE, NE), i=bf(O_LGS + 4 * qt * NE, NE),
           m=bf(O_M1 + 4 * qt, 1):
           e.tensor_scalar(oo, i, m, None, ALU.is_ge).then_inc(V.h, 1))
        V.bump()
    v_eq = V.n
    wait("vector", V, v_eq)
    for qt in range(QT):
        em("vector", lambda e, oo=bf(O_MSK + 4 * qt * NE, NE), i=bf(O_MSK + 4 * qt * NE, NE),
           l=bf(O_LGS + 4 * qt * NE, NE):
           e.scalar_tensor_tensor(oo, i, -1e30, l, ALU.mult, ALU.add).then_inc(V.h, 1))
        V.bump()
    v_mk = V.n
    wait("vector", V, v_mk)
    em("vector", lambda e, oo=bf(O_M2, QT),
       i=bf(O_MSK, NE * QT).rearrange("p (q n) -> p q n", q=QT):
       e.reduce_max(oo, i, axis=AXX).then_inc(V.h, 1))
    v_m2 = V.bump()
    wait("vector", V, v_m2)
    em("vector", lambda e, oo=bf(O_M2, QT), a=bf(O_M2, QT), b=bf(O_M1, QT):
       e.tensor_sub(oo, a, b).then_inc(V.h, 1))
    v_d = V.bump()
    wait("scalar", V, v_d)
    em("scalar", lambda e, oo=bf(O_M2, QT), i=bf(O_M2, QT):
       e.activation(oo, i, AF.Exp).then_inc(SC.h, 1))
    sc_t = SC.bump()
    wait("vector", SC, sc_t)
    em("vector", lambda e, oo=bf(O_M1, QT), i=bf(O_M2, QT):
       e.tensor_scalar_add(oo, i, 1.0).then_inc(V.h, 1))
    v_o = V.bump()
    wait("vector", V, v_o)
    em("vector", lambda e, oo=bf(O_M2, QT), i=bf(O_M1, QT):
       e.reciprocal(oo, i).then_inc(V.h, 1))
    v_r = V.bump()
    wait("vector", V, v_r)
    em("vector", lambda e, oo=bf(O_COMBT, QT), a=bf(O_M1, QT), b=bf(O_M2, QT):
       e.tensor_mul(oo, a, b).then_inc(V.h, 1))
    v_combtok = V.bump()
    wait("tensor", V, v_combtok)
    em("tensor", lambda e, oo=P[2][0:max(QT, 1), 0:128], i=bf(O_COMBT, QT), idt=ident_f32:
       e.transpose(oo, i, idt).then_inc(TE.h, 1))
    t_ct = TE.bump()
    wait("vector", TE, t_ct)
    em("vector", lambda e, oo=bf(O_COMBT, 128, parts=QT), i=P[2][0:QT, 0:128]:
       e.tensor_copy(oo, i).then_inc(V.h, 1))
    v_ctc = V.bump()
    wait("sync", V, v_ctc)
    cbst = dma("sync", s_cd2, comb_dr[:, :], bf(O_COMBT, 128, parts=QT))

    yacc = bf(C_Y, DC * T)
    NSL = (DC + 1) // 2

    def load_wo(sidx):
        pr = sidx % 2
        ncc = min(2, DC - sidx * 2)
        wait("gpsimd", g_wo[pr], g_wo[pr].n)
        dma("gpsimd", g_wo[pr],
            bb(F_WO[pr], DC * ncc * 128).rearrange("p (dc c) -> p dc c", dc=DC),
            wo_d.ap()[:, sidx * 256:sidx * 256 + ncc * 128]
            .rearrange("(dc p) c -> p dc c", p=128))

    C_SILU = {}
    C_HM = {}
    C_YACC_V = {}
    C_TDOWN = {}
    C_GLAST = {}
    C_ULAST = {}

    for gi, (kind, g0, gs) in enumerate(allgroups):
        gsrc = (wg_d, wu_d, wd_d) if kind == "s" else (rg_d, ru_d, rd_d)
        if gi == 0:
            lgld, luld, ldld = lgld0, luld0, ldld0
        else:
            wait("gpsimd", TE, C_GLAST[gi - 1])
            wait("gpsimd", g_slab[0], g_slab[0].n)
            lgld = dma("gpsimd", g_slab[0],
                       bb(C_WG, DC * gs * 128).rearrange("p (dc i) -> p dc i", dc=DC),
                       gsrc[0].ap()[:, g0 * 128:(g0 + gs) * 128].rearrange("(dc p) i -> p dc i", p=128))
            wait("gpsimd", TE, C_ULAST[gi - 1])
            wait("gpsimd", g_slab[1], g_slab[1].n)
            luld = dma("gpsimd", g_slab[1],
                       bb(C_WU, DC * gs * 128).rearrange("p (dc i) -> p dc i", dc=DC),
                       gsrc[1].ap()[:, g0 * 128:(g0 + gs) * 128].rearrange("(dc p) i -> p dc i", p=128))
            wait("gpsimd", TE, C_TDOWN[gi - 1])
            wait("gpsimd", g_slab[2], g_slab[2].n)
            ldld = dma("gpsimd", g_slab[2],
                       bb(C_WD, gs * D).rearrange("p (il c) -> p il c", il=gs),
                       gsrc[2].ap()[g0 * 128:(g0 + gs) * 128, :].rearrange("(il p) c -> p il c", p=128))
        if gi == 2:
            emit_cc_v(0)
        if gi == 4:
            emit_cc_k(1)
        if gi == 6:
            emit_cc_v(1)
        if gi == 8:
            emit_cc_k(2)
        if gi == 11:
            emit_cc_v(2)
        if gi == 13:
            emit_cc_k(3)
        if gi == 15:
            emit_cc_v(3)
        if gi == 14:
            em("gpsimd", lambda e: e.memset(bf(O_ONESF, 128), 1.0).then_inc(g_ms.h, 1))
            ONESF_OK = g_ms.bump()
        if gi == 10:
            wait("gpsimd", g_cc, CC_KN)
            knld = dma("gpsimd", g_kn, bf(O_KNG, NH, parts=1), kn_gat[0:1, :])
        if gi == 8:
            wait("gpsimd", s_cd2, cbst)
            cbld = dma("gpsimd", g_cb, bb(O_COMB, T), bass.AP(comb_dr, 0, [[0, 128], [1, T]]))
        if gi == 12:
            wait("vector", g_kn, knld)
            em("vector", lambda e, oo=bf(O_CROW, NH, parts=1), a=bf(O_QN2, NH, parts=1),
               b=bf(O_KNG, NH, parts=1):
               e.tensor_mul(oo, a, b).then_inc(V.h, 1))
            v_c1 = V.bump()
            wait("scalar", V, v_c1)
            em("scalar", lambda e, oo=bf(O_CROW2, NH, parts=1), i=bf(O_CROW, NH, parts=1):
               e.activation(oo, i, AF.Sqrt).then_inc(SC.h, 1))
            sc_cs = SC.bump()
            wait("vector", SC, sc_cs)
            em("vector", lambda e, oo=bf(O_CROW, NH, parts=1), i=bf(O_CROW2, NH, parts=1):
               e.tensor_scalar_mul(oo, i, -1.05 * scale).then_inc(V.h, 1))
            v_cn = V.bump()
            wait("sync", V, v_cn)
            cst2 = dma("sync", s_cd1, cneg_dr[0:1, :], bf(O_CROW, NH, parts=1))
        for il in range(gs):
            if gi == 0 and il == 0:
                wait("tensor", V, QT_EV[DC - 1])
            wait("tensor", g_slab[0], lgld)
            prev_silu = C_SILU.get((gi, il - 1)) or (C_SILU.get((gi - 1, allgroups[gi - 1][2] - 1)) if gi > 0 else 0)
            wait("tensor", SC, prev_silu or 0)
            tiles = []
            for dc in range(DC):
                w = bb(C_WG + 2 * (dc * gs * 128 + il * 128), 128)
                for f in range(NF):
                    tiles.append((P[0][:, f * NW:(f + 1) * NW], w,
                                  bb(O_XNT + 2 * (dc * T + f * NW), NW),
                                  dc == 0, dc == DC - 1))
            t_g = mm_group(None, tiles, TE)
            wait("tensor", g_slab[1], luld)
            prev_hm = C_HM.get((gi, il - 1)) or (C_HM.get((gi - 1, allgroups[gi - 1][2] - 1)) if gi > 0 else 0)
            wait("tensor", V, prev_hm or 0)
            tiles = []
            for dc in range(DC):
                w = bb(C_WU + 2 * (dc * gs * 128 + il * 128), 128)
                for f in range(NF):
                    tiles.append((P[1][:, f * NW:(f + 1) * NW], w,
                                  bb(O_XNT + 2 * (dc * T + f * NW), NW),
                                  dc == 0, dc == DC - 1))
            t_u = mm_group(None, tiles, TE)
            C_GLAST[gi] = t_g
            C_ULAST[gi] = t_u
            wait("scalar", TE, t_g)
            wait("scalar", V, prev_hm or 0)
            em("scalar", lambda e, oo=bb(C_SG, T), i=P[0][:, 0:T]:
               e.activation(oo, i, AF.Silu).then_inc(SC.h, 1))
            C_SILU[gi, il] = SC.bump()
            wait("vector", SC, C_SILU[gi, il])
            wait("vector", TE, t_u)
            if gi >= 2:
                wait("vector", TE, C_TDOWN[gi - 2])
            if kind == "s":
                em("vector", lambda e, oo=bb(C_H[gi % 2] + 2 * il * T, T), a=bb(C_SG, T),
                   b=P[1][:, 0:T]: e.tensor_mul(oo, a, b).then_inc(V.h, 1))
                C_HM[gi, il] = V.bump()
            else:
                em("vector", lambda e, oo=bb(C_T2, T), a=bb(C_SG, T), b=P[1][:, 0:T]:
                   e.tensor_mul(oo, a, b).then_inc(V.h, 1))
                v_t2 = V.bump()
                wait("vector", V, v_t2)
                wait("vector", g_cb, cbld)
                em("vector", lambda e, oo=bb(C_H[gi % 2] + 2 * il * T, T), a=bb(C_T2, T),
                   b=bb(O_COMB, T): e.tensor_mul(oo, a, b).then_inc(V.h, 1))
                C_HM[gi, il] = V.bump()
        wait("tensor", g_slab[2], ldld)
        wait("tensor", V, C_HM[gi, gs - 1])
        for cc in range(DC):
            pp = P[2 + (cc % 2)]
            if cc >= 2:
                wait("tensor", V, C_YACC_V[gi, cc - 2])
            elif gi > 0:
                wait("tensor", V, C_YACC_V[gi - 1, DC - 2 + cc])
            else:
                wait("tensor", V, v_ctc)
            tiles = []
            for il in range(gs):
                w = bb(C_WD + 2 * (il * D + cc * 128), 128)
                for f in range(NF):
                    tiles.append((pp[:, f * NW:(f + 1) * NW], w,
                                  bb(C_H[gi % 2] + 2 * (il * T + f * NW), NW),
                                  il == 0, il == gs - 1))
            t_d = mm_group(None, tiles, TE)
            wait("vector", TE, t_d)
            if gi == 0:
                if cc == 0:
                    wait("vector", s_kst, KST_ALL)
                    wait("vector", s_vst, VST_ALL)
                em("vector", lambda e, oo=bf(C_Y + 4 * cc * T, T), i=pp[:, 0:T]:
                   e.tensor_copy(oo, i).then_inc(V.h, 1))
            else:
                wait("vector", V, C_YACC_V[gi - 1, cc])
                em("vector", lambda e, oo=bf(C_Y + 4 * cc * T, T), i=pp[:, 0:T],
                   a=bf(C_Y + 4 * cc * T, T):
                   e.tensor_add(oo, a, i).then_inc(V.h, 1))
            C_YACC_V[gi, cc] = V.bump()
        C_TDOWN[gi] = TE.n

    FFN_LAST_V = C_YACC_V[len(allgroups) - 1, DC - 1]
    C_END_SC = SC.n

    wait("gpsimd", s_cd1, cst2)
    cngld = dma("gpsimd", g_kn, bf(O_CNEG, NH),
                bass.AP(cneg_dr, 0, [[0, 128], [1, NH]]))

    wait("gpsimd", TE, C_TDOWN[len(allgroups) - 1])
    load_wo(0)

    LG = len(allgroups) - 1
    wait("scalar", V, C_YACC_V[LG, 3])
    ysp_a = dma("scalar", s_ysp[0], yffn_dr[:, 0:4 * T], bf(C_Y, 4 * T))
    wait("scalar", V, C_YACC_V[LG, 9])
    ysp_b = dma("scalar", s_ysp[1], yffn_dr[:, 4 * T:10 * T], bf(C_Y + 16 * T, 6 * T))
    wait("scalar", V, C_YACC_V[LG, 15])
    ysp_c = dma("scalar", s_ysp[2], yffn_dr[:, 10 * T:16 * T], bf(C_Y + 40 * T, 6 * T))

    for h in range(NH):
        pr = h % 2
        vpr = (h + 1) % 2
        if h == 0:
            wait("sync", TE, C_GLAST[LG])
            wait("sync", g_cc, CC_V[0])
        if h % 4 == 0 and h > 0:
            wait("sync", g_cc, CC_V[h // 4])
        if h >= 2:
            wait("sync", pvsem, (h - 1) * KC)
        ktg = kT_gatQ[h // 4]
        vg = v_gatQ[h // 4]
        hh = h % 4
        if h == 1:
            wait("sync", TE, C_ULAST[LG])
        for r in range(C):
            dma("sync", s_kv[pr], bb(E_KH[pr] + 2 * r * T, T),
                ktg[r * (D // NQ) + hh * 128:r * (D // NQ) + (hh + 1) * 128, :])
        if h == 0:
            wait("sync", s_ysp[0], ysp_a)
        if h == 1:
            wait("sync", TE, C_TDOWN[LG])
        for r in range(C):
            dma("sync", s_kv[pr],
                bb(E_VH[vpr] + 2 * r * T, T).rearrange("p (kc d) -> p kc d", d=128),
                vg[r * T:(r + 1) * T, hh * 128:(hh + 1) * 128]
                .rearrange("(kc p) d -> p kc d", p=128))

    slot_wait = [(expsem, 0), (expsem, 0), (expsem, 0)]
    eslot = [(dacc, 0)] * 4
    pb_rd = [(dacc, 0), (pacc, 0)]
    FIN_V = {}
    COMB_V = {}
    PACC_AT = {}
    DACC_AT = {}

    def emit_combine(fh):
        wait("vector", pacc, PACC_AT[fh])
        wait("vector", dacc, DACC_AT[fh])
        em("vector", lambda e, oo=bf(E_EA, T), a=bf(E_EA, T), b=bf(E_EA2, T):
           e.tensor_tensor(oo, a, b, ALU.add).then_inc(V.h, 1))
        COMB_V[fh] = V.bump()

    DEN_T = {}
    DEN_SLOT = {}

    def emit_den(fh, slot):
        wsem, wval = slot_wait[slot]
        wait("tensor", wsem, wval)
        wait("tensor", V, COMB_V[fh])
        ppD = P[slot]
        tiles = [(ppD[:, f * NW:(f + 1) * NW], ones_f32,
                  bf(E_EA + 4 * f * NW, NW), True, True) for f in range(NF)]
        DEN_T[fh] = mm_group(None, tiles, TE)
        DEN_SLOT[fh] = slot

    def emit_rcp(fh):
        slot = DEN_SLOT[fh]
        wait("scalar", TE, DEN_T[fh])
        if fh >= 1:
            wait("scalar", V, FIN_V[fh - 1])
        em("scalar", lambda e, oo=bf(O_RCPA, T), i=P[slot][:, 0:T]:
           e.activation(oo, i, AF.Ln).then_inc(SC.h, 1))
        sc_ln = SC.bump()
        slot_wait[slot] = (SC, sc_ln)
        wait("scalar", SC, sc_ln)
        em("scalar", lambda e, oo=bf(O_RCPB, T), i=bf(O_RCPA, T):
           e.activation(oo, i, AF.Exp, scale=-1.0).then_inc(SC.h, 1))
        sc_rcp = SC.bump()
        wait("vector", SC, sc_rcp)
        wait("vector", vfin, fh + 1)
        em("vector", lambda e, oo=bb(E_AO + 2 * fh * T, T), a=bb(E_AO + 2 * fh * T, T),
           b=bf(O_RCPB, T): e.tensor_mul(oo, a, b).then_inc(V.h, 1))
        FIN_V[fh] = V.bump()

    for h in range(NH):
        pr = h % 2
        g_end_prev = h * KC
        kv_target = 16 * (2 * C) * (h // 2 + 1)
        wait("tensor", s_kv[pr], kv_target)
        if h == 0:
            wait("tensor", V, FFN_LAST_V)
            wait("tensor", SC, C_END_SC)
            wait("tensor", g_ms, ONESF_OK)
            wait("scalar", g_kn, cngld)
            wait("scalar", s_ysp[1], ysp_b)
            wait("vector", s_ysp[1], ysp_b)
            wait("gpsimd", s_ysp[1], ysp_b)
        for kc in range(KC):
            g = h * KC + kc
            wsem, wval = slot_wait[g % 3]
            wait("tensor", wsem, wval)
            pp = P[g % 3]
            tiles = [(pp[:, f * NW:(f + 1) * NW],
                      bb(E_KH[pr] + 2 * kc * 128, 128),
                      bb(O_QTT + 2 * (h * T + f * NW), NW), True, True)
                     for f in range(NF)]
            mm_group(None, tiles, scsem)
            slot_wait[g % 3] = (expsem, g + 1)
            wait("scalar", scsem, g + 1)
            wait("scalar", pvsem, g - 3)
            if g >= 4:
                esem, ev = eslot[g % 4]
                wait("scalar", esem, ev)
            em("scalar", lambda e, oo=bb(E_E[g % 4], T), i=pp[:, 0:T],
               b=bf(O_CNEG + 4 * h, 1), ssc=scale:
               e.activation(oo, i, AF.Exp, bias=b, scale=ssc).then_inc(expsem.h, 1))
            expsem.bump()
            if h >= 1 and kc == 1:
                emit_combine(h - 1)
            if h >= 1 and kc == 3:
                emit_den(h - 1, g_end_prev % 3)
            if h >= 1 and kc == 5:
                emit_rcp(h - 1)
            if kc % 2 == 1:
                j = kc // 2
                on_pool = j % 4 == 0
                wait("vector", expsem, g + 1)
                if on_pool and j >= 4:
                    wait("vector", pacc, pb_rd[0][1])
                if not on_pool and j >= 2:
                    wait("vector", dacc, pb_rd[1][1])
                em("vector", lambda e, oo=bb(E_PB[0 if on_pool else 1], T),
                   a=bb(E_E[(g - 1) % 4], T),
                   bbb=bb(E_E[g % 4], T), hh=dacc.h:
                   e.tensor_tensor(oo, a, bbb, ALU.add).then_inc(hh, 1))
                pj = dacc.bump()
                eslot[(g - 1) % 4] = (dacc, pj)
                eslot[g % 4] = (dacc, pj)
                if not on_pool:
                    wait("vector", dacc, pj)
                    if j == 1:
                        if h > 0:
                            wait("vector", TE, DEN_T[h - 1])
                        em("vector", lambda e, oo=bf(E_EA, T), i=bb(E_PB[1], T), hh=dacc.h:
                           e.tensor_copy(oo, i).then_inc(hh, 1))
                    else:
                        em("vector", lambda e, oo=bf(E_EA, T), a=bf(E_EA, T),
                           i=bb(E_PB[1], T), hh=dacc.h:
                           e.tensor_tensor(oo, a, i, ALU.add).then_inc(hh, 1))
                    pb_rd[1] = (dacc, dacc.bump())
                else:
                    wait("gpsimd", dacc, pj)
                    if j == 0:
                        if h > 0:
                            wait("gpsimd", V, COMB_V[h - 1])
                        em("gpsimd", lambda e, oo=bf(E_EA2, T), i=bb(E_PB[0], T), hh=pacc.h:
                           e.tensor_copy(oo, i).then_inc(hh, 1))
                    else:
                        em("gpsimd", lambda e, oo=bf(E_EA2, T), a=bf(E_EA2, T),
                           i=bb(E_PB[0], T), hh=pacc.h:
                           e.tensor_tensor(oo, a, i, ALU.add).then_inc(hh, 1))
                    pb_rd[0] = (pacc, pacc.bump())
            def emit_pv(gg, kk):
                wait("tensor", expsem, gg + 1)
                if kk == 0:
                    wait("tensor", vfin, h)
                tiles = [(P[3][:, f * NW:(f + 1) * NW],
                          bb(E_VH[(h + 1) % 2] + 2 * kk * 128, 128),
                          bb(E_E[gg % 4] + 2 * f * NW, NW),
                          kk == 0, kk == KC - 1) for f in range(NF)]
                mm_group(None, tiles, pvsem)
            if kc >= 2:
                emit_pv(g - 2, kc - 2)
        emit_pv(h * KC + KC - 2, KC - 2)
        emit_pv(h * KC + KC - 1, KC - 1)
        PACC_AT[h] = pacc.n
        DACC_AT[h] = dacc.n
        wait("vector", pvsem, (h + 1) * KC)
        if h == 0:
            wait("vector", s_ysp[2], ysp_c)
        em("vector", lambda e, oo=bb(E_AO + 2 * h * T, T), i=P[3][:, 0:T]:
           e.tensor_copy(oo, i).then_inc(vfin.h, 1))
        vfin.bump()
        if h == NH - 1:
            emit_combine(h)
            emit_den(h, ((h + 1) * KC) % 3)
            emit_rcp(h)
    ATT_DONE = NH * KC

    wait("gpsimd", V, FIN_V[NH - 1])
    load_wo(1)

    F_EV = {}
    XLD = {}
    PADD_V = {}
    for cc in range(DC):
        sidx = cc // 2
        pr = sidx % 2
        if cc == 0:
            wait("sync", pvsem, ATT_DONE)
            wait("sync", V, FIN_V[NH - 1])
            wait("sync", s_ysp[2], ysp_c)
        if cc >= 2:
            wait("sync", V, F_EV[cc - 2])
        ybl = dma("sync", s_yb[cc % 2], bf(F_YB[cc % 2], T), yffn_dr[:, cc * T:(cc + 1) * T])
        if cc >= 2:
            wait("scalar", V, PADD_V[cc - 2])
        if cc >= 4:
            wait("scalar", V, F_EV[cc - 4])
        XLD[cc] = dma("scalar", s_xr[cc % 2], bf(F_XR[cc % 2] + 4 * ((cc // 2) % 2) * T, T),
                      xT_d[cc * 128:(cc + 1) * 128, :])
        wait("tensor", g_wo[pr], 16 * (sidx // 2 + 1))
        if cc == 0:
            wait("tensor", V, FIN_V[NH - 1])
        if cc >= 2:
            wait("tensor", V, F_EV[cc - 2])
        pp = P[2 + (cc % 2)]
        tiles = []
        for dc in range(DC):
            w = bb(F_WO[pr] + 2 * (dc * 2 * 128 + (cc - sidx * 2) * 128), 128)
            for f in range(NF):
                tiles.append((pp[:, f * NW:(f + 1) * NW], w,
                              bb(E_AO + 2 * (dc * T + f * NW), NW),
                              dc == 0, dc == DC - 1))
        t_wo = mm_group(None, tiles, TE)
        if (cc % 2 == 1 or cc == DC - 1) and sidx + 2 < NSL:
            wait("gpsimd", TE, t_wo)
            load_wo(sidx + 2)
        wait("vector", s_yb[cc % 2], ybl)
        wait("vector", s_xr[cc % 2], XLD[cc])
        em("vector", lambda e, oo=bf(F_YB[cc % 2], T), a=bf(F_YB[cc % 2], T),
           b=bf(F_XR[cc % 2] + 4 * ((cc // 2) % 2) * T, T):
           e.tensor_add(oo, a, b).then_inc(V.h, 1))
        PADD_V[cc] = V.bump()
        wait("vector", TE, t_wo)
        wait("vector", V, PADD_V[cc])
        if cc >= 4:
            wait("vector", s_out[cc % 4], 16 * (cc // 4))
        em("vector", lambda e, oo=bf(F_OS[cc % 2] + 4 * ((cc // 2) % 2) * T, T),
           i=pp[:, 0:T], b=bf(O_BO + 4 * cc, 1), y=bf(F_YB[cc % 2], T):
           e.scalar_tensor_tensor(oo, i, b, y, ALU.add, ALU.add).then_inc(V.h, 1))
        F_EV[cc] = V.bump()
        wait("sync", V, F_EV[cc])
        dma("sync", s_out[cc % 4], out_d[cc * 128:(cc + 1) * 128, :],
            bf(F_OS[cc % 2] + 4 * ((cc // 2) % 2) * T, T))

    for so in s_out:
        wait("sync", so, so.n)

    for s, v in max_waited.items():
        assert v <= s.n, f"wait beyond final count on sem {s.name}: waited {v}, final {s.n}"

    with nc.Block() as block:
        @block.sync
        def _(e):
            for f in ops["sync"]:
                f(e)

        @block.gpsimd
        def _(e):
            for f in ops["gpsimd"]:
                f(e)

        @block.tensor
        def _(e):
            for f in ops["tensor"]:
                f(e)

        @block.scalar
        def _(e):
            for f in ops["scalar"]:
                f(e)

        @block.vector
        def _(e):
            for f in ops["vector"]:
                f(e)

    return nc


def _host_inputs(cfg, inputs):
    S, D, T, C = cfg.S, cfg.D, cfg.T, cfg.C
    f32 = np.float32

    def t2(a):
        return np.ascontiguousarray(np.asarray(a, dtype=f32).T)

    def b2(b, nch):
        return np.ascontiguousarray(np.asarray(b, dtype=f32).reshape(nch, 128).T)

    x = np.ascontiguousarray(np.asarray(inputs["L0_attn_hidden_states"], dtype=f32).reshape(S, D))

    shared = dict(
        wqdT=t2(inputs["L0_attn_Wq_down"]), bqd2=b2(inputs["L0_attn_bq_down"], cfg.LC),
        wkvdT=t2(inputs["L0_attn_Wkv_down"]), bkv2=b2(inputs["L0_attn_bkv_down"], cfg.LC),
        wquT=t2(inputs["L0_attn_Wq_up"]), bqu2=b2(inputs["L0_attn_bq_up"], cfg.DC),
        wknT=t2(inputs["L0_attn_Wk_nope"]), bkn2=b2(inputs["L0_attn_bk_nope"], cfg.NC2),
        wkrT=t2(inputs["L0_attn_Wk_rope"]), bkr2=b2(inputs["L0_attn_bk_rope"], 2 * cfg.JH),
        wvuT=t2(inputs["L0_attn_Wv_up"]), bvu2=b2(inputs["L0_attn_bv_up"], cfg.DC),
        woT=t2(inputs["L0_attn_Wo"]), bo2=b2(inputs["L0_attn_bo"], cfg.DC),
        routerT=t2(inputs["L0_ffn_router_weight"]),
        wgT=t2(inputs["L0_ffn_sh_gate"]), wuT=t2(inputs["L0_ffn_sh_up"]),
        wdT=t2(inputs["L0_ffn_sh_down"]),
        rgT=t2(inputs["L0_ffn_r_gate"]), ruT=t2(inputs["L0_ffn_r_up"]),
        rdT=t2(inputs["L0_ffn_r_down"]),
    )

    ROT = D // 2
    inv = (1.0 / (np.float32(cfg.base) ** (np.arange(0, ROT, 2, dtype=f32) / f32(ROT)))).astype(f32)
    tpos = np.arange(S, dtype=f32)
    freqs = np.outer(tpos, inv).astype(f32)
    cosT = np.ascontiguousarray(np.cos(freqs).astype(f32).T)
    sinT = np.ascontiguousarray(np.sin(freqs).astype(f32).T)

    in_maps = []
    for c in range(C):
        m = dict(shared)
        m["x"] = np.ascontiguousarray(x[c * T:(c + 1) * T])
        m["xT"] = np.ascontiguousarray(x[c * T:(c + 1) * T].T)
        m["cosT"] = np.ascontiguousarray(cosT[:, c * T:(c + 1) * T])
        m["sinT"] = np.ascontiguousarray(sinT[:, c * T:(c + 1) * T])
        in_maps.append(m)
    return in_maps


def kernel(**inputs):
    cfg = CFG
    nc = build(cfg)
    in_maps = _host_inputs(cfg, inputs)
    res = run_bass_kernel_spmd(nc, in_maps, core_ids=list(range(cfg.C)))
    out = np.concatenate(
        [np.ascontiguousarray(np.asarray(r["out"], dtype=np.float32).T) for r in res.results],
        axis=0)
    return out.reshape(1, cfg.S, cfg.D)


# version 29
# speedup vs baseline: 1.0167x; 1.0124x over previous
import numpy as np
import ml_dtypes

import concourse.bass as bass
import concourse.mybir as mybir
from concourse.bass_utils import run_bass_kernel_spmd

F32 = mybir.dt.float32
BF16 = mybir.dt.bfloat16
U8 = mybir.dt.uint8
AF = mybir.ActivationFunctionType
ALU = mybir.AluOpType
AXX = mybir.AxisListType.X
BF = ml_dtypes.bfloat16


class Cfg:
    def __init__(self, S=8192, D=2048, L=512, IS=8192, IR=1024, NE=8, C=8,
                 eps=1e-5, base=10000.0):
        self.S, self.D, self.L, self.IS, self.IR, self.NE, self.C = S, D, L, IS, IR, NE, C
        self.eps, self.base = eps, base
        self.T = S // C
        self.QT = self.T // 128
        self.DC = D // 128
        self.LC = L // 128
        self.NH = D // 128
        self.NC2 = self.DC // 2
        self.JH = (D // 4) // 128
        self.KC = S // 128
        self.ISC = IS // 128
        self.IRC = IR // 128
        self.IG = 4
        self.NF = max(1, self.T // 512)
        self.NW = min(512, self.T)


CFG = Cfg()


def _groups(nchunks, gsz):
    out, i = [], 0
    while i < nchunks:
        out.append((i, min(gsz, nchunks - i)))
        i += gsz
    return out


def build(cfg):
    S, D, L, T = cfg.S, cfg.D, cfg.L, cfg.T
    DC, LC, QT, NH, KC = cfg.DC, cfg.LC, cfg.QT, cfg.NH, cfg.KC
    JH, NC2 = cfg.JH, cfg.NC2
    NE, NF, NW = cfg.NE, cfg.NF, cfg.NW
    C = cfg.C
    scale = float(np.float32(1.0 / np.sqrt(np.float32(128.0))))

    nc = bass.Bass("TRN2", target_bir_lowering=False)

    x_d = nc.dram_tensor("x", [T, D], F32, kind="ExternalInput")
    cos_d = nc.dram_tensor("cosT", [JH * 128, T], F32, kind="ExternalInput")
    sin_d = nc.dram_tensor("sinT", [JH * 128, T], F32, kind="ExternalInput")
    wqd_d = nc.dram_tensor("wqdT", [D, L], F32, kind="ExternalInput")
    wkvd_d = nc.dram_tensor("wkvdT", [D, L], F32, kind="ExternalInput")
    wqu_d = nc.dram_tensor("wquT", [L, D], F32, kind="ExternalInput")
    wkn_d = nc.dram_tensor("wknT", [L, D // 2], F32, kind="ExternalInput")
    wkr_d = nc.dram_tensor("wkrT", [L, D // 2], F32, kind="ExternalInput")
    wvu_d = nc.dram_tensor("wvuT", [L, D], F32, kind="ExternalInput")
    wo_d = nc.dram_tensor("woT", [D, D], F32, kind="ExternalInput")
    rt_d = nc.dram_tensor("routerT", [D, NE], F32, kind="ExternalInput")
    wg_d = nc.dram_tensor("wgT", [D, cfg.IS], F32, kind="ExternalInput")
    wu_d = nc.dram_tensor("wuT", [D, cfg.IS], F32, kind="ExternalInput")
    wd_d = nc.dram_tensor("wdT", [cfg.IS, D], F32, kind="ExternalInput")
    rg_d = nc.dram_tensor("rgT", [D, cfg.IR], F32, kind="ExternalInput")
    ru_d = nc.dram_tensor("ruT", [D, cfg.IR], F32, kind="ExternalInput")
    rd_d = nc.dram_tensor("rdT", [cfg.IR, D], F32, kind="ExternalInput")
    bqd_d = nc.dram_tensor("bqd2", [128, LC], F32, kind="ExternalInput")
    bkv_d = nc.dram_tensor("bkv2", [128, LC], F32, kind="ExternalInput")
    bqu_d = nc.dram_tensor("bqu2", [128, DC], F32, kind="ExternalInput")
    bkn_d = nc.dram_tensor("bkn2", [128, NC2], F32, kind="ExternalInput")
    bkr_d = nc.dram_tensor("bkr2", [128, 2 * JH], F32, kind="ExternalInput")
    bvu_d = nc.dram_tensor("bvu2", [128, DC], F32, kind="ExternalInput")
    bo_d = nc.dram_tensor("bo2", [128, DC], F32, kind="ExternalInput")
    xT_d = nc.dram_tensor("xT", [D, T], F32, kind="ExternalInput")
    out_d = nc.dram_tensor("out", [D, T], F32, kind="ExternalOutput")

    id_bf_d = nc.inline_tensor(np.eye(128, dtype=BF), "id_bf")
    id_f32_d = nc.inline_tensor(np.eye(128, dtype=np.float32), "id_f32")

    NQ = 4
    kT_locQ = [nc.dram_tensor(f"kT_loc{q}", [D // NQ, T], BF16) for q in range(NQ)]
    v_locQ = [nc.dram_tensor(f"v_loc{q}", [T, D // NQ], BF16) for q in range(NQ)]
    kT_gatQ = [nc.dram_tensor(f"kT_gat{q}", [C * (D // NQ), T], BF16) for q in range(NQ)]
    v_gatQ = [nc.dram_tensor(f"v_gat{q}", [C * T, D // NQ], BF16) for q in range(NQ)]
    kn_loc = nc.dram_tensor("kn_loc", [1, NH], F32)
    kn_gat = nc.dram_tensor("kn_gat", [1, NH], F32)
    comb_dr = nc.dram_tensor("comb_dr", [QT, 128], F32)
    cneg_dr = nc.dram_tensor("cneg_dr", [1, NH], F32)
    yffn_dr = nc.dram_tensor("yffn_dr", [128, DC * T], F32)

    o = 0

    def take(nbytes, align=256):
        nonlocal o
        o = (o + align - 1) // align * align
        r = o
        o += nbytes
        return r

    O_IDBF = take(256)
    O_IDF32 = take(512)
    O_ONES = take(256)
    O_BQD = take(4 * LC); O_BKV = take(4 * LC); O_BQU = take(4 * DC)
    O_BKN = take(4 * NC2); O_BKR = take(4 * 2 * JH); O_BVU = take(4 * DC)
    O_BO = take(4 * DC)
    O_COMB = take(2 * T)
    O_CNEG = take(4 * NH)
    O_KN2 = take(4 * NH)
    O_QN2 = take(4 * NH)
    O_KNG = take(4 * NH)
    O_CROW = take(4 * NH)
    O_CROW2 = take(4 * NH)
    O_LGS = take(4 * NE * QT)
    O_COMBT = take(4 * 128)
    O_MSK = take(4 * NE * QT)
    O_M1 = take(4 * QT)
    O_M2 = take(4 * QT)
    O_RTR = take(2 * DC * NE)
    O_EPS = take(64)
    O_SMALL = take(64 * 16)
    FIXED_END = take(0, 512)

    def small(i, n=1):
        return (O_SMALL + i * 64, n)

    O_XNT = FIXED_END
    O_QTT = O_XNT + 2 * DC * T
    BIG = O_QTT + 2 * DC * T

    D4 = D * 4
    A_X = [BIG, BIG + D4, BIG + 2 * D4]
    A_XSQ = BIG + 3 * D4
    A_XNB = [A_XSQ + D4, A_XSQ + D4 + 2 * D]
    A_END = A_XSQ + D4 + 4 * D
    B_COS = BIG
    B_SIN = B_COS + 4 * JH * T
    B_KVL = B_SIN + 4 * JH * T
    B_W = B_KVL + 2 * LC * T
    B_WSZ = max(2 * DC * L, 2 * LC * D)
    B_KT = B_W + B_WSZ
    B_KR = B_KT + 2 * DC * T
    B_VSB = B_KR
    B_RT1 = B_KR + 4 * 2 * JH * T
    B_RT2 = B_RT1 + 4 * T
    B_SQ = B_RT2 + 4 * T
    B_VT = [B_SQ + 2 * T, B_SQ + 4 * T]
    B_QL = B_KVL
    B_END = B_VT[1] + 2 * T
    C_WG = BIG
    C_WU = C_WG + 2 * DC * cfg.IG * 128
    C_WD = C_WU + 2 * DC * cfg.IG * 128
    C_Y = C_WD + 2 * cfg.IG * D
    C_H = [C_Y + 4 * DC * T, C_Y + 4 * DC * T + 2 * cfg.IG * T]
    C_SG = C_H[1] + 2 * cfg.IG * T
    C_T2 = C_SG + 2 * T
    C_END = C_T2 + 2 * T
    E_KH = [BIG, BIG + 2 * S]
    E_VH = [BIG + 4 * S, BIG + 6 * S]
    E_E = [BIG + 8 * S + i * 2 * T for i in range(4)]
    E_PB = [BIG + 8 * S + 8 * T, BIG + 8 * S + 10 * T]
    E_EA = BIG + 8 * S + 12 * T
    E_EA2 = E_EA + 4 * T
    E_AO = E_EA2 + 4 * T
    E_END = E_AO + 2 * DC * T
    F_YB = [BIG, BIG + 4 * T]
    F_XR = [BIG + 8 * T, BIG + 16 * T]
    F_OS = [BIG + 24 * T, BIG + 32 * T]
    F_WO = [E_END, E_END + 2 * DC * 256]
    F_END = F_WO[1] + 2 * DC * 256

    ARENA = max(A_END, B_END, C_END, E_END, F_END)
    ARENA = (ARENA + 511) // 512 * 512
    O_QL = B_RT1
    O_ONESF = B_VT[1]
    O_RCPA = B_RT2
    O_RCPB = C_SG
    O_NSQ = B_VSB
    arena = nc.alloc_sbuf_tensor("arena", [128, ARENA], U8)

    def bb(off, n, parts=128, p0=0):
        return arena[p0:p0 + parts, off:off + 2 * n].bitcast(BF16)

    def bf(off, n, parts=128, p0=0):
        return arena[p0:p0 + parts, off:off + 4 * n].bitcast(F32)

    PT = max(512, T)
    P = [nc.alloc_psum_tensor(f"P{i}", [128, PT], F32) for i in range(4)]

    class Sem:
        def __init__(self, name):
            self.name = name
            self.h = nc.semaphore(name).__enter__()
            self.n = 0

        def bump(self, k=1):
            self.n += k
            return self.n

    s_xa = [Sem("s_xa0"), Sem("s_xa1"), Sem("s_xa2")]
    g_cst = Sem("g_cst")
    g_cs = Sem("g_cs")
    g_wB = Sem("g_wB")
    g_slab = [Sem("g_sg"), Sem("g_su"), Sem("g_sd")]
    g_wo = [Sem("g_wo0"), Sem("g_wo1")]
    g_kn = Sem("g_kn")
    g_cb = Sem("g_cb")
    g_ms = Sem("g_ms")
    g_cc = Sem("g_cc")
    s_kst = Sem("s_kst")
    s_kn2 = Sem("s_kn2")
    s_vst = Sem("s_vst")
    s_cd1 = Sem("s_cd1")
    s_cd2 = Sem("s_cd2")
    s_kv = [Sem("s_kv0"), Sem("s_kv1")]
    s_ysp = [Sem("s_yspa"), Sem("s_yspb"), Sem("s_yspc")]
    s_yb = [Sem("s_yb0"), Sem("s_yb1")]
    s_xr = [Sem("s_xr0"), Sem("s_xr1")]
    s_out = [Sem("s_out0"), Sem("s_out1"), Sem("s_out2"), Sem("s_out3")]
    scsem = Sem("scsem")
    expsem = Sem("expsem")
    pvsem = Sem("pvsem")
    dacc = Sem("dacc")
    pacc = Sem("pacc")
    vfin = Sem("vfin")
    TE = Sem("TE")
    V = Sem("V")
    SC = Sem("SC")

    ops = {k: [] for k in ("sync", "gpsimd", "tensor", "scalar", "vector")}
    max_waited = {}

    def em(eng, fn):
        ops[eng].append(fn)

    def wait(eng, sem, v):
        if v > 0:
            max_waited[sem] = max(max_waited.get(sem, 0), v)
            em(eng, lambda e, h=sem.h, vv=v: e.wait_ge(h, vv))

    def dma(eng, sem, out, in_):
        tgt = sem.bump(16)
        em(eng, lambda e, oo=out, ii=in_, h=sem.h: e.dma_start(out=oo, in_=ii).then_inc(h, 16))
        return tgt

    def mm(pp, w, r, st, sp, inc=None, inc_amt=1):
        if inc is None:
            em("tensor", lambda e, o=pp, ww=w, rr=r, s1=st, s2=sp:
               e.matmul(o, ww, rr, start=s1, stop=s2, skip_group_check=True))
        else:
            em("tensor", lambda e, o=pp, ww=w, rr=r, s1=st, s2=sp, h=inc.h, a=inc_amt:
               e.matmul(o, ww, rr, start=s1, stop=s2, skip_group_check=True).then_inc(h, a))

    ident_bf = bb(O_IDBF, 128)
    ident_f32 = bf(O_IDF32, 128)
    ones_bf = bb(O_ONES, 128)
    ones_f32 = bf(O_ONESF, 128)

    dma("gpsimd", g_cst, bb(O_IDBF, 128), id_bf_d[:, :])
    dma("gpsimd", g_cst, bf(O_IDF32, 128), id_f32_d[:, :])
    for od, dd in ((O_BQD, bqd_d), (O_BKV, bkv_d), (O_BQU, bqu_d),
                   (O_BKN, bkn_d), (O_BKR, bkr_d), (O_BVU, bvu_d), (O_BO, bo_d)):
        dma("gpsimd", g_cst, bf(od, dd.shape[1]), dd[:, :])
    dma("gpsimd", g_cst, bb(O_RTR, DC * NE).rearrange("p (dc e) -> p dc e", dc=DC),
        rt_d.ap().rearrange("(dc p) e -> p dc e", p=128))
    CST_ALL = g_cst.n
    em("gpsimd", lambda e: e.memset(bb(O_ONES, 128), 1.0).then_inc(g_ms.h, 1))
    ONES_OK = g_ms.bump()
    em("gpsimd", lambda e, v=float(cfg.eps): e.memset(bf(O_EPS, 1), v).then_inc(g_ms.h, 1))
    EPS_OK = g_ms.bump()

    xnT = bb(O_XNT, DC * T)
    A_VDONE = {}
    A_TCOPY = {}
    A_TTR = {}
    A_SCEND = {}
    for qt in range(QT):
        pr = qt % 3
        p2 = qt % 2
        sb = 8 * p2
        if qt >= 3:
            wait("sync", V, A_VDONE[qt - 3])
        xld = dma("sync", s_xa[pr], bf(A_X[pr], D), x_d[qt * 128:(qt + 1) * 128, :])
        xap = bf(A_X[pr], D)

        wait("scalar", s_xa[pr], xld)
        if qt >= 2:
            wait("scalar", SC, A_SCEND[qt - 2])
            wait("scalar", V, A_VDONE[qt - 2])
        em("scalar", lambda e, x=xap, oo=bf(A_XSQ, D), a=bf(*small(sb + 0)):
           e.activation(oo, x, AF.Copy, accum_out=a).then_inc(SC.h, 1))
        sc_sum = SC.bump()
        wait("scalar", SC, sc_sum)
        em("scalar", lambda e, x=xap, oo=bf(A_XSQ, D), a=bf(*small(sb + 2)):
           e.activation(oo, x, AF.Square, accum_out=a).then_inc(SC.h, 1))
        sc_sq = SC.bump()
        wait("vector", SC, sc_sum)
        if qt >= 2:
            wait("vector", V, A_VDONE[qt - 2])
        em("vector", lambda e, i=bf(*small(sb + 0)), oo=bf(*small(sb + 1)):
           e.tensor_scalar_mul(oo, i, 1.0 / D).then_inc(V.h, 1))
        v_mean = V.bump()
        wait("vector", V, v_mean)
        em("vector", lambda e, m=bf(*small(sb + 1)), oo=bf(*small(sb + 3)):
           e.tensor_mul(oo, m, m).then_inc(V.h, 1))
        v_m2 = V.bump()
        wait("vector", SC, sc_sq)
        wait("vector", V, v_m2)
        em("vector", lambda e, sq=bf(*small(sb + 2)), m2=bf(*small(sb + 3)), oo=bf(*small(sb + 4)):
           e.scalar_tensor_tensor(oo, sq, 1.0 / D, m2, ALU.mult, ALU.subtract).then_inc(V.h, 1))
        v_var = V.bump()
        wait("scalar", V, v_var)
        if qt == 0:
            wait("scalar", g_ms, EPS_OK)
        em("scalar", lambda e, i=bf(*small(sb + 4)), oo=bf(*small(sb + 5)), b=bf(O_EPS, 1):
           e.activation(oo, i, AF.Ln, bias=b).then_inc(SC.h, 1))
        sc_ln = SC.bump()
        wait("scalar", SC, sc_ln)
        em("scalar", lambda e, i=bf(*small(sb + 5)), oo=bf(*small(sb + 6)):
           e.activation(oo, i, AF.Exp, scale=-0.5).then_inc(SC.h, 1))
        sc_rs = SC.bump()
        wait("vector", SC, sc_rs)
        if qt >= 2:
            wait("vector", TE, A_TTR[qt - 2])
        em("vector", lambda e, x=xap, m=bf(*small(sb + 1)), r=bf(*small(sb + 6)), oo=bb(A_XNB[p2], D):
           e.tensor_scalar(oo, x, m, r, ALU.subtract, ALU.mult).then_inc(V.h, 1))
        v_xn = V.bump()
        A_VDONE[qt] = v_xn
        A_SCEND[qt] = SC.n

        ppr = P[(qt % 2) + (2 if qt >= 4 else 0)]
        pview = ppr[:, 0:(DC * 128) // 2].bitcast(BF16)
        if qt == 0:
            wait("tensor", g_cst, CST_ALL)
        wait("tensor", V, v_xn)
        if qt not in (0, 1, 4, 5):
            wait("tensor", V, A_TCOPY[qt - 2])
        for dc in range(DC):
            if dc == DC - 1:
                em("tensor", lambda e, oo=pview[:, dc * 128:(dc + 1) * 128],
                   i=bb(A_XNB[p2] + dc * 256, 128), idt=ident_bf:
                   e.transpose(oo, i, idt).then_inc(TE.h, 1))
            else:
                em("tensor", lambda e, oo=pview[:, dc * 128:(dc + 1) * 128],
                   i=bb(A_XNB[p2] + dc * 256, 128), idt=ident_bf:
                   e.transpose(oo, i, idt))
        A_TTR[qt] = TE.bump()
        wait("vector", TE, A_TTR[qt])
        dst = xnT.rearrange("p (dc t) -> p dc t", dc=DC)[:, :, qt * 128:(qt + 1) * 128]
        src = pview.rearrange("p (dc i) -> p dc i", dc=DC)
        em("vector", lambda e, oo=dst, i=src: e.tensor_copy(oo, i).then_inc(V.h, 1))
        A_TCOPY[qt] = V.bump()

    A_DONE_V = V.n
    A_DONE_SC = SC.n

    def mm_group(pp, tiles, inc_sem):
        n = len(tiles)
        for i, (po, w, r, st, sp) in enumerate(tiles):
            if i == n - 1:
                em("tensor", lambda e, o=po, ww=w, rr=r, s1=st, s2=sp, h=inc_sem.h:
                   e.matmul(o, ww, rr, start=s1, stop=s2, skip_group_check=True).then_inc(h, 1))
            else:
                em("tensor", lambda e, o=po, ww=w, rr=r, s1=st, s2=sp:
                   e.matmul(o, ww, rr, start=s1, stop=s2, skip_group_check=True))
        return inc_sem.bump()

    def mm_proj2(rhs_off, rhs_nchunks, w_off, nout, dst_writer,
                 pre_tensor_wait=None, psel=None, first_wait=None, f_waits=None):
        first_v = V.n if first_wait is None else first_wait
        psel = psel or [P[0], P[1]]
        evac_v = {}
        for oc in range(nout):
            pp = psel[oc % len(psel)]
            if oc == 0 and pre_tensor_wait is not None:
                pre_tensor_wait()
            if oc < len(psel):
                wait("tensor", V, first_v)
            else:
                wait("tensor", V, evac_v[oc - len(psel)])
            if f_waits is None:
                tiles = []
                for c in range(rhs_nchunks):
                    w = bb(w_off + 2 * (c * nout + oc) * 128, 128)
                    for f in range(NF):
                        tiles.append((pp[:, f * NW:(f + 1) * NW], w,
                                      bb(rhs_off + 2 * (c * T + f * NW), NW),
                                      c == 0, c == rhs_nchunks - 1))
                t_mm = mm_group(pp, tiles, TE)
            else:
                for f in range(NF):
                    if oc == 0 and f in f_waits:
                        wait("tensor", f_waits[f][0], f_waits[f][1])
                    tiles = [(pp[:, f * NW:(f + 1) * NW],
                              bb(w_off + 2 * (c * nout + oc) * 128, 128),
                              bb(rhs_off + 2 * (c * T + f * NW), NW),
                              c == 0, c == rhs_nchunks - 1)
                             for c in range(rhs_nchunks)]
                    t_mm = mm_group(pp, tiles, TE)
            wait("vector", TE, t_mm)
            evac_v[oc] = dst_writer(oc, pp)
        return evac_v

    def evac_bias(dst_off, bias_off, f32out=False, first_wait=None):
        def w(oc, pp):
            if oc == 0 and first_wait is not None:
                wait("vector", first_wait[0], first_wait[1])
            bias = bf(bias_off + 4 * oc, 1)
            if f32out:
                oap = bf(dst_off + 4 * oc * T, T)
            else:
                oap = bb(dst_off + 2 * oc * T, T)
            em("vector", lambda e, oo=oap, i=pp[:, 0:T], b=bias:
               e.tensor_scalar(oo, i, b, None, ALU.add).then_inc(V.h, 1))
            return V.bump()
        return w

    wait("sync", V, A_DONE_V)
    wait("sync", SC, A_DONE_SC)
    dma("sync", g_cs, bf(B_COS, JH * T).rearrange("p (j t) -> p j t", j=JH),
        cos_d.ap().rearrange("(j p) t -> p j t", p=128))
    dma("sync", g_cs, bf(B_SIN, JH * T).rearrange("p (j t) -> p j t", j=JH),
        sin_d.ap().rearrange("(j p) t -> p j t", p=128))
    CS_ALL = g_cs.n

    def loadB(src_ap, te_wait=None):
        wait("gpsimd", TE, TE.n if te_wait is None else te_wait)
        wait("gpsimd", g_wB, g_wB.n)
        nch = src_ap.shape[0] // 128
        return dma("gpsimd", g_wB,
                   bb(B_W, nch * src_ap.shape[1]).rearrange("p (c n) -> p c n", c=nch),
                   src_ap.rearrange("(c p) n -> p c n", p=128))

    if B_W >= A_END:
        ld = loadB(wkvd_d.ap(), te_wait=0)
    else:
        wait("gpsimd", V, A_DONE_V)
        wait("gpsimd", SC, A_DONE_SC)
        ld = loadB(wkvd_d.ap())

    def pre_kv():
        wait("tensor", g_wB, ld)
    mm_proj2(O_XNT, DC, B_W, LC, evac_bias(B_KVL, O_BKV), pre_tensor_wait=pre_kv,
             first_wait=A_TCOPY[QT // NF - 1],
             f_waits={f: (V, A_TCOPY[(f + 1) * (QT // NF) - 1]) for f in range(NF)})

    ldn = loadB(wkn_d.ap())
    mm_proj2(B_KVL, LC, B_W, NC2, evac_bias(B_KT, O_BKN),
             pre_tensor_wait=lambda: wait("tensor", g_wB, ldn))

    ldr = loadB(wkr_d.ap())
    KR_EV = mm_proj2(B_KVL, LC, B_W, 2 * JH, evac_bias(B_KR, O_BKR, f32out=True),
                     pre_tensor_wait=lambda: wait("tensor", g_wB, ldr))

    wait("vector", g_cs, CS_ALL)
    wait("vector", V, KR_EV[2 * JH - 1])
    for jc in range(JH):
        x1 = bf(B_KR + 4 * jc * T, T)
        x2 = bf(B_KR + 4 * (JH + jc) * T, T)
        co = bf(B_COS + 4 * jc * T, T)
        si = bf(B_SIN + 4 * jc * T, T)
        em("vector", lambda e, oo=bf(B_RT1, T), a=x1, b=co: e.tensor_mul(oo, a, b).then_inc(V.h, 1))
        V.bump()
        em("vector", lambda e, oo=bf(B_RT2, T), a=x2, b=si: e.tensor_mul(oo, a, b).then_inc(V.h, 1))
        v2 = V.bump()
        wait("vector", V, v2)
        em("vector", lambda e, oo=bb(B_KT + 2 * (NC2 + jc) * T, T), a=bf(B_RT1, T), b=bf(B_RT2, T):
           e.tensor_sub(oo, a, b).then_inc(V.h, 1))
        v3 = V.bump()
        wait("vector", V, v3)
        em("vector", lambda e, oo=bf(B_RT1, T), a=x1, b=si: e.tensor_mul(oo, a, b).then_inc(V.h, 1))
        V.bump()
        em("vector", lambda e, oo=bf(B_RT2, T), a=x2, b=co: e.tensor_mul(oo, a, b).then_inc(V.h, 1))
        v5 = V.bump()
        wait("vector", V, v5)
        em("vector", lambda e, oo=bb(B_KT + 2 * (NC2 + JH + jc) * T, T), a=bf(B_RT1, T), b=bf(B_RT2, T):
           e.tensor_add(oo, a, b).then_inc(V.h, 1))
        v6 = V.bump()
        if jc < JH - 1:
            wait("vector", V, v6)
    KT_DONE_V = V.n

    wait("sync", V, KT_DONE_V)
    for q in range(NQ):
        kst = dma("sync", s_kst, kT_locQ[q].ap().rearrange("(dc p) t -> p dc t", p=128),
                  bb(B_KT + q * (DC // NQ) * 2 * T, (DC // NQ) * T)
                  .rearrange("p (dc t) -> p dc t", dc=DC // NQ))
    KST_ALL = kst

    ldq = loadB(wqd_d.ap())
    QD_EV = mm_proj2(O_XNT, DC, B_W, LC,
                     evac_bias(O_QL, O_BQD, first_wait=(V, KT_DONE_V)),
                     pre_tensor_wait=lambda: wait("tensor", g_wB, ldq),
                     first_wait=KR_EV[2 * JH - 1])

    wait("gpsimd", V, KT_DONE_V)
    wait("gpsimd", g_wB, g_wB.n)
    ldv = dma("gpsimd", g_wB, bb(C_WG, LC * D).rearrange("p (c n) -> p c n", c=LC),
              wvu_d.ap().rearrange("(c p) n -> p c n", p=128))
    wait("gpsimd", g_wB, g_wB.n)
    ldqu = dma("gpsimd", g_wB, bb(C_WU, LC * D).rearrange("p (c n) -> p c n", c=LC),
               wqu_d.ap().rearrange("(c p) n -> p c n", p=128))

    KSQ_SC = {}
    KN_MM = {}
    KN_V = {}
    for h in range(NH):
        half = h // 8
        if h % 8 == 0:
            wait("scalar", V, KT_DONE_V)
            if half == 1:
                wait("scalar", TE, KN_MM[7])
            em("scalar", lambda e, oo=bb(O_NSQ, 8 * T), i=bb(B_KT + 2 * half * 8 * T, 8 * T):
               e.activation(oo, i, AF.Square).then_inc(SC.h, 1))
            KSQ_SC[half] = SC.bump()
        if h == 0:
            wait("tensor", g_ms, ONES_OK)
        wait("tensor", SC, KSQ_SC[half])
        if h >= 2:
            wait("tensor", V, KN_V[h - 2])
        pp = P[2 + (h % 2)]
        tiles = [(pp[:, f * NW:(f + 1) * NW], ones_bf,
                  bb(O_NSQ + 2 * ((h % 8) * T + f * NW), NW), True, True)
                 for f in range(NF)]
        KN_MM[h] = mm_group(pp, tiles, TE)
        wait("vector", TE, KN_MM[h])
        em("vector", lambda e, oo=bf(O_KN2 + 4 * h, 1), i=pp[:, 0:T]:
           e.reduce_max(oo, i, axis=AXX).then_inc(V.h, 1))
        KN_V[h] = V.bump()
    KN_DONE_V = V.n

    wait("sync", V, KN_DONE_V)
    knst = dma("sync", s_kn2, kn_loc[0:1, :], bf(O_KN2, NH, parts=1))

    vsb = bb(B_VSB, QT * D)
    VT_COPY = {}
    VT_TR = {}

    def evac_v_writer(dc, pp):
        pr = dc % 2
        bias = bf(O_BVU + 4 * dc, 1)
        if dc >= 2:
            wait("vector", TE, VT_TR[dc - 2])
        em("vector", lambda e, oo=bb(B_VT[pr], T), i=pp[:, 0:T], b=bias:
           e.tensor_scalar(oo, i, b, None, ALU.add).then_inc(V.h, 1))
        v_e = V.bump()
        wait("tensor", V, v_e)
        ptr = P[2 + pr]
        pview = ptr[:, 0:(QT * 128) // 2].bitcast(BF16)
        if dc < 2:
            wait("tensor", V, KN_V[NH - 2 + dc])
        else:
            wait("tensor", V, VT_COPY[dc - 2])
        for qt in range(QT):
            if qt == QT - 1:
                em("tensor", lambda e, oo=pview[:, qt * 128:(qt + 1) * 128],
                   i=bb(B_VT[pr] + 2 * qt * 128, 128), idt=ident_bf:
                   e.transpose(oo, i, idt).then_inc(TE.h, 1))
            else:
                em("tensor", lambda e, oo=pview[:, qt * 128:(qt + 1) * 128],
                   i=bb(B_VT[pr] + 2 * qt * 128, 128), idt=ident_bf:
                   e.transpose(oo, i, idt))
        VT_TR[dc] = TE.bump()
        wait("vector", TE, VT_TR[dc])
        dst = vsb.rearrange("p (qt d) -> p qt d", qt=QT)[:, :, dc * 128:(dc + 1) * 128]
        src = pview.rearrange("p (qt i) -> p qt i", qt=QT)
        em("vector", lambda e, oo=dst, i=src: e.tensor_copy(oo, i).then_inc(V.h, 1))
        VT_COPY[dc] = V.bump()
        return VT_COPY[dc]

    def pre_v():
        wait("tensor", g_wB, ldv)
        wait("tensor", V, KT_DONE_V)
    mm_proj2(B_KVL, LC, C_WG, DC, evac_v_writer, pre_tensor_wait=pre_v,
             first_wait=QD_EV[LC - 1])
    V_TE = TE.n

    wait("sync", V, VT_COPY[DC - 1])
    for q in range(NQ):
        vst = dma("sync", s_vst, v_locQ[q].ap().rearrange("(qt p) d -> p qt d", p=128),
                  vsb.rearrange("p (qt d) -> p qt d", qt=QT)[:, :, q * (D // NQ):(q + 1) * (D // NQ)])
    VST_ALL = vst

    QT_EV = mm_proj2(O_QL, LC, C_WU, DC, evac_bias(O_QTT, O_BQU),
                     pre_tensor_wait=lambda: wait("tensor", g_wB, ldqu))
    QU_TE = TE.n

    sgroups = [(0, 1), (1, 1), (2, 2)] + [(g0 + 4, gs) for g0, gs in _groups(cfg.ISC - 4, cfg.IG)]
    rgroups = _groups(cfg.IRC, cfg.IG)
    allgroups = [("s", g0, gs) for g0, gs in sgroups] + [("r", g0, gs) for g0, gs in rgroups]
    g0_gs = allgroups[0][2]
    wait("gpsimd", TE, V_TE)
    lgld0 = dma("gpsimd", g_slab[0],
                bb(C_WG, DC * g0_gs * 128).rearrange("p (dc i) -> p dc i", dc=DC),
                wg_d.ap()[:, 0:g0_gs * 128].rearrange("(dc p) i -> p dc i", p=128))
    wait("gpsimd", TE, QU_TE)
    luld0 = dma("gpsimd", g_slab[1],
                bb(C_WU, DC * g0_gs * 128).rearrange("p (dc i) -> p dc i", dc=DC),
                wu_d.ap()[:, 0:g0_gs * 128].rearrange("(dc p) i -> p dc i", p=128))
    ldld0 = dma("gpsimd", g_slab[2],
                bb(C_WD, g0_gs * D).rearrange("p (il c) -> p il c", il=g0_gs),
                wd_d.ap()[0:g0_gs * 128, :].rearrange("(il p) c -> p il c", p=128))

    CC_K = {}
    CC_V = {}

    def emit_cc_k(q):
        wait("gpsimd", s_kst, KST_ALL)
        em("gpsimd", lambda e: e.collective_compute(
            "AllGather", ALU.bypass, replica_groups=[list(range(C))],
            ins=[kT_locQ[q].ap().opt()], outs=[kT_gatQ[q].ap().opt()]).then_inc(g_cc.h, 1))
        CC_K[q] = g_cc.bump()

    def emit_cc_v(q):
        wait("gpsimd", s_vst, VST_ALL)
        em("gpsimd", lambda e: e.collective_compute(
            "AllGather", ALU.bypass, replica_groups=[list(range(C))],
            ins=[v_locQ[q].ap().opt()], outs=[v_gatQ[q].ap().opt()]).then_inc(g_cc.h, 1))
        CC_V[q] = g_cc.bump()

    emit_cc_k(0)
    wait("gpsimd", s_kn2, knst)
    em("gpsimd", lambda e: e.collective_compute(
        "AllReduce", ALU.max, replica_groups=[list(range(C))],
        ins=[kn_loc.ap().opt()], outs=[kn_gat.ap().opt()]).then_inc(g_cc.h, 1))
    CC_KN = g_cc.bump()

    QSQ_SC = {}
    QN_MM = {}
    QN_V = {}
    for h in range(NH):
        half = h // 8
        if h % 8 == 0:
            wait("scalar", V, QT_EV[DC - 1])
            wait("scalar", s_vst, VST_ALL)
            wait("scalar", TE, KN_MM[15] if half == 0 else QN_MM[7])
            em("scalar", lambda e, oo=bb(O_NSQ, 8 * T), i=bb(O_QTT + 2 * half * 8 * T, 8 * T):
               e.activation(oo, i, AF.Square).then_inc(SC.h, 1))
            QSQ_SC[half] = SC.bump()
        wait("tensor", SC, QSQ_SC[half])
        if h < 2:
            wait("tensor", V, VT_COPY[DC - 2 + h])
        else:
            wait("tensor", V, QN_V[h - 2])
        pp = P[2 + (h % 2)]
        tiles = [(pp[:, f * NW:(f + 1) * NW], ones_bf,
                  bb(O_NSQ + 2 * ((h % 8) * T + f * NW), NW), True, True)
                 for f in range(NF)]
        QN_MM[h] = mm_group(pp, tiles, TE)
        wait("vector", TE, QN_MM[h])
        em("vector", lambda e, oo=bf(O_QN2 + 4 * h, 1), i=pp[:, 0:T]:
           e.reduce_max(oo, i, axis=AXX).then_inc(V.h, 1))
        QN_V[h] = V.bump()
    QN_DONE_V = V.n

    wait("tensor", V, QN_V[NH - 1])
    lg_tiles = []
    for qt in range(QT):
        for dc in range(DC):
            lg_tiles.append((P[3][:, qt * NE:(qt + 1) * NE],
                             bb(O_XNT + 2 * (dc * T + qt * 128), 128),
                             bb(O_RTR + 2 * dc * NE, NE),
                             dc == 0, dc == DC - 1))
    t_lg = mm_group(None, lg_tiles, TE)
    wait("vector", TE, t_lg)
    em("vector", lambda e, oo=bf(O_LGS, NE * QT), i=P[3][:, 0:NE * QT]:
       e.tensor_copy(oo, i).then_inc(V.h, 1))
    v_lg = V.bump()
    wait("vector", V, v_lg)
    em("vector", lambda e, oo=bf(O_M1, QT),
       i=bf(O_LGS, NE * QT).rearrange("p (q n) -> p q n", q=QT):
       e.reduce_max(oo, i, axis=AXX).then_inc(V.h, 1))
    v_m1 = V.bump()
    wait("vector", V, v_m1)
    for qt in range(QT):
        em("vector", lambda e, oo=bf(O_MSK + 4 * qt * NE, NE), i=bf(O_LGS + 4 * qt * NE, NE),
           m=bf(O_M1 + 4 * qt, 1):
           e.tensor_scalar(oo, i, m, None, ALU.is_ge).then_inc(V.h, 1))
        V.bump()
    v_eq = V.n
    wait("vector", V, v_eq)
    for qt in range(QT):
        em("vector", lambda e, oo=bf(O_MSK + 4 * qt * NE, NE), i=bf(O_MSK + 4 * qt * NE, NE),
           l=bf(O_LGS + 4 * qt * NE, NE):
           e.scalar_tensor_tensor(oo, i, -1e30, l, ALU.mult, ALU.add).then_inc(V.h, 1))
        V.bump()
    v_mk = V.n
    wait("vector", V, v_mk)
    em("vector", lambda e, oo=bf(O_M2, QT),
       i=bf(O_MSK, NE * QT).rearrange("p (q n) -> p q n", q=QT):
       e.reduce_max(oo, i, axis=AXX).then_inc(V.h, 1))
    v_m2 = V.bump()
    wait("vector", V, v_m2)
    em("vector", lambda e, oo=bf(O_M2, QT), a=bf(O_M2, QT), b=bf(O_M1, QT):
       e.tensor_sub(oo, a, b).then_inc(V.h, 1))
    v_d = V.bump()
    wait("scalar", V, v_d)
    em("scalar", lambda e, oo=bf(O_M2, QT), i=bf(O_M2, QT):
       e.activation(oo, i, AF.Exp).then_inc(SC.h, 1))
    sc_t = SC.bump()
    wait("vector", SC, sc_t)
    em("vector", lambda e, oo=bf(O_M1, QT), i=bf(O_M2, QT):
       e.tensor_scalar_add(oo, i, 1.0).then_inc(V.h, 1))
    v_o = V.bump()
    wait("vector", V, v_o)
    em("vector", lambda e, oo=bf(O_M2, QT), i=bf(O_M1, QT):
       e.reciprocal(oo, i).then_inc(V.h, 1))
    v_r = V.bump()
    wait("vector", V, v_r)
    em("vector", lambda e, oo=bf(O_COMBT, QT), a=bf(O_M1, QT), b=bf(O_M2, QT):
       e.tensor_mul(oo, a, b).then_inc(V.h, 1))
    v_combtok = V.bump()
    wait("tensor", V, v_combtok)
    em("tensor", lambda e, oo=P[2][0:max(QT, 1), 0:128], i=bf(O_COMBT, QT), idt=ident_f32:
       e.transpose(oo, i, idt).then_inc(TE.h, 1))
    t_ct = TE.bump()
    wait("vector", TE, t_ct)
    em("vector", lambda e, oo=bf(O_COMBT, 128, parts=QT), i=P[2][0:QT, 0:128]:
       e.tensor_copy(oo, i).then_inc(V.h, 1))
    v_ctc = V.bump()
    wait("sync", V, v_ctc)
    cbst = dma("sync", s_cd2, comb_dr[:, :], bf(O_COMBT, 128, parts=QT))

    yacc = bf(C_Y, DC * T)
    NSL = (DC + 1) // 2

    def load_wo(sidx):
        pr = sidx % 2
        ncc = min(2, DC - sidx * 2)
        wait("gpsimd", g_wo[pr], g_wo[pr].n)
        dma("gpsimd", g_wo[pr],
            bb(F_WO[pr], DC * ncc * 128).rearrange("p (dc c) -> p dc c", dc=DC),
            wo_d.ap()[:, sidx * 256:sidx * 256 + ncc * 128]
            .rearrange("(dc p) c -> p dc c", p=128))

    C_SILU = {}
    C_HM = {}
    C_YACC_V = {}
    C_TDOWN = {}
    C_GLAST = {}
    C_ULAST = {}

    for gi, (kind, g0, gs) in enumerate(allgroups):
        gsrc = (wg_d, wu_d, wd_d) if kind == "s" else (rg_d, ru_d, rd_d)
        if gi == 0:
            lgld, luld, ldld = lgld0, luld0, ldld0
        else:
            wait("gpsimd", TE, C_GLAST[gi - 1])
            wait("gpsimd", g_slab[0], g_slab[0].n)
            lgld = dma("gpsimd", g_slab[0],
                       bb(C_WG, DC * gs * 128).rearrange("p (dc i) -> p dc i", dc=DC),
                       gsrc[0].ap()[:, g0 * 128:(g0 + gs) * 128].rearrange("(dc p) i -> p dc i", p=128))
            wait("gpsimd", TE, C_ULAST[gi - 1])
            wait("gpsimd", g_slab[1], g_slab[1].n)
            luld = dma("gpsimd", g_slab[1],
                       bb(C_WU, DC * gs * 128).rearrange("p (dc i) -> p dc i", dc=DC),
                       gsrc[1].ap()[:, g0 * 128:(g0 + gs) * 128].rearrange("(dc p) i -> p dc i", p=128))
            wait("gpsimd", TE, C_TDOWN[gi - 1])
            wait("gpsimd", g_slab[2], g_slab[2].n)
            ldld = dma("gpsimd", g_slab[2],
                       bb(C_WD, gs * D).rearrange("p (il c) -> p il c", il=gs),
                       gsrc[2].ap()[g0 * 128:(g0 + gs) * 128, :].rearrange("(il p) c -> p il c", p=128))
        if gi == 2:
            emit_cc_v(0)
        if gi == 4:
            emit_cc_k(1)
        if gi == 6:
            emit_cc_v(1)
        if gi == 8:
            emit_cc_k(2)
        if gi == 11:
            emit_cc_v(2)
        if gi == 13:
            emit_cc_k(3)
        if gi == 15:
            emit_cc_v(3)
        if gi == 14:
            em("gpsimd", lambda e: e.memset(bf(O_ONESF, 128), 1.0).then_inc(g_ms.h, 1))
            ONESF_OK = g_ms.bump()
        if gi == 10:
            wait("gpsimd", g_cc, CC_KN)
            knld = dma("gpsimd", g_kn, bf(O_KNG, NH, parts=1), kn_gat[0:1, :])
        if gi == 8:
            wait("gpsimd", s_cd2, cbst)
            cbld = dma("gpsimd", g_cb, bb(O_COMB, T), bass.AP(comb_dr, 0, [[0, 128], [1, T]]))
        if gi == 12:
            wait("vector", g_kn, knld)
            em("vector", lambda e, oo=bf(O_CROW, NH, parts=1), a=bf(O_QN2, NH, parts=1),
               b=bf(O_KNG, NH, parts=1):
               e.tensor_mul(oo, a, b).then_inc(V.h, 1))
            v_c1 = V.bump()
            wait("scalar", V, v_c1)
            em("scalar", lambda e, oo=bf(O_CROW2, NH, parts=1), i=bf(O_CROW, NH, parts=1):
               e.activation(oo, i, AF.Sqrt).then_inc(SC.h, 1))
            sc_cs = SC.bump()
            wait("vector", SC, sc_cs)
            em("vector", lambda e, oo=bf(O_CROW, NH, parts=1), i=bf(O_CROW2, NH, parts=1):
               e.tensor_scalar_mul(oo, i, -1.05 * scale).then_inc(V.h, 1))
            v_cn = V.bump()
            wait("sync", V, v_cn)
            cst2 = dma("sync", s_cd1, cneg_dr[0:1, :], bf(O_CROW, NH, parts=1))
        for il in range(gs):
            if gi == 0 and il == 0:
                wait("tensor", V, QT_EV[DC - 1])
            wait("tensor", g_slab[0], lgld)
            prev_silu = C_SILU.get((gi, il - 1)) or (C_SILU.get((gi - 1, allgroups[gi - 1][2] - 1)) if gi > 0 else 0)
            wait("tensor", SC, prev_silu or 0)
            tiles = []
            for dc in range(DC):
                w = bb(C_WG + 2 * (dc * gs * 128 + il * 128), 128)
                for f in range(NF):
                    tiles.append((P[0][:, f * NW:(f + 1) * NW], w,
                                  bb(O_XNT + 2 * (dc * T + f * NW), NW),
                                  dc == 0, dc == DC - 1))
            t_g = mm_group(None, tiles, TE)
            wait("tensor", g_slab[1], luld)
            prev_hm = C_HM.get((gi, il - 1)) or (C_HM.get((gi - 1, allgroups[gi - 1][2] - 1)) if gi > 0 else 0)
            wait("tensor", V, prev_hm or 0)
            tiles = []
            for dc in range(DC):
                w = bb(C_WU + 2 * (dc * gs * 128 + il * 128), 128)
                for f in range(NF):
                    tiles.append((P[1][:, f * NW:(f + 1) * NW], w,
                                  bb(O_XNT + 2 * (dc * T + f * NW), NW),
                                  dc == 0, dc == DC - 1))
            t_u = mm_group(None, tiles, TE)
            C_GLAST[gi] = t_g
            C_ULAST[gi] = t_u
            wait("scalar", TE, t_g)
            wait("scalar", V, prev_hm or 0)
            em("scalar", lambda e, oo=bb(C_SG, T), i=P[0][:, 0:T]:
               e.activation(oo, i, AF.Silu).then_inc(SC.h, 1))
            C_SILU[gi, il] = SC.bump()
            wait("vector", SC, C_SILU[gi, il])
            wait("vector", TE, t_u)
            if gi >= 2:
                wait("vector", TE, C_TDOWN[gi - 2])
            if kind == "s":
                em("vector", lambda e, oo=bb(C_H[gi % 2] + 2 * il * T, T), a=bb(C_SG, T),
                   b=P[1][:, 0:T]: e.tensor_mul(oo, a, b).then_inc(V.h, 1))
                C_HM[gi, il] = V.bump()
            else:
                em("vector", lambda e, oo=bb(C_T2, T), a=bb(C_SG, T), b=P[1][:, 0:T]:
                   e.tensor_mul(oo, a, b).then_inc(V.h, 1))
                v_t2 = V.bump()
                wait("vector", V, v_t2)
                wait("vector", g_cb, cbld)
                em("vector", lambda e, oo=bb(C_H[gi % 2] + 2 * il * T, T), a=bb(C_T2, T),
                   b=bb(O_COMB, T): e.tensor_mul(oo, a, b).then_inc(V.h, 1))
                C_HM[gi, il] = V.bump()
        wait("tensor", g_slab[2], ldld)
        wait("tensor", V, C_HM[gi, gs - 1])
        for cc in range(DC):
            pp = P[2 + (cc % 2)]
            if cc >= 2:
                wait("tensor", V, C_YACC_V[gi, cc - 2])
            elif gi > 0:
                wait("tensor", V, C_YACC_V[gi - 1, DC - 2 + cc])
            else:
                wait("tensor", V, v_ctc)
            tiles = []
            for il in range(gs):
                w = bb(C_WD + 2 * (il * D + cc * 128), 128)
                for f in range(NF):
                    tiles.append((pp[:, f * NW:(f + 1) * NW], w,
                                  bb(C_H[gi % 2] + 2 * (il * T + f * NW), NW),
                                  il == 0, il == gs - 1))
            t_d = mm_group(None, tiles, TE)
            wait("vector", TE, t_d)
            if gi == 0:
                if cc == 0:
                    wait("vector", s_kst, KST_ALL)
                    wait("vector", s_vst, VST_ALL)
                em("vector", lambda e, oo=bf(C_Y + 4 * cc * T, T), i=pp[:, 0:T]:
                   e.tensor_copy(oo, i).then_inc(V.h, 1))
            else:
                wait("vector", V, C_YACC_V[gi - 1, cc])
                em("vector", lambda e, oo=bf(C_Y + 4 * cc * T, T), i=pp[:, 0:T],
                   a=bf(C_Y + 4 * cc * T, T):
                   e.tensor_add(oo, a, i).then_inc(V.h, 1))
            C_YACC_V[gi, cc] = V.bump()
        C_TDOWN[gi] = TE.n

    FFN_LAST_V = C_YACC_V[len(allgroups) - 1, DC - 1]
    C_END_SC = SC.n

    wait("gpsimd", s_cd1, cst2)
    cngld = dma("gpsimd", g_kn, bf(O_CNEG, NH),
                bass.AP(cneg_dr, 0, [[0, 128], [1, NH]]))

    wait("gpsimd", TE, C_TDOWN[len(allgroups) - 1])
    load_wo(0)

    LG = len(allgroups) - 1
    wait("scalar", V, C_YACC_V[LG, 3])
    ysp_a = dma("scalar", s_ysp[0], yffn_dr[:, 0:4 * T], bf(C_Y, 4 * T))
    wait("scalar", V, C_YACC_V[LG, 9])
    ysp_b = dma("scalar", s_ysp[1], yffn_dr[:, 4 * T:10 * T], bf(C_Y + 16 * T, 6 * T))
    wait("scalar", V, C_YACC_V[LG, 15])
    ysp_c = dma("scalar", s_ysp[2], yffn_dr[:, 10 * T:16 * T], bf(C_Y + 40 * T, 6 * T))

    for h in range(NH):
        pr = h % 2
        vpr = (h + 1) % 2
        if h == 0:
            wait("sync", TE, C_GLAST[LG])
            wait("sync", g_cc, CC_V[0])
        if h % 4 == 0 and h > 0:
            wait("sync", g_cc, CC_V[h // 4])
        if h >= 2:
            wait("sync", pvsem, (h - 1) * KC)
        ktg = kT_gatQ[h // 4]
        vg = v_gatQ[h // 4]
        hh = h % 4
        if h == 1:
            wait("sync", TE, C_ULAST[LG])
        for r in range(C):
            dma("sync", s_kv[pr], bb(E_KH[pr] + 2 * r * T, T),
                ktg[r * (D // NQ) + hh * 128:r * (D // NQ) + (hh + 1) * 128, :])
        if h == 0:
            wait("sync", s_ysp[0], ysp_a)
        if h == 1:
            wait("sync", TE, C_TDOWN[LG])
        for r in range(C):
            dma("sync", s_kv[pr],
                bb(E_VH[vpr] + 2 * r * T, T).rearrange("p (kc d) -> p kc d", d=128),
                vg[r * T:(r + 1) * T, hh * 128:(hh + 1) * 128]
                .rearrange("(kc p) d -> p kc d", p=128))

    slot_wait = [(expsem, 0), (expsem, 0), (expsem, 0)]
    eslot = [(dacc, 0)] * 4
    pb_rd = [(dacc, 0), (pacc, 0)]
    FIN_V = {}
    COMB_V = {}
    PACC_AT = {}
    DACC_AT = {}

    def emit_combine(fh):
        wait("vector", pacc, PACC_AT[fh])
        wait("vector", dacc, DACC_AT[fh])
        em("vector", lambda e, oo=bf(E_EA, T), a=bf(E_EA, T), b=bf(E_EA2, T):
           e.tensor_tensor(oo, a, b, ALU.add).then_inc(V.h, 1))
        COMB_V[fh] = V.bump()

    DEN_T = {}
    DEN_SLOT = {}

    def emit_den(fh, slot):
        wsem, wval = slot_wait[slot]
        wait("tensor", wsem, wval)
        wait("tensor", V, COMB_V[fh])
        ppD = P[slot]
        tiles = [(ppD[:, f * NW:(f + 1) * NW], ones_f32,
                  bf(E_EA + 4 * f * NW, NW), True, True) for f in range(NF)]
        DEN_T[fh] = mm_group(None, tiles, TE)
        DEN_SLOT[fh] = slot

    def emit_rcp(fh):
        slot = DEN_SLOT[fh]
        wait("scalar", TE, DEN_T[fh])
        if fh >= 1:
            wait("scalar", V, FIN_V[fh - 1])
        em("scalar", lambda e, oo=bf(O_RCPA, T), i=P[slot][:, 0:T]:
           e.activation(oo, i, AF.Ln).then_inc(SC.h, 1))
        sc_ln = SC.bump()
        slot_wait[slot] = (SC, sc_ln)
        wait("scalar", SC, sc_ln)
        em("scalar", lambda e, oo=bf(O_RCPB, T), i=bf(O_RCPA, T):
           e.activation(oo, i, AF.Exp, scale=-1.0).then_inc(SC.h, 1))
        sc_rcp = SC.bump()
        wait("vector", SC, sc_rcp)
        wait("vector", vfin, fh + 1)
        em("vector", lambda e, oo=bb(E_AO + 2 * fh * T, T), a=bb(E_AO + 2 * fh * T, T),
           b=bf(O_RCPB, T): e.tensor_mul(oo, a, b).then_inc(V.h, 1))
        FIN_V[fh] = V.bump()

    for h in range(NH):
        pr = h % 2
        g_end_prev = h * KC
        kv_target = 16 * (2 * C) * (h // 2 + 1)
        wait("tensor", s_kv[pr], kv_target)
        if h == 0:
            wait("tensor", V, FFN_LAST_V)
            wait("tensor", SC, C_END_SC)
            wait("tensor", g_ms, ONESF_OK)
            wait("scalar", g_kn, cngld)
            wait("scalar", s_ysp[1], ysp_b)
            wait("vector", s_ysp[1], ysp_b)
            wait("gpsimd", s_ysp[1], ysp_b)
        for kc in range(KC):
            g = h * KC + kc
            wsem, wval = slot_wait[g % 3]
            wait("tensor", wsem, wval)
            pp = P[g % 3]
            tiles = [(pp[:, f * NW:(f + 1) * NW],
                      bb(E_KH[pr] + 2 * kc * 128, 128),
                      bb(O_QTT + 2 * (h * T + f * NW), NW), True, True)
                     for f in range(NF)]
            mm_group(None, tiles, scsem)
            slot_wait[g % 3] = (expsem, g + 1)
            wait("scalar", scsem, g + 1)
            wait("scalar", pvsem, g - 3)
            if g >= 4:
                esem, ev = eslot[g % 4]
                wait("scalar", esem, ev)
            em("scalar", lambda e, oo=bb(E_E[g % 4], T), i=pp[:, 0:T],
               b=bf(O_CNEG + 4 * h, 1), ssc=scale:
               e.activation(oo, i, AF.Exp, bias=b, scale=ssc).then_inc(expsem.h, 1))
            expsem.bump()
            if h >= 1 and kc == 1:
                emit_combine(h - 1)
            if h >= 1 and kc == 3:
                emit_den(h - 1, g_end_prev % 3)
            if h >= 1 and kc == 5:
                emit_rcp(h - 1)
            if kc % 2 == 1:
                j = kc // 2
                on_pool = j % 4 == 0
                wait("vector", expsem, g + 1)
                if on_pool and j >= 4:
                    wait("vector", pacc, pb_rd[0][1])
                if not on_pool and j >= 2:
                    wait("vector", dacc, pb_rd[1][1])
                em("vector", lambda e, oo=bb(E_PB[0 if on_pool else 1], T),
                   a=bb(E_E[(g - 1) % 4], T),
                   bbb=bb(E_E[g % 4], T), hh=dacc.h:
                   e.tensor_tensor(oo, a, bbb, ALU.add).then_inc(hh, 1))
                pj = dacc.bump()
                eslot[(g - 1) % 4] = (dacc, pj)
                eslot[g % 4] = (dacc, pj)
                if not on_pool:
                    wait("vector", dacc, pj)
                    if j == 1:
                        if h > 0:
                            wait("vector", TE, DEN_T[h - 1])
                        em("vector", lambda e, oo=bf(E_EA, T), i=bb(E_PB[1], T), hh=dacc.h:
                           e.tensor_copy(oo, i).then_inc(hh, 1))
                    else:
                        em("vector", lambda e, oo=bf(E_EA, T), a=bf(E_EA, T),
                           i=bb(E_PB[1], T), hh=dacc.h:
                           e.tensor_tensor(oo, a, i, ALU.add).then_inc(hh, 1))
                    pb_rd[1] = (dacc, dacc.bump())
                else:
                    wait("gpsimd", dacc, pj)
                    if j == 0:
                        if h > 0:
                            wait("gpsimd", V, COMB_V[h - 1])
                        em("gpsimd", lambda e, oo=bf(E_EA2, T), i=bb(E_PB[0], T), hh=pacc.h:
                           e.tensor_copy(oo, i).then_inc(hh, 1))
                    else:
                        em("gpsimd", lambda e, oo=bf(E_EA2, T), a=bf(E_EA2, T),
                           i=bb(E_PB[0], T), hh=pacc.h:
                           e.tensor_tensor(oo, a, i, ALU.add).then_inc(hh, 1))
                    pb_rd[0] = (pacc, pacc.bump())
            def emit_pv(gg, kk):
                wait("tensor", expsem, gg + 1)
                if kk == 0:
                    wait("tensor", vfin, h)
                tiles = [(P[3][:, f * NW:(f + 1) * NW],
                          bb(E_VH[(h + 1) % 2] + 2 * kk * 128, 128),
                          bb(E_E[gg % 4] + 2 * f * NW, NW),
                          kk == 0, kk == KC - 1) for f in range(NF)]
                mm_group(None, tiles, pvsem)
            if kc >= 2:
                emit_pv(g - 2, kc - 2)
        emit_pv(h * KC + KC - 2, KC - 2)
        emit_pv(h * KC + KC - 1, KC - 1)
        PACC_AT[h] = pacc.n
        DACC_AT[h] = dacc.n
        wait("vector", pvsem, (h + 1) * KC)
        if h == 0:
            wait("vector", s_ysp[2], ysp_c)
        em("vector", lambda e, oo=bb(E_AO + 2 * h * T, T), i=P[3][:, 0:T]:
           e.tensor_copy(oo, i).then_inc(vfin.h, 1))
        vfin.bump()
        if h == NH - 1:
            emit_combine(h)
            emit_den(h, ((h + 1) * KC) % 3)
            emit_rcp(h)
    ATT_DONE = NH * KC

    wait("gpsimd", V, FIN_V[NH - 1])
    load_wo(1)

    F_EV = {}
    XLD = {}
    PADD_V = {}
    for cc in range(DC):
        sidx = cc // 2
        pr = sidx % 2
        if cc == 0:
            wait("sync", pvsem, ATT_DONE)
            wait("sync", V, FIN_V[NH - 1])
            wait("sync", s_ysp[2], ysp_c)
        if cc >= 2:
            wait("sync", V, F_EV[cc - 2])
        ybl = dma("sync", s_yb[cc % 2], bf(F_YB[cc % 2], T), yffn_dr[:, cc * T:(cc + 1) * T])
        if cc >= 2:
            wait("scalar", V, PADD_V[cc - 2])
        if cc >= 4:
            wait("scalar", V, F_EV[cc - 4])
        XLD[cc] = dma("scalar", s_xr[cc % 2], bf(F_XR[cc % 2] + 4 * ((cc // 2) % 2) * T, T),
                      xT_d[cc * 128:(cc + 1) * 128, :])
        wait("tensor", g_wo[pr], 16 * (sidx // 2 + 1))
        if cc == 0:
            wait("tensor", V, FIN_V[NH - 1])
        if cc >= 2:
            wait("tensor", V, F_EV[cc - 2])
        pp = P[2 + (cc % 2)]
        tiles = []
        for dc in range(DC):
            w = bb(F_WO[pr] + 2 * (dc * 2 * 128 + (cc - sidx * 2) * 128), 128)
            for f in range(NF):
                tiles.append((pp[:, f * NW:(f + 1) * NW], w,
                              bb(E_AO + 2 * (dc * T + f * NW), NW),
                              dc == 0, dc == DC - 1))
        t_wo = mm_group(None, tiles, TE)
        if (cc % 2 == 1 or cc == DC - 1) and sidx + 2 < NSL:
            wait("gpsimd", TE, t_wo)
            load_wo(sidx + 2)
        wait("vector", s_yb[cc % 2], ybl)
        wait("vector", s_xr[cc % 2], XLD[cc])
        em("vector", lambda e, oo=bf(F_YB[cc % 2], T), a=bf(F_YB[cc % 2], T),
           b=bf(F_XR[cc % 2] + 4 * ((cc // 2) % 2) * T, T):
           e.tensor_add(oo, a, b).then_inc(V.h, 1))
        PADD_V[cc] = V.bump()
        wait("vector", TE, t_wo)
        wait("vector", V, PADD_V[cc])
        if cc >= 4:
            wait("vector", s_out[cc % 4], 16 * (cc // 4))
        em("vector", lambda e, oo=bf(F_OS[cc % 2] + 4 * ((cc // 2) % 2) * T, T),
           i=pp[:, 0:T], b=bf(O_BO + 4 * cc, 1), y=bf(F_YB[cc % 2], T):
           e.scalar_tensor_tensor(oo, i, b, y, ALU.add, ALU.add).then_inc(V.h, 1))
        F_EV[cc] = V.bump()
        wait("sync", V, F_EV[cc])
        dma("sync", s_out[cc % 4], out_d[cc * 128:(cc + 1) * 128, :],
            bf(F_OS[cc % 2] + 4 * ((cc // 2) % 2) * T, T))

    for so in s_out:
        wait("sync", so, so.n)

    for s, v in max_waited.items():
        assert v <= s.n, f"wait beyond final count on sem {s.name}: waited {v}, final {s.n}"

    with nc.Block() as block:
        @block.sync
        def _(e):
            for f in ops["sync"]:
                f(e)

        @block.gpsimd
        def _(e):
            for f in ops["gpsimd"]:
                f(e)

        @block.tensor
        def _(e):
            for f in ops["tensor"]:
                f(e)

        @block.scalar
        def _(e):
            for f in ops["scalar"]:
                f(e)

        @block.vector
        def _(e):
            for f in ops["vector"]:
                f(e)

    return nc


def _host_inputs(cfg, inputs):
    S, D, T, C = cfg.S, cfg.D, cfg.T, cfg.C
    f32 = np.float32

    def t2(a):
        return np.ascontiguousarray(np.asarray(a, dtype=f32).T)

    def b2(b, nch):
        return np.ascontiguousarray(np.asarray(b, dtype=f32).reshape(nch, 128).T)

    x = np.ascontiguousarray(np.asarray(inputs["L0_attn_hidden_states"], dtype=f32).reshape(S, D))

    shared = dict(
        wqdT=t2(inputs["L0_attn_Wq_down"]), bqd2=b2(inputs["L0_attn_bq_down"], cfg.LC),
        wkvdT=t2(inputs["L0_attn_Wkv_down"]), bkv2=b2(inputs["L0_attn_bkv_down"], cfg.LC),
        wquT=t2(inputs["L0_attn_Wq_up"]), bqu2=b2(inputs["L0_attn_bq_up"], cfg.DC),
        wknT=t2(inputs["L0_attn_Wk_nope"]), bkn2=b2(inputs["L0_attn_bk_nope"], cfg.NC2),
        wkrT=t2(inputs["L0_attn_Wk_rope"]), bkr2=b2(inputs["L0_attn_bk_rope"], 2 * cfg.JH),
        wvuT=t2(inputs["L0_attn_Wv_up"]), bvu2=b2(inputs["L0_attn_bv_up"], cfg.DC),
        woT=t2(inputs["L0_attn_Wo"]), bo2=b2(inputs["L0_attn_bo"], cfg.DC),
        routerT=t2(inputs["L0_ffn_router_weight"]),
        wgT=t2(inputs["L0_ffn_sh_gate"]), wuT=t2(inputs["L0_ffn_sh_up"]),
        wdT=t2(inputs["L0_ffn_sh_down"]),
        rgT=t2(inputs["L0_ffn_r_gate"]), ruT=t2(inputs["L0_ffn_r_up"]),
        rdT=t2(inputs["L0_ffn_r_down"]),
    )

    ROT = D // 2
    inv = (1.0 / (np.float32(cfg.base) ** (np.arange(0, ROT, 2, dtype=f32) / f32(ROT)))).astype(f32)
    tpos = np.arange(S, dtype=f32)
    freqs = np.outer(tpos, inv).astype(f32)
    cosT = np.ascontiguousarray(np.cos(freqs).astype(f32).T)
    sinT = np.ascontiguousarray(np.sin(freqs).astype(f32).T)

    in_maps = []
    for c in range(C):
        m = dict(shared)
        m["x"] = np.ascontiguousarray(x[c * T:(c + 1) * T])
        m["xT"] = np.ascontiguousarray(x[c * T:(c + 1) * T].T)
        m["cosT"] = np.ascontiguousarray(cosT[:, c * T:(c + 1) * T])
        m["sinT"] = np.ascontiguousarray(sinT[:, c * T:(c + 1) * T])
        in_maps.append(m)
    return in_maps


def kernel(**inputs):
    cfg = CFG
    nc = build(cfg)
    in_maps = _host_inputs(cfg, inputs)
    res = run_bass_kernel_spmd(nc, in_maps, core_ids=list(range(cfg.C)))
    out = np.concatenate(
        [np.ascontiguousarray(np.asarray(r["out"], dtype=np.float32).T) for r in res.results],
        axis=0)
    return out.reshape(1, cfg.S, cfg.D)


# version 30
# speedup vs baseline: 1.0191x; 1.0024x over previous
import numpy as np
import ml_dtypes

import concourse.bass as bass
import concourse.mybir as mybir
from concourse.bass_utils import run_bass_kernel_spmd

F32 = mybir.dt.float32
BF16 = mybir.dt.bfloat16
U8 = mybir.dt.uint8
AF = mybir.ActivationFunctionType
ALU = mybir.AluOpType
AXX = mybir.AxisListType.X
BF = ml_dtypes.bfloat16


class Cfg:
    def __init__(self, S=8192, D=2048, L=512, IS=8192, IR=1024, NE=8, C=8,
                 eps=1e-5, base=10000.0):
        self.S, self.D, self.L, self.IS, self.IR, self.NE, self.C = S, D, L, IS, IR, NE, C
        self.eps, self.base = eps, base
        self.T = S // C
        self.QT = self.T // 128
        self.DC = D // 128
        self.LC = L // 128
        self.NH = D // 128
        self.NC2 = self.DC // 2
        self.JH = (D // 4) // 128
        self.KC = S // 128
        self.ISC = IS // 128
        self.IRC = IR // 128
        self.IG = 4
        self.NF = max(1, self.T // 512)
        self.NW = min(512, self.T)


CFG = Cfg()


def _groups(nchunks, gsz):
    out, i = [], 0
    while i < nchunks:
        out.append((i, min(gsz, nchunks - i)))
        i += gsz
    return out


def build(cfg):
    S, D, L, T = cfg.S, cfg.D, cfg.L, cfg.T
    DC, LC, QT, NH, KC = cfg.DC, cfg.LC, cfg.QT, cfg.NH, cfg.KC
    JH, NC2 = cfg.JH, cfg.NC2
    NE, NF, NW = cfg.NE, cfg.NF, cfg.NW
    C = cfg.C
    scale = float(np.float32(1.0 / np.sqrt(np.float32(128.0))))

    nc = bass.Bass("TRN2", target_bir_lowering=False)

    x_d = nc.dram_tensor("x", [T, D], F32, kind="ExternalInput")
    cos_d = nc.dram_tensor("cosT", [JH * 128, T], F32, kind="ExternalInput")
    sin_d = nc.dram_tensor("sinT", [JH * 128, T], F32, kind="ExternalInput")
    wqd_d = nc.dram_tensor("wqdT", [D, L], F32, kind="ExternalInput")
    wkvd_d = nc.dram_tensor("wkvdT", [D, L], F32, kind="ExternalInput")
    wqu_d = nc.dram_tensor("wquT", [L, D], F32, kind="ExternalInput")
    wkn_d = nc.dram_tensor("wknT", [L, D // 2], F32, kind="ExternalInput")
    wkr_d = nc.dram_tensor("wkrT", [L, D // 2], F32, kind="ExternalInput")
    wvu_d = nc.dram_tensor("wvuT", [L, D], F32, kind="ExternalInput")
    wo_d = nc.dram_tensor("woT", [D, D], F32, kind="ExternalInput")
    rt_d = nc.dram_tensor("routerT", [D, NE], F32, kind="ExternalInput")
    wg_d = nc.dram_tensor("wgT", [D, cfg.IS], F32, kind="ExternalInput")
    wu_d = nc.dram_tensor("wuT", [D, cfg.IS], F32, kind="ExternalInput")
    wd_d = nc.dram_tensor("wdT", [cfg.IS, D], F32, kind="ExternalInput")
    rg_d = nc.dram_tensor("rgT", [D, cfg.IR], F32, kind="ExternalInput")
    ru_d = nc.dram_tensor("ruT", [D, cfg.IR], F32, kind="ExternalInput")
    rd_d = nc.dram_tensor("rdT", [cfg.IR, D], F32, kind="ExternalInput")
    bqd_d = nc.dram_tensor("bqd2", [128, LC], F32, kind="ExternalInput")
    bkv_d = nc.dram_tensor("bkv2", [128, LC], F32, kind="ExternalInput")
    bqu_d = nc.dram_tensor("bqu2", [128, DC], F32, kind="ExternalInput")
    bkn_d = nc.dram_tensor("bkn2", [128, NC2], F32, kind="ExternalInput")
    bkr_d = nc.dram_tensor("bkr2", [128, 2 * JH], F32, kind="ExternalInput")
    bvu_d = nc.dram_tensor("bvu2", [128, DC], F32, kind="ExternalInput")
    bo_d = nc.dram_tensor("bo2", [128, DC], F32, kind="ExternalInput")
    xT_d = nc.dram_tensor("xT", [D, T], F32, kind="ExternalInput")
    out_d = nc.dram_tensor("out", [D, T], F32, kind="ExternalOutput")

    id_bf_d = nc.inline_tensor(np.eye(128, dtype=BF), "id_bf")
    id_f32_d = nc.inline_tensor(np.eye(128, dtype=np.float32), "id_f32")

    NQ = 4
    kT_locQ = [nc.dram_tensor(f"kT_loc{q}", [D // NQ, T], BF16) for q in range(NQ)]
    v_locQ = [nc.dram_tensor(f"v_loc{q}", [T, D // NQ], BF16) for q in range(NQ)]
    kT_gatQ = [nc.dram_tensor(f"kT_gat{q}", [C * (D // NQ), T], BF16) for q in range(NQ)]
    v_gatQ = [nc.dram_tensor(f"v_gat{q}", [C * T, D // NQ], BF16) for q in range(NQ)]
    kn_loc = nc.dram_tensor("kn_loc", [1, NH], F32)
    kn_gat = nc.dram_tensor("kn_gat", [1, NH], F32)
    comb_dr = nc.dram_tensor("comb_dr", [QT, 128], F32)
    cneg_dr = nc.dram_tensor("cneg_dr", [1, NH], F32)
    yffn_dr = nc.dram_tensor("yffn_dr", [128, DC * T], F32)

    o = 0

    def take(nbytes, align=256):
        nonlocal o
        o = (o + align - 1) // align * align
        r = o
        o += nbytes
        return r

    O_IDBF = take(256)
    O_IDF32 = take(512)
    O_ONES = take(256)
    O_BQD = take(4 * LC); O_BKV = take(4 * LC); O_BQU = take(4 * DC)
    O_BKN = take(4 * NC2); O_BKR = take(4 * 2 * JH); O_BVU = take(4 * DC)
    O_BO = take(4 * DC)
    O_COMB = take(2 * T)
    O_CNEG = take(4 * NH)
    O_KN2 = take(4 * NH)
    O_QN2 = take(4 * NH)
    O_KNG = take(4 * NH)
    O_CROW = take(4 * NH)
    O_CROW2 = take(4 * NH)
    O_LGS = take(4 * NE * QT)
    O_COMBT = take(4 * 128)
    O_MSK = take(4 * NE * QT)
    O_M1 = take(4 * QT)
    O_M2 = take(4 * QT)
    O_RTR = take(2 * DC * NE)
    O_EPS = take(64)
    O_SMALL = take(64 * 16)
    FIXED_END = take(0, 512)

    def small(i, n=1):
        return (O_SMALL + i * 64, n)

    O_XNT = FIXED_END
    O_QTT = O_XNT + 2 * DC * T
    BIG = O_QTT + 2 * DC * T

    D4 = D * 4
    A_X = [BIG, BIG + D4, BIG + 2 * D4]
    A_XSQ = BIG + 3 * D4
    A_XNB = [A_XSQ + D4, A_XSQ + D4 + 2 * D]
    A_END = A_XSQ + D4 + 4 * D
    B_COS = BIG
    B_SIN = B_COS + 4 * JH * T
    B_KVL = B_SIN + 4 * JH * T
    B_W = B_KVL + 2 * LC * T
    B_WSZ = max(2 * DC * L, 2 * LC * D)
    B_KT = B_W + B_WSZ
    B_KR = B_KT + 2 * DC * T
    B_VSB = B_KR
    B_RT1 = B_KR + 4 * 2 * JH * T
    B_RT2 = B_RT1 + 4 * T
    B_SQ = B_RT2 + 4 * T
    B_VT = [B_SQ + 2 * T, B_SQ + 4 * T]
    B_QL = B_KVL
    B_END = B_VT[1] + 2 * T
    C_WG = BIG
    C_WU = C_WG + 2 * DC * cfg.IG * 128
    C_WD = C_WU + 2 * DC * cfg.IG * 128
    C_Y = C_WD + 2 * cfg.IG * D
    C_H = [C_Y + 4 * DC * T, C_Y + 4 * DC * T + 2 * cfg.IG * T]
    C_SG = C_H[1] + 2 * cfg.IG * T
    C_T2 = C_SG + 2 * T
    C_END = C_T2 + 2 * T
    E_KH = [BIG, BIG + 2 * S]
    E_VH = [BIG + 4 * S, BIG + 6 * S]
    E_E = [BIG + 8 * S + i * 2 * T for i in range(4)]
    E_PB = [BIG + 8 * S + 8 * T, BIG + 8 * S + 10 * T]
    E_EA = BIG + 8 * S + 12 * T
    E_EA2 = E_EA + 4 * T
    E_AO = E_EA2 + 4 * T
    E_END = E_AO + 2 * DC * T
    F_YB = [BIG, BIG + 4 * T]
    F_XR = [BIG + 8 * T, BIG + 16 * T]
    F_OS = [BIG + 24 * T, BIG + 32 * T]
    F_WO = [E_END, E_END + 2 * DC * 256]
    F_END = F_WO[1] + 2 * DC * 256

    ARENA = max(A_END, B_END, C_END, E_END, F_END)
    ARENA = (ARENA + 511) // 512 * 512
    O_QL = B_RT1
    O_ONESF = B_VT[1]
    O_RCPA = B_RT2
    O_RCPB = C_SG
    O_NSQ = B_VSB
    arena = nc.alloc_sbuf_tensor("arena", [128, ARENA], U8)

    def bb(off, n, parts=128, p0=0):
        return arena[p0:p0 + parts, off:off + 2 * n].bitcast(BF16)

    def bf(off, n, parts=128, p0=0):
        return arena[p0:p0 + parts, off:off + 4 * n].bitcast(F32)

    PT = max(512, T)
    P = [nc.alloc_psum_tensor(f"P{i}", [128, PT], F32) for i in range(4)]

    class Sem:
        def __init__(self, name):
            self.name = name
            self.h = nc.semaphore(name).__enter__()
            self.n = 0

        def bump(self, k=1):
            self.n += k
            return self.n

    s_xa = [Sem("s_xa0"), Sem("s_xa1"), Sem("s_xa2")]
    g_cst = Sem("g_cst")
    g_cs = Sem("g_cs")
    g_wB = Sem("g_wB")
    g_slab = [Sem("g_sg"), Sem("g_su"), Sem("g_sd")]
    g_wo = [Sem("g_wo0"), Sem("g_wo1")]
    g_kn = Sem("g_kn")
    g_cb = Sem("g_cb")
    g_ms = Sem("g_ms")
    g_cc = Sem("g_cc")
    s_kst = Sem("s_kst")
    s_kn2 = Sem("s_kn2")
    s_vst = Sem("s_vst")
    s_cd1 = Sem("s_cd1")
    s_cd2 = Sem("s_cd2")
    s_kv = [Sem("s_kv0"), Sem("s_kv1")]
    s_ysp = [Sem("s_yspa"), Sem("s_yspb"), Sem("s_yspc")]
    s_yb = [Sem("s_yb0"), Sem("s_yb1")]
    s_xr = [Sem("s_xr0"), Sem("s_xr1")]
    s_out = [Sem("s_out0"), Sem("s_out1"), Sem("s_out2"), Sem("s_out3")]
    scsem = Sem("scsem")
    expsem = Sem("expsem")
    pvsem = Sem("pvsem")
    dacc = Sem("dacc")
    pacc = Sem("pacc")
    vfin = Sem("vfin")
    TE = Sem("TE")
    V = Sem("V")
    SC = Sem("SC")

    ops = {k: [] for k in ("sync", "gpsimd", "tensor", "scalar", "vector")}
    max_waited = {}

    def em(eng, fn):
        ops[eng].append(fn)

    def wait(eng, sem, v):
        if v > 0:
            max_waited[sem] = max(max_waited.get(sem, 0), v)
            em(eng, lambda e, h=sem.h, vv=v: e.wait_ge(h, vv))

    def dma(eng, sem, out, in_):
        tgt = sem.bump(16)
        em(eng, lambda e, oo=out, ii=in_, h=sem.h: e.dma_start(out=oo, in_=ii).then_inc(h, 16))
        return tgt

    def mm(pp, w, r, st, sp, inc=None, inc_amt=1):
        if inc is None:
            em("tensor", lambda e, o=pp, ww=w, rr=r, s1=st, s2=sp:
               e.matmul(o, ww, rr, start=s1, stop=s2, skip_group_check=True))
        else:
            em("tensor", lambda e, o=pp, ww=w, rr=r, s1=st, s2=sp, h=inc.h, a=inc_amt:
               e.matmul(o, ww, rr, start=s1, stop=s2, skip_group_check=True).then_inc(h, a))

    ident_bf = bb(O_IDBF, 128)
    ident_f32 = bf(O_IDF32, 128)
    ones_bf = bb(O_ONES, 128)
    ones_f32 = bf(O_ONESF, 128)

    dma("gpsimd", g_cst, bb(O_IDBF, 128), id_bf_d[:, :])
    dma("gpsimd", g_cst, bf(O_IDF32, 128), id_f32_d[:, :])
    for od, dd in ((O_BQD, bqd_d), (O_BKV, bkv_d), (O_BQU, bqu_d),
                   (O_BKN, bkn_d), (O_BKR, bkr_d), (O_BVU, bvu_d), (O_BO, bo_d)):
        dma("gpsimd", g_cst, bf(od, dd.shape[1]), dd[:, :])
    dma("gpsimd", g_cst, bb(O_RTR, DC * NE).rearrange("p (dc e) -> p dc e", dc=DC),
        rt_d.ap().rearrange("(dc p) e -> p dc e", p=128))
    CST_ALL = g_cst.n
    em("gpsimd", lambda e: e.memset(bb(O_ONES, 128), 1.0).then_inc(g_ms.h, 1))
    ONES_OK = g_ms.bump()
    em("gpsimd", lambda e, v=float(cfg.eps): e.memset(bf(O_EPS, 1), v).then_inc(g_ms.h, 1))
    EPS_OK = g_ms.bump()

    xnT = bb(O_XNT, DC * T)
    A_VDONE = {}
    A_TCOPY = {}
    A_TTR = {}
    A_SCEND = {}
    for qt in range(QT):
        pr = qt % 3
        p2 = qt % 2
        sb = 8 * p2
        if qt >= 3:
            wait("sync", V, A_VDONE[qt - 3])
        xld = dma("sync", s_xa[pr], bf(A_X[pr], D), x_d[qt * 128:(qt + 1) * 128, :])
        xap = bf(A_X[pr], D)

        wait("scalar", s_xa[pr], xld)
        if qt >= 2:
            wait("scalar", SC, A_SCEND[qt - 2])
            wait("scalar", V, A_VDONE[qt - 2])
        em("scalar", lambda e, x=xap, oo=bf(A_XSQ, D), a=bf(*small(sb + 0)):
           e.activation(oo, x, AF.Copy, accum_out=a).then_inc(SC.h, 1))
        sc_sum = SC.bump()
        wait("scalar", SC, sc_sum)
        em("scalar", lambda e, x=xap, oo=bf(A_XSQ, D), a=bf(*small(sb + 2)):
           e.activation(oo, x, AF.Square, accum_out=a).then_inc(SC.h, 1))
        sc_sq = SC.bump()
        wait("vector", SC, sc_sum)
        if qt >= 2:
            wait("vector", V, A_VDONE[qt - 2])
        em("vector", lambda e, i=bf(*small(sb + 0)), oo=bf(*small(sb + 1)):
           e.tensor_scalar_mul(oo, i, 1.0 / D).then_inc(V.h, 1))
        v_mean = V.bump()
        wait("vector", V, v_mean)
        em("vector", lambda e, m=bf(*small(sb + 1)), oo=bf(*small(sb + 3)):
           e.tensor_mul(oo, m, m).then_inc(V.h, 1))
        v_m2 = V.bump()
        wait("vector", SC, sc_sq)
        wait("vector", V, v_m2)
        em("vector", lambda e, sq=bf(*small(sb + 2)), m2=bf(*small(sb + 3)), oo=bf(*small(sb + 4)):
           e.scalar_tensor_tensor(oo, sq, 1.0 / D, m2, ALU.mult, ALU.subtract).then_inc(V.h, 1))
        v_var = V.bump()
        wait("scalar", V, v_var)
        if qt == 0:
            wait("scalar", g_ms, EPS_OK)
        em("scalar", lambda e, i=bf(*small(sb + 4)), oo=bf(*small(sb + 5)), b=bf(O_EPS, 1):
           e.activation(oo, i, AF.Ln, bias=b).then_inc(SC.h, 1))
        sc_ln = SC.bump()
        wait("scalar", SC, sc_ln)
        em("scalar", lambda e, i=bf(*small(sb + 5)), oo=bf(*small(sb + 6)):
           e.activation(oo, i, AF.Exp, scale=-0.5).then_inc(SC.h, 1))
        sc_rs = SC.bump()
        wait("vector", SC, sc_rs)
        if qt >= 2:
            wait("vector", TE, A_TTR[qt - 2])
        em("vector", lambda e, x=xap, m=bf(*small(sb + 1)), r=bf(*small(sb + 6)), oo=bb(A_XNB[p2], D):
           e.tensor_scalar(oo, x, m, r, ALU.subtract, ALU.mult).then_inc(V.h, 1))
        v_xn = V.bump()
        A_VDONE[qt] = v_xn
        A_SCEND[qt] = SC.n

        ppr = P[(qt % 2) + (2 if qt >= 4 else 0)]
        pview = ppr[:, 0:(DC * 128) // 2].bitcast(BF16)
        if qt == 0:
            wait("tensor", g_cst, CST_ALL)
        wait("tensor", V, v_xn)
        if qt not in (0, 1, 4, 5):
            wait("tensor", V, A_TCOPY[qt - 2])
        for dc in range(DC):
            if dc == DC - 1:
                em("tensor", lambda e, oo=pview[:, dc * 128:(dc + 1) * 128],
                   i=bb(A_XNB[p2] + dc * 256, 128), idt=ident_bf:
                   e.transpose(oo, i, idt).then_inc(TE.h, 1))
            else:
                em("tensor", lambda e, oo=pview[:, dc * 128:(dc + 1) * 128],
                   i=bb(A_XNB[p2] + dc * 256, 128), idt=ident_bf:
                   e.transpose(oo, i, idt))
        A_TTR[qt] = TE.bump()
        wait("vector", TE, A_TTR[qt])
        dst = xnT.rearrange("p (dc t) -> p dc t", dc=DC)[:, :, qt * 128:(qt + 1) * 128]
        src = pview.rearrange("p (dc i) -> p dc i", dc=DC)
        em("vector", lambda e, oo=dst, i=src: e.tensor_copy(oo, i).then_inc(V.h, 1))
        A_TCOPY[qt] = V.bump()

    A_DONE_V = V.n
    A_DONE_SC = SC.n

    def mm_group(pp, tiles, inc_sem):
        n = len(tiles)
        for i, (po, w, r, st, sp) in enumerate(tiles):
            if i == n - 1:
                em("tensor", lambda e, o=po, ww=w, rr=r, s1=st, s2=sp, h=inc_sem.h:
                   e.matmul(o, ww, rr, start=s1, stop=s2, skip_group_check=True).then_inc(h, 1))
            else:
                em("tensor", lambda e, o=po, ww=w, rr=r, s1=st, s2=sp:
                   e.matmul(o, ww, rr, start=s1, stop=s2, skip_group_check=True))
        return inc_sem.bump()

    def mm_proj2(rhs_off, rhs_nchunks, w_off, nout, dst_writer,
                 pre_tensor_wait=None, psel=None, first_wait=None, f_waits=None):
        first_v = V.n if first_wait is None else first_wait
        psel = psel or [P[0], P[1]]
        evac_v = {}
        for oc in range(nout):
            pp = psel[oc % len(psel)]
            if oc == 0 and pre_tensor_wait is not None:
                pre_tensor_wait()
            if oc < len(psel):
                wait("tensor", V, first_v)
            else:
                wait("tensor", V, evac_v[oc - len(psel)])
            if f_waits is None:
                tiles = []
                for c in range(rhs_nchunks):
                    w = bb(w_off + 2 * (c * nout + oc) * 128, 128)
                    for f in range(NF):
                        tiles.append((pp[:, f * NW:(f + 1) * NW], w,
                                      bb(rhs_off + 2 * (c * T + f * NW), NW),
                                      c == 0, c == rhs_nchunks - 1))
                t_mm = mm_group(pp, tiles, TE)
            else:
                for f in range(NF):
                    if oc == 0 and f in f_waits:
                        wait("tensor", f_waits[f][0], f_waits[f][1])
                    tiles = [(pp[:, f * NW:(f + 1) * NW],
                              bb(w_off + 2 * (c * nout + oc) * 128, 128),
                              bb(rhs_off + 2 * (c * T + f * NW), NW),
                              c == 0, c == rhs_nchunks - 1)
                             for c in range(rhs_nchunks)]
                    t_mm = mm_group(pp, tiles, TE)
            wait("vector", TE, t_mm)
            evac_v[oc] = dst_writer(oc, pp)
        return evac_v

    def evac_bias(dst_off, bias_off, f32out=False, first_wait=None):
        def w(oc, pp):
            if oc == 0 and first_wait is not None:
                wait("vector", first_wait[0], first_wait[1])
            bias = bf(bias_off + 4 * oc, 1)
            if f32out:
                oap = bf(dst_off + 4 * oc * T, T)
            else:
                oap = bb(dst_off + 2 * oc * T, T)
            em("vector", lambda e, oo=oap, i=pp[:, 0:T], b=bias:
               e.tensor_scalar(oo, i, b, None, ALU.add).then_inc(V.h, 1))
            return V.bump()
        return w

    wait("sync", V, A_DONE_V)
    wait("sync", SC, A_DONE_SC)
    dma("sync", g_cs, bf(B_COS, JH * T).rearrange("p (j t) -> p j t", j=JH),
        cos_d.ap().rearrange("(j p) t -> p j t", p=128))
    dma("sync", g_cs, bf(B_SIN, JH * T).rearrange("p (j t) -> p j t", j=JH),
        sin_d.ap().rearrange("(j p) t -> p j t", p=128))
    CS_ALL = g_cs.n

    def loadB(src_ap, te_wait=None):
        wait("gpsimd", TE, TE.n if te_wait is None else te_wait)
        wait("gpsimd", g_wB, g_wB.n)
        nch = src_ap.shape[0] // 128
        return dma("gpsimd", g_wB,
                   bb(B_W, nch * src_ap.shape[1]).rearrange("p (c n) -> p c n", c=nch),
                   src_ap.rearrange("(c p) n -> p c n", p=128))

    if B_W >= A_END:
        ld = loadB(wkvd_d.ap(), te_wait=0)
    else:
        wait("gpsimd", V, A_DONE_V)
        wait("gpsimd", SC, A_DONE_SC)
        ld = loadB(wkvd_d.ap())

    def pre_kv():
        wait("tensor", g_wB, ld)
    mm_proj2(O_XNT, DC, B_W, LC, evac_bias(B_KVL, O_BKV), pre_tensor_wait=pre_kv,
             first_wait=A_TCOPY[QT // NF - 1],
             f_waits={f: (V, A_TCOPY[(f + 1) * (QT // NF) - 1]) for f in range(NF)})

    ldn = loadB(wkn_d.ap())
    mm_proj2(B_KVL, LC, B_W, NC2, evac_bias(B_KT, O_BKN),
             pre_tensor_wait=lambda: wait("tensor", g_wB, ldn))

    ldr = loadB(wkr_d.ap())
    KR_EV = mm_proj2(B_KVL, LC, B_W, 2 * JH, evac_bias(B_KR, O_BKR, f32out=True),
                     pre_tensor_wait=lambda: wait("tensor", g_wB, ldr))

    wait("vector", g_cs, CS_ALL)
    wait("vector", V, KR_EV[2 * JH - 1])
    for jc in range(JH):
        x1 = bf(B_KR + 4 * jc * T, T)
        x2 = bf(B_KR + 4 * (JH + jc) * T, T)
        co = bf(B_COS + 4 * jc * T, T)
        si = bf(B_SIN + 4 * jc * T, T)
        em("vector", lambda e, oo=bf(B_RT1, T), a=x1, b=co: e.tensor_mul(oo, a, b).then_inc(V.h, 1))
        V.bump()
        em("vector", lambda e, oo=bf(B_RT2, T), a=x2, b=si: e.tensor_mul(oo, a, b).then_inc(V.h, 1))
        v2 = V.bump()
        wait("vector", V, v2)
        em("vector", lambda e, oo=bb(B_KT + 2 * (NC2 + jc) * T, T), a=bf(B_RT1, T), b=bf(B_RT2, T):
           e.tensor_sub(oo, a, b).then_inc(V.h, 1))
        v3 = V.bump()
        wait("vector", V, v3)
        em("vector", lambda e, oo=bf(B_RT1, T), a=x1, b=si: e.tensor_mul(oo, a, b).then_inc(V.h, 1))
        V.bump()
        em("vector", lambda e, oo=bf(B_RT2, T), a=x2, b=co: e.tensor_mul(oo, a, b).then_inc(V.h, 1))
        v5 = V.bump()
        wait("vector", V, v5)
        em("vector", lambda e, oo=bb(B_KT + 2 * (NC2 + JH + jc) * T, T), a=bf(B_RT1, T), b=bf(B_RT2, T):
           e.tensor_add(oo, a, b).then_inc(V.h, 1))
        v6 = V.bump()
        if jc < JH - 1:
            wait("vector", V, v6)
    KT_DONE_V = V.n

    wait("sync", V, KT_DONE_V)
    for q in range(NQ):
        kst = dma("sync", s_kst, kT_locQ[q].ap().rearrange("(dc p) t -> p dc t", p=128),
                  bb(B_KT + q * (DC // NQ) * 2 * T, (DC // NQ) * T)
                  .rearrange("p (dc t) -> p dc t", dc=DC // NQ))
    KST_ALL = kst

    ldq = loadB(wqd_d.ap())
    QD_EV = mm_proj2(O_XNT, DC, B_W, LC,
                     evac_bias(O_QL, O_BQD, first_wait=(V, KT_DONE_V)),
                     pre_tensor_wait=lambda: wait("tensor", g_wB, ldq),
                     first_wait=KR_EV[2 * JH - 1])
    QD_TE = TE.n

    wait("gpsimd", V, KT_DONE_V)
    wait("gpsimd", g_wB, g_wB.n)
    ldv = dma("gpsimd", g_wB, bb(C_WG, LC * D).rearrange("p (c n) -> p c n", c=LC),
              wvu_d.ap().rearrange("(c p) n -> p c n", p=128))
    wait("gpsimd", g_wB, g_wB.n)
    ldqu = dma("gpsimd", g_wB, bb(C_WU, LC * D).rearrange("p (c n) -> p c n", c=LC),
               wqu_d.ap().rearrange("(c p) n -> p c n", p=128))

    KSQ_SC = {}
    KN_MM = {}
    KN_V = {}
    for h in range(NH):
        half = h // 8
        if h % 8 == 0:
            wait("scalar", V, KT_DONE_V)
            if half == 1:
                wait("scalar", TE, KN_MM[7])
            em("scalar", lambda e, oo=bb(O_NSQ, 8 * T), i=bb(B_KT + 2 * half * 8 * T, 8 * T):
               e.activation(oo, i, AF.Square).then_inc(SC.h, 1))
            KSQ_SC[half] = SC.bump()
        if h == 0:
            wait("tensor", g_ms, ONES_OK)
        wait("tensor", SC, KSQ_SC[half])
        if h >= 2:
            wait("tensor", V, KN_V[h - 2])
        pp = P[2 + (h % 2)]
        tiles = [(pp[:, f * NW:(f + 1) * NW], ones_bf,
                  bb(O_NSQ + 2 * ((h % 8) * T + f * NW), NW), True, True)
                 for f in range(NF)]
        KN_MM[h] = mm_group(pp, tiles, TE)
        wait("vector", TE, KN_MM[h])
        em("vector", lambda e, oo=bf(O_KN2 + 4 * h, 1), i=pp[:, 0:T]:
           e.reduce_max(oo, i, axis=AXX).then_inc(V.h, 1))
        KN_V[h] = V.bump()
    KN_DONE_V = V.n

    wait("sync", V, KN_DONE_V)
    knst = dma("sync", s_kn2, kn_loc[0:1, :], bf(O_KN2, NH, parts=1))

    vsb = bb(B_VSB, QT * D)
    VT_COPY = {}
    VT_TR = {}

    def evac_v_writer(dc, pp):
        pr = dc % 2
        bias = bf(O_BVU + 4 * dc, 1)
        if dc >= 2:
            wait("vector", TE, VT_TR[dc - 2])
        em("vector", lambda e, oo=bb(B_VT[pr], T), i=pp[:, 0:T], b=bias:
           e.tensor_scalar(oo, i, b, None, ALU.add).then_inc(V.h, 1))
        v_e = V.bump()
        wait("tensor", V, v_e)
        ptr = P[2 + pr]
        pview = ptr[:, 0:(QT * 128) // 2].bitcast(BF16)
        if dc < 2:
            wait("tensor", V, KN_V[NH - 2 + dc])
        else:
            wait("tensor", V, VT_COPY[dc - 2])
        for qt in range(QT):
            if qt == QT - 1:
                em("tensor", lambda e, oo=pview[:, qt * 128:(qt + 1) * 128],
                   i=bb(B_VT[pr] + 2 * qt * 128, 128), idt=ident_bf:
                   e.transpose(oo, i, idt).then_inc(TE.h, 1))
            else:
                em("tensor", lambda e, oo=pview[:, qt * 128:(qt + 1) * 128],
                   i=bb(B_VT[pr] + 2 * qt * 128, 128), idt=ident_bf:
                   e.transpose(oo, i, idt))
        VT_TR[dc] = TE.bump()
        wait("vector", TE, VT_TR[dc])
        dst = vsb.rearrange("p (qt d) -> p qt d", qt=QT)[:, :, dc * 128:(dc + 1) * 128]
        src = pview.rearrange("p (qt i) -> p qt i", qt=QT)
        em("vector", lambda e, oo=dst, i=src: e.tensor_copy(oo, i).then_inc(V.h, 1))
        VT_COPY[dc] = V.bump()
        return VT_COPY[dc]

    def pre_v():
        wait("tensor", g_wB, ldv)
        wait("tensor", V, KT_DONE_V)
    mm_proj2(B_KVL, LC, C_WG, DC, evac_v_writer, pre_tensor_wait=pre_v,
             first_wait=QD_EV[LC - 1])
    V_TE = TE.n

    wait("sync", V, VT_COPY[DC - 1])
    for q in range(NQ):
        vst = dma("sync", s_vst, v_locQ[q].ap().rearrange("(qt p) d -> p qt d", p=128),
                  vsb.rearrange("p (qt d) -> p qt d", qt=QT)[:, :, q * (D // NQ):(q + 1) * (D // NQ)])
    VST_ALL = vst

    QT_EV = mm_proj2(O_QL, LC, C_WU, DC, evac_bias(O_QTT, O_BQU),
                     pre_tensor_wait=lambda: wait("tensor", g_wB, ldqu))
    QU_TE = TE.n

    sgroups = [(0, 1), (1, 1), (2, 2)] + [(g0 + 4, gs) for g0, gs in _groups(cfg.ISC - 4, cfg.IG)]
    rgroups = _groups(cfg.IRC, cfg.IG)
    allgroups = [("s", g0, gs) for g0, gs in sgroups] + [("r", g0, gs) for g0, gs in rgroups]
    g0_gs = allgroups[0][2]
    W0_G, W0_U, W0_D = B_W, B_KT, C_Y + 4 * 15 * T
    wait("gpsimd", TE, QD_TE)
    wait("gpsimd", g_slab[0], g_slab[0].n)
    lgld0 = dma("gpsimd", g_slab[0],
                bb(W0_G, DC * g0_gs * 128).rearrange("p (dc i) -> p dc i", dc=DC),
                wg_d.ap()[:, 0:g0_gs * 128].rearrange("(dc p) i -> p dc i", p=128))
    wait("gpsimd", SC, KSQ_SC[1])
    wait("gpsimd", s_kst, KST_ALL)
    wait("gpsimd", g_slab[1], g_slab[1].n)
    luld0 = dma("gpsimd", g_slab[1],
                bb(W0_U, DC * g0_gs * 128).rearrange("p (dc i) -> p dc i", dc=DC),
                wu_d.ap()[:, 0:g0_gs * 128].rearrange("(dc p) i -> p dc i", p=128))
    wait("gpsimd", s_vst, VST_ALL)
    wait("gpsimd", g_slab[2], g_slab[2].n)
    ldld0 = dma("gpsimd", g_slab[2],
                bb(W0_D, g0_gs * D).rearrange("p (il c) -> p il c", il=g0_gs),
                wd_d.ap()[0:g0_gs * 128, :].rearrange("(il p) c -> p il c", p=128))

    CC_K = {}
    CC_V = {}

    def emit_cc_k(q):
        wait("gpsimd", s_kst, KST_ALL)
        em("gpsimd", lambda e: e.collective_compute(
            "AllGather", ALU.bypass, replica_groups=[list(range(C))],
            ins=[kT_locQ[q].ap().opt()], outs=[kT_gatQ[q].ap().opt()]).then_inc(g_cc.h, 1))
        CC_K[q] = g_cc.bump()

    def emit_cc_v(q):
        wait("gpsimd", s_vst, VST_ALL)
        em("gpsimd", lambda e: e.collective_compute(
            "AllGather", ALU.bypass, replica_groups=[list(range(C))],
            ins=[v_locQ[q].ap().opt()], outs=[v_gatQ[q].ap().opt()]).then_inc(g_cc.h, 1))
        CC_V[q] = g_cc.bump()

    emit_cc_k(0)
    wait("gpsimd", s_kn2, knst)
    em("gpsimd", lambda e: e.collective_compute(
        "AllReduce", ALU.max, replica_groups=[list(range(C))],
        ins=[kn_loc.ap().opt()], outs=[kn_gat.ap().opt()]).then_inc(g_cc.h, 1))
    CC_KN = g_cc.bump()

    QSQ_SC = {}
    QN_MM = {}
    QN_V = {}
    for h in range(NH):
        half = h // 8
        if h % 8 == 0:
            wait("scalar", V, QT_EV[DC - 1])
            wait("scalar", s_vst, VST_ALL)
            wait("scalar", TE, KN_MM[15] if half == 0 else QN_MM[7])
            em("scalar", lambda e, oo=bb(O_NSQ, 8 * T), i=bb(O_QTT + 2 * half * 8 * T, 8 * T):
               e.activation(oo, i, AF.Square).then_inc(SC.h, 1))
            QSQ_SC[half] = SC.bump()
        wait("tensor", SC, QSQ_SC[half])
        if h < 2:
            wait("tensor", V, VT_COPY[DC - 2 + h])
        else:
            wait("tensor", V, QN_V[h - 2])
        pp = P[2 + (h % 2)]
        tiles = [(pp[:, f * NW:(f + 1) * NW], ones_bf,
                  bb(O_NSQ + 2 * ((h % 8) * T + f * NW), NW), True, True)
                 for f in range(NF)]
        QN_MM[h] = mm_group(pp, tiles, TE)
        wait("vector", TE, QN_MM[h])
        em("vector", lambda e, oo=bf(O_QN2 + 4 * h, 1), i=pp[:, 0:T]:
           e.reduce_max(oo, i, axis=AXX).then_inc(V.h, 1))
        QN_V[h] = V.bump()
    QN_DONE_V = V.n

    wait("tensor", V, QN_V[NH - 1])
    lg_tiles = []
    for qt in range(QT):
        for dc in range(DC):
            lg_tiles.append((P[3][:, qt * NE:(qt + 1) * NE],
                             bb(O_XNT + 2 * (dc * T + qt * 128), 128),
                             bb(O_RTR + 2 * dc * NE, NE),
                             dc == 0, dc == DC - 1))
    t_lg = mm_group(None, lg_tiles, TE)
    wait("vector", TE, t_lg)
    em("vector", lambda e, oo=bf(O_LGS, NE * QT), i=P[3][:, 0:NE * QT]:
       e.tensor_copy(oo, i).then_inc(V.h, 1))
    v_lg = V.bump()
    wait("vector", V, v_lg)
    em("vector", lambda e, oo=bf(O_M1, QT),
       i=bf(O_LGS, NE * QT).rearrange("p (q n) -> p q n", q=QT):
       e.reduce_max(oo, i, axis=AXX).then_inc(V.h, 1))
    v_m1 = V.bump()
    wait("vector", V, v_m1)
    for qt in range(QT):
        em("vector", lambda e, oo=bf(O_MSK + 4 * qt * NE, NE), i=bf(O_LGS + 4 * qt * NE, NE),
           m=bf(O_M1 + 4 * qt, 1):
           e.tensor_scalar(oo, i, m, None, ALU.is_ge).then_inc(V.h, 1))
        V.bump()
    v_eq = V.n
    wait("vector", V, v_eq)
    for qt in range(QT):
        em("vector", lambda e, oo=bf(O_MSK + 4 * qt * NE, NE), i=bf(O_MSK + 4 * qt * NE, NE),
           l=bf(O_LGS + 4 * qt * NE, NE):
           e.scalar_tensor_tensor(oo, i, -1e30, l, ALU.mult, ALU.add).then_inc(V.h, 1))
        V.bump()
    v_mk = V.n
    wait("vector", V, v_mk)
    em("vector", lambda e, oo=bf(O_M2, QT),
       i=bf(O_MSK, NE * QT).rearrange("p (q n) -> p q n", q=QT):
       e.reduce_max(oo, i, axis=AXX).then_inc(V.h, 1))
    v_m2 = V.bump()
    wait("vector", V, v_m2)
    em("vector", lambda e, oo=bf(O_M2, QT), a=bf(O_M2, QT), b=bf(O_M1, QT):
       e.tensor_sub(oo, a, b).then_inc(V.h, 1))
    v_d = V.bump()
    wait("scalar", V, v_d)
    em("scalar", lambda e, oo=bf(O_M2, QT), i=bf(O_M2, QT):
       e.activation(oo, i, AF.Exp).then_inc(SC.h, 1))
    sc_t = SC.bump()
    wait("vector", SC, sc_t)
    em("vector", lambda e, oo=bf(O_M1, QT), i=bf(O_M2, QT):
       e.tensor_scalar_add(oo, i, 1.0).then_inc(V.h, 1))
    v_o = V.bump()
    wait("vector", V, v_o)
    em("vector", lambda e, oo=bf(O_M2, QT), i=bf(O_M1, QT):
       e.reciprocal(oo, i).then_inc(V.h, 1))
    v_r = V.bump()
    wait("vector", V, v_r)
    em("vector", lambda e, oo=bf(O_COMBT, QT), a=bf(O_M1, QT), b=bf(O_M2, QT):
       e.tensor_mul(oo, a, b).then_inc(V.h, 1))
    v_combtok = V.bump()
    wait("tensor", V, v_combtok)
    em("tensor", lambda e, oo=P[2][0:max(QT, 1), 0:128], i=bf(O_COMBT, QT), idt=ident_f32:
       e.transpose(oo, i, idt).then_inc(TE.h, 1))
    t_ct = TE.bump()
    wait("vector", TE, t_ct)
    em("vector", lambda e, oo=bf(O_COMBT, 128, parts=QT), i=P[2][0:QT, 0:128]:
       e.tensor_copy(oo, i).then_inc(V.h, 1))
    v_ctc = V.bump()
    wait("sync", V, v_ctc)
    cbst = dma("sync", s_cd2, comb_dr[:, :], bf(O_COMBT, 128, parts=QT))

    yacc = bf(C_Y, DC * T)
    NSL = (DC + 1) // 2

    def load_wo(sidx):
        pr = sidx % 2
        ncc = min(2, DC - sidx * 2)
        wait("gpsimd", g_wo[pr], g_wo[pr].n)
        dma("gpsimd", g_wo[pr],
            bb(F_WO[pr], DC * ncc * 128).rearrange("p (dc c) -> p dc c", dc=DC),
            wo_d.ap()[:, sidx * 256:sidx * 256 + ncc * 128]
            .rearrange("(dc p) c -> p dc c", p=128))

    C_SILU = {}
    C_HM = {}
    C_YACC_V = {}
    C_TDOWN = {}
    C_GLAST = {}
    C_ULAST = {}

    for gi, (kind, g0, gs) in enumerate(allgroups):
        gsrc = (wg_d, wu_d, wd_d) if kind == "s" else (rg_d, ru_d, rd_d)
        if gi == 0:
            lgld, luld, ldld = lgld0, luld0, ldld0
        else:
            wait("gpsimd", TE, C_GLAST[gi - 1])
            wait("gpsimd", g_slab[0], g_slab[0].n)
            lgld = dma("gpsimd", g_slab[0],
                       bb(C_WG, DC * gs * 128).rearrange("p (dc i) -> p dc i", dc=DC),
                       gsrc[0].ap()[:, g0 * 128:(g0 + gs) * 128].rearrange("(dc p) i -> p dc i", p=128))
            wait("gpsimd", TE, C_ULAST[gi - 1])
            wait("gpsimd", g_slab[1], g_slab[1].n)
            luld = dma("gpsimd", g_slab[1],
                       bb(C_WU, DC * gs * 128).rearrange("p (dc i) -> p dc i", dc=DC),
                       gsrc[1].ap()[:, g0 * 128:(g0 + gs) * 128].rearrange("(dc p) i -> p dc i", p=128))
            wait("gpsimd", TE, C_TDOWN[gi - 1])
            wait("gpsimd", g_slab[2], g_slab[2].n)
            ldld = dma("gpsimd", g_slab[2],
                       bb(C_WD, gs * D).rearrange("p (il c) -> p il c", il=gs),
                       gsrc[2].ap()[g0 * 128:(g0 + gs) * 128, :].rearrange("(il p) c -> p il c", p=128))
        if gi == 2:
            emit_cc_v(0)
        if gi == 4:
            emit_cc_k(1)
        if gi == 6:
            emit_cc_v(1)
        if gi == 8:
            emit_cc_k(2)
        if gi == 11:
            emit_cc_v(2)
        if gi == 13:
            emit_cc_k(3)
        if gi == 15:
            emit_cc_v(3)
        if gi == 14:
            em("gpsimd", lambda e: e.memset(bf(O_ONESF, 128), 1.0).then_inc(g_ms.h, 1))
            ONESF_OK = g_ms.bump()
        if gi == 10:
            wait("gpsimd", g_cc, CC_KN)
            knld = dma("gpsimd", g_kn, bf(O_KNG, NH, parts=1), kn_gat[0:1, :])
        if gi == 8:
            wait("gpsimd", s_cd2, cbst)
            cbld = dma("gpsimd", g_cb, bb(O_COMB, T), bass.AP(comb_dr, 0, [[0, 128], [1, T]]))
        if gi == 12:
            wait("vector", g_kn, knld)
            em("vector", lambda e, oo=bf(O_CROW, NH, parts=1), a=bf(O_QN2, NH, parts=1),
               b=bf(O_KNG, NH, parts=1):
               e.tensor_mul(oo, a, b).then_inc(V.h, 1))
            v_c1 = V.bump()
            wait("scalar", V, v_c1)
            em("scalar", lambda e, oo=bf(O_CROW2, NH, parts=1), i=bf(O_CROW, NH, parts=1):
               e.activation(oo, i, AF.Sqrt).then_inc(SC.h, 1))
            sc_cs = SC.bump()
            wait("vector", SC, sc_cs)
            em("vector", lambda e, oo=bf(O_CROW, NH, parts=1), i=bf(O_CROW2, NH, parts=1):
               e.tensor_scalar_mul(oo, i, -1.05 * scale).then_inc(V.h, 1))
            v_cn = V.bump()
            wait("sync", V, v_cn)
            cst2 = dma("sync", s_cd1, cneg_dr[0:1, :], bf(O_CROW, NH, parts=1))
        for il in range(gs):
            if gi == 0 and il == 0:
                wait("tensor", V, QT_EV[DC - 1])
            wait("tensor", g_slab[0], lgld)
            prev_silu = C_SILU.get((gi, il - 1)) or (C_SILU.get((gi - 1, allgroups[gi - 1][2] - 1)) if gi > 0 else 0)
            wait("tensor", SC, prev_silu or 0)
            tiles = []
            for dc in range(DC):
                w = bb((W0_G if gi == 0 else C_WG) + 2 * (dc * gs * 128 + il * 128), 128)
                for f in range(NF):
                    tiles.append((P[0][:, f * NW:(f + 1) * NW], w,
                                  bb(O_XNT + 2 * (dc * T + f * NW), NW),
                                  dc == 0, dc == DC - 1))
            t_g = mm_group(None, tiles, TE)
            wait("tensor", g_slab[1], luld)
            prev_hm = C_HM.get((gi, il - 1)) or (C_HM.get((gi - 1, allgroups[gi - 1][2] - 1)) if gi > 0 else 0)
            wait("tensor", V, prev_hm or 0)
            tiles = []
            for dc in range(DC):
                w = bb((W0_U if gi == 0 else C_WU) + 2 * (dc * gs * 128 + il * 128), 128)
                for f in range(NF):
                    tiles.append((P[1][:, f * NW:(f + 1) * NW], w,
                                  bb(O_XNT + 2 * (dc * T + f * NW), NW),
                                  dc == 0, dc == DC - 1))
            t_u = mm_group(None, tiles, TE)
            C_GLAST[gi] = t_g
            C_ULAST[gi] = t_u
            wait("scalar", TE, t_g)
            wait("scalar", V, prev_hm or 0)
            em("scalar", lambda e, oo=bb(C_SG, T), i=P[0][:, 0:T]:
               e.activation(oo, i, AF.Silu).then_inc(SC.h, 1))
            C_SILU[gi, il] = SC.bump()
            wait("vector", SC, C_SILU[gi, il])
            wait("vector", TE, t_u)
            if gi >= 2:
                wait("vector", TE, C_TDOWN[gi - 2])
            if kind == "s":
                em("vector", lambda e, oo=bb(C_H[gi % 2] + 2 * il * T, T), a=bb(C_SG, T),
                   b=P[1][:, 0:T]: e.tensor_mul(oo, a, b).then_inc(V.h, 1))
                C_HM[gi, il] = V.bump()
            else:
                em("vector", lambda e, oo=bb(C_T2, T), a=bb(C_SG, T), b=P[1][:, 0:T]:
                   e.tensor_mul(oo, a, b).then_inc(V.h, 1))
                v_t2 = V.bump()
                wait("vector", V, v_t2)
                wait("vector", g_cb, cbld)
                em("vector", lambda e, oo=bb(C_H[gi % 2] + 2 * il * T, T), a=bb(C_T2, T),
                   b=bb(O_COMB, T): e.tensor_mul(oo, a, b).then_inc(V.h, 1))
                C_HM[gi, il] = V.bump()
        wait("tensor", g_slab[2], ldld)
        wait("tensor", V, C_HM[gi, gs - 1])
        for cc in range(DC):
            pp = P[2 + (cc % 2)]
            if cc >= 2:
                wait("tensor", V, C_YACC_V[gi, cc - 2])
            elif gi > 0:
                wait("tensor", V, C_YACC_V[gi - 1, DC - 2 + cc])
            else:
                wait("tensor", V, v_ctc)
            tiles = []
            for il in range(gs):
                w = bb((W0_D if gi == 0 else C_WD) + 2 * (il * D + cc * 128), 128)
                for f in range(NF):
                    tiles.append((pp[:, f * NW:(f + 1) * NW], w,
                                  bb(C_H[gi % 2] + 2 * (il * T + f * NW), NW),
                                  il == 0, il == gs - 1))
            t_d = mm_group(None, tiles, TE)
            wait("vector", TE, t_d)
            if gi == 0:
                if cc == 0:
                    wait("vector", s_kst, KST_ALL)
                    wait("vector", s_vst, VST_ALL)
                em("vector", lambda e, oo=bf(C_Y + 4 * cc * T, T), i=pp[:, 0:T]:
                   e.tensor_copy(oo, i).then_inc(V.h, 1))
            else:
                wait("vector", V, C_YACC_V[gi - 1, cc])
                em("vector", lambda e, oo=bf(C_Y + 4 * cc * T, T), i=pp[:, 0:T],
                   a=bf(C_Y + 4 * cc * T, T):
                   e.tensor_add(oo, a, i).then_inc(V.h, 1))
            C_YACC_V[gi, cc] = V.bump()
        C_TDOWN[gi] = TE.n

    FFN_LAST_V = C_YACC_V[len(allgroups) - 1, DC - 1]
    C_END_SC = SC.n

    wait("gpsimd", s_cd1, cst2)
    cngld = dma("gpsimd", g_kn, bf(O_CNEG, NH),
                bass.AP(cneg_dr, 0, [[0, 128], [1, NH]]))

    wait("gpsimd", TE, C_TDOWN[len(allgroups) - 1])
    load_wo(0)

    LG = len(allgroups) - 1
    wait("scalar", V, C_YACC_V[LG, 3])
    ysp_a = dma("scalar", s_ysp[0], yffn_dr[:, 0:4 * T], bf(C_Y, 4 * T))
    wait("scalar", V, C_YACC_V[LG, 9])
    ysp_b = dma("scalar", s_ysp[1], yffn_dr[:, 4 * T:10 * T], bf(C_Y + 16 * T, 6 * T))
    wait("scalar", V, C_YACC_V[LG, 15])
    ysp_c = dma("scalar", s_ysp[2], yffn_dr[:, 10 * T:16 * T], bf(C_Y + 40 * T, 6 * T))

    for h in range(NH):
        pr = h % 2
        vpr = (h + 1) % 2
        if h == 0:
            wait("sync", TE, C_GLAST[LG])
            wait("sync", g_cc, CC_V[0])
        if h % 4 == 0 and h > 0:
            wait("sync", g_cc, CC_V[h // 4])
        if h >= 2:
            wait("sync", pvsem, (h - 1) * KC)
        ktg = kT_gatQ[h // 4]
        vg = v_gatQ[h // 4]
        hh = h % 4
        if h == 1:
            wait("sync", TE, C_ULAST[LG])
        for r in range(C):
            dma("sync", s_kv[pr], bb(E_KH[pr] + 2 * r * T, T),
                ktg[r * (D // NQ) + hh * 128:r * (D // NQ) + (hh + 1) * 128, :])
        if h == 0:
            wait("sync", s_ysp[0], ysp_a)
        if h == 1:
            wait("sync", TE, C_TDOWN[LG])
        for r in range(C):
            dma("sync", s_kv[pr],
                bb(E_VH[vpr] + 2 * r * T, T).rearrange("p (kc d) -> p kc d", d=128),
                vg[r * T:(r + 1) * T, hh * 128:(hh + 1) * 128]
                .rearrange("(kc p) d -> p kc d", p=128))

    slot_wait = [(expsem, 0), (expsem, 0), (expsem, 0)]
    eslot = [(dacc, 0)] * 4
    pb_rd = [(dacc, 0), (pacc, 0)]
    FIN_V = {}
    COMB_V = {}
    PACC_AT = {}
    DACC_AT = {}

    def emit_combine(fh):
        wait("vector", pacc, PACC_AT[fh])
        wait("vector", dacc, DACC_AT[fh])
        em("vector", lambda e, oo=bf(E_EA, T), a=bf(E_EA, T), b=bf(E_EA2, T):
           e.tensor_tensor(oo, a, b, ALU.add).then_inc(V.h, 1))
        COMB_V[fh] = V.bump()

    DEN_T = {}
    DEN_SLOT = {}

    def emit_den(fh, slot):
        wsem, wval = slot_wait[slot]
        wait("tensor", wsem, wval)
        wait("tensor", V, COMB_V[fh])
        ppD = P[slot]
        tiles = [(ppD[:, f * NW:(f + 1) * NW], ones_f32,
                  bf(E_EA + 4 * f * NW, NW), True, True) for f in range(NF)]
        DEN_T[fh] = mm_group(None, tiles, TE)
        DEN_SLOT[fh] = slot

    def emit_rcp(fh):
        slot = DEN_SLOT[fh]
        wait("scalar", TE, DEN_T[fh])
        if fh >= 1:
            wait("scalar", V, FIN_V[fh - 1])
        em("scalar", lambda e, oo=bf(O_RCPA, T), i=P[slot][:, 0:T]:
           e.activation(oo, i, AF.Ln).then_inc(SC.h, 1))
        sc_ln = SC.bump()
        slot_wait[slot] = (SC, sc_ln)
        wait("scalar", SC, sc_ln)
        em("scalar", lambda e, oo=bf(O_RCPB, T), i=bf(O_RCPA, T):
           e.activation(oo, i, AF.Exp, scale=-1.0).then_inc(SC.h, 1))
        sc_rcp = SC.bump()
        wait("vector", SC, sc_rcp)
        wait("vector", vfin, fh + 1)
        em("vector", lambda e, oo=bb(E_AO + 2 * fh * T, T), a=bb(E_AO + 2 * fh * T, T),
           b=bf(O_RCPB, T): e.tensor_mul(oo, a, b).then_inc(V.h, 1))
        FIN_V[fh] = V.bump()

    for h in range(NH):
        pr = h % 2
        g_end_prev = h * KC
        kv_target = 16 * (2 * C) * (h // 2 + 1)
        wait("tensor", s_kv[pr], kv_target)
        if h == 0:
            wait("tensor", V, FFN_LAST_V)
            wait("tensor", SC, C_END_SC)
            wait("tensor", g_ms, ONESF_OK)
            wait("scalar", g_kn, cngld)
            wait("scalar", s_ysp[1], ysp_b)
            wait("vector", s_ysp[1], ysp_b)
            wait("gpsimd", s_ysp[1], ysp_b)
        for kc in range(KC):
            g = h * KC + kc
            wsem, wval = slot_wait[g % 3]
            wait("tensor", wsem, wval)
            pp = P[g % 3]
            tiles = [(pp[:, f * NW:(f + 1) * NW],
                      bb(E_KH[pr] + 2 * kc * 128, 128),
                      bb(O_QTT + 2 * (h * T + f * NW), NW), True, True)
                     for f in range(NF)]
            mm_group(None, tiles, scsem)
            slot_wait[g % 3] = (expsem, g + 1)
            wait("scalar", scsem, g + 1)
            wait("scalar", pvsem, g - 3)
            if g >= 4:
                esem, ev = eslot[g % 4]
                wait("scalar", esem, ev)
            em("scalar", lambda e, oo=bb(E_E[g % 4], T), i=pp[:, 0:T],
               b=bf(O_CNEG + 4 * h, 1), ssc=scale:
               e.activation(oo, i, AF.Exp, bias=b, scale=ssc).then_inc(expsem.h, 1))
            expsem.bump()
            if h >= 1 and kc == 1:
                emit_combine(h - 1)
            if h >= 1 and kc == 3:
                emit_den(h - 1, g_end_prev % 3)
            if h >= 1 and kc == 5:
                emit_rcp(h - 1)
            if kc % 2 == 1:
                j = kc // 2
                on_pool = j % 4 == 0
                wait("vector", expsem, g + 1)
                if on_pool and j >= 4:
                    wait("vector", pacc, pb_rd[0][1])
                if not on_pool and j >= 2:
                    wait("vector", dacc, pb_rd[1][1])
                em("vector", lambda e, oo=bb(E_PB[0 if on_pool else 1], T),
                   a=bb(E_E[(g - 1) % 4], T),
                   bbb=bb(E_E[g % 4], T), hh=dacc.h:
                   e.tensor_tensor(oo, a, bbb, ALU.add).then_inc(hh, 1))
                pj = dacc.bump()
                eslot[(g - 1) % 4] = (dacc, pj)
                eslot[g % 4] = (dacc, pj)
                if not on_pool:
                    wait("vector", dacc, pj)
                    if j == 1:
                        if h > 0:
                            wait("vector", TE, DEN_T[h - 1])
                        em("vector", lambda e, oo=bf(E_EA, T), i=bb(E_PB[1], T), hh=dacc.h:
                           e.tensor_copy(oo, i).then_inc(hh, 1))
                    else:
                        em("vector", lambda e, oo=bf(E_EA, T), a=bf(E_EA, T),
                           i=bb(E_PB[1], T), hh=dacc.h:
                           e.tensor_tensor(oo, a, i, ALU.add).then_inc(hh, 1))
                    pb_rd[1] = (dacc, dacc.bump())
                else:
                    wait("gpsimd", dacc, pj)
                    if j == 0:
                        if h > 0:
                            wait("gpsimd", V, COMB_V[h - 1])
                        em("gpsimd", lambda e, oo=bf(E_EA2, T), i=bb(E_PB[0], T), hh=pacc.h:
                           e.tensor_copy(oo, i).then_inc(hh, 1))
                    else:
                        em("gpsimd", lambda e, oo=bf(E_EA2, T), a=bf(E_EA2, T),
                           i=bb(E_PB[0], T), hh=pacc.h:
                           e.tensor_tensor(oo, a, i, ALU.add).then_inc(hh, 1))
                    pb_rd[0] = (pacc, pacc.bump())
            def emit_pv(gg, kk):
                wait("tensor", expsem, gg + 1)
                if kk == 0:
                    wait("tensor", vfin, h)
                tiles = [(P[3][:, f * NW:(f + 1) * NW],
                          bb(E_VH[(h + 1) % 2] + 2 * kk * 128, 128),
                          bb(E_E[gg % 4] + 2 * f * NW, NW),
                          kk == 0, kk == KC - 1) for f in range(NF)]
                mm_group(None, tiles, pvsem)
            if kc >= 2:
                emit_pv(g - 2, kc - 2)
        emit_pv(h * KC + KC - 2, KC - 2)
        emit_pv(h * KC + KC - 1, KC - 1)
        PACC_AT[h] = pacc.n
        DACC_AT[h] = dacc.n
        wait("vector", pvsem, (h + 1) * KC)
        if h == 0:
            wait("vector", s_ysp[2], ysp_c)
        em("vector", lambda e, oo=bb(E_AO + 2 * h * T, T), i=P[3][:, 0:T]:
           e.tensor_copy(oo, i).then_inc(vfin.h, 1))
        vfin.bump()
        if h == NH - 1:
            emit_combine(h)
            emit_den(h, ((h + 1) * KC) % 3)
            emit_rcp(h)
    ATT_DONE = NH * KC

    wait("gpsimd", V, FIN_V[NH - 1])
    load_wo(1)

    F_EV = {}
    XLD = {}
    PADD_V = {}
    for cc in range(DC):
        sidx = cc // 2
        pr = sidx % 2
        if cc == 0:
            wait("sync", pvsem, ATT_DONE)
            wait("sync", V, FIN_V[NH - 1])
            wait("sync", s_ysp[2], ysp_c)
        if cc >= 2:
            wait("sync", V, F_EV[cc - 2])
        ybl = dma("sync", s_yb[cc % 2], bf(F_YB[cc % 2], T), yffn_dr[:, cc * T:(cc + 1) * T])
        if cc >= 2:
            wait("scalar", V, PADD_V[cc - 2])
        if cc >= 4:
            wait("scalar", V, F_EV[cc - 4])
        XLD[cc] = dma("scalar", s_xr[cc % 2], bf(F_XR[cc % 2] + 4 * ((cc // 2) % 2) * T, T),
                      xT_d[cc * 128:(cc + 1) * 128, :])
        wait("tensor", g_wo[pr], 16 * (sidx // 2 + 1))
        if cc == 0:
            wait("tensor", V, FIN_V[NH - 1])
        if cc >= 2:
            wait("tensor", V, F_EV[cc - 2])
        pp = P[2 + (cc % 2)]
        tiles = []
        for dc in range(DC):
            w = bb(F_WO[pr] + 2 * (dc * 2 * 128 + (cc - sidx * 2) * 128), 128)
            for f in range(NF):
                tiles.append((pp[:, f * NW:(f + 1) * NW], w,
                              bb(E_AO + 2 * (dc * T + f * NW), NW),
                              dc == 0, dc == DC - 1))
        t_wo = mm_group(None, tiles, TE)
        if (cc % 2 == 1 or cc == DC - 1) and sidx + 2 < NSL:
            wait("gpsimd", TE, t_wo)
            load_wo(sidx + 2)
        wait("vector", s_yb[cc % 2], ybl)
        wait("vector", s_xr[cc % 2], XLD[cc])
        em("vector", lambda e, oo=bf(F_YB[cc % 2], T), a=bf(F_YB[cc % 2], T),
           b=bf(F_XR[cc % 2] + 4 * ((cc // 2) % 2) * T, T):
           e.tensor_add(oo, a, b).then_inc(V.h, 1))
        PADD_V[cc] = V.bump()
        wait("vector", TE, t_wo)
        wait("vector", V, PADD_V[cc])
        if cc >= 4:
            wait("vector", s_out[cc % 4], 16 * (cc // 4))
        em("vector", lambda e, oo=bf(F_OS[cc % 2] + 4 * ((cc // 2) % 2) * T, T),
           i=pp[:, 0:T], b=bf(O_BO + 4 * cc, 1), y=bf(F_YB[cc % 2], T):
           e.scalar_tensor_tensor(oo, i, b, y, ALU.add, ALU.add).then_inc(V.h, 1))
        F_EV[cc] = V.bump()
        wait("sync", V, F_EV[cc])
        dma("sync", s_out[cc % 4], out_d[cc * 128:(cc + 1) * 128, :],
            bf(F_OS[cc % 2] + 4 * ((cc // 2) % 2) * T, T))

    for so in s_out:
        wait("sync", so, so.n)

    for s, v in max_waited.items():
        assert v <= s.n, f"wait beyond final count on sem {s.name}: waited {v}, final {s.n}"

    with nc.Block() as block:
        @block.sync
        def _(e):
            for f in ops["sync"]:
                f(e)

        @block.gpsimd
        def _(e):
            for f in ops["gpsimd"]:
                f(e)

        @block.tensor
        def _(e):
            for f in ops["tensor"]:
                f(e)

        @block.scalar
        def _(e):
            for f in ops["scalar"]:
                f(e)

        @block.vector
        def _(e):
            for f in ops["vector"]:
                f(e)

    return nc


def _host_inputs(cfg, inputs):
    S, D, T, C = cfg.S, cfg.D, cfg.T, cfg.C
    f32 = np.float32

    def t2(a):
        return np.ascontiguousarray(np.asarray(a, dtype=f32).T)

    def b2(b, nch):
        return np.ascontiguousarray(np.asarray(b, dtype=f32).reshape(nch, 128).T)

    x = np.ascontiguousarray(np.asarray(inputs["L0_attn_hidden_states"], dtype=f32).reshape(S, D))

    shared = dict(
        wqdT=t2(inputs["L0_attn_Wq_down"]), bqd2=b2(inputs["L0_attn_bq_down"], cfg.LC),
        wkvdT=t2(inputs["L0_attn_Wkv_down"]), bkv2=b2(inputs["L0_attn_bkv_down"], cfg.LC),
        wquT=t2(inputs["L0_attn_Wq_up"]), bqu2=b2(inputs["L0_attn_bq_up"], cfg.DC),
        wknT=t2(inputs["L0_attn_Wk_nope"]), bkn2=b2(inputs["L0_attn_bk_nope"], cfg.NC2),
        wkrT=t2(inputs["L0_attn_Wk_rope"]), bkr2=b2(inputs["L0_attn_bk_rope"], 2 * cfg.JH),
        wvuT=t2(inputs["L0_attn_Wv_up"]), bvu2=b2(inputs["L0_attn_bv_up"], cfg.DC),
        woT=t2(inputs["L0_attn_Wo"]), bo2=b2(inputs["L0_attn_bo"], cfg.DC),
        routerT=t2(inputs["L0_ffn_router_weight"]),
        wgT=t2(inputs["L0_ffn_sh_gate"]), wuT=t2(inputs["L0_ffn_sh_up"]),
        wdT=t2(inputs["L0_ffn_sh_down"]),
        rgT=t2(inputs["L0_ffn_r_gate"]), ruT=t2(inputs["L0_ffn_r_up"]),
        rdT=t2(inputs["L0_ffn_r_down"]),
    )

    ROT = D // 2
    inv = (1.0 / (np.float32(cfg.base) ** (np.arange(0, ROT, 2, dtype=f32) / f32(ROT)))).astype(f32)
    tpos = np.arange(S, dtype=f32)
    freqs = np.outer(tpos, inv).astype(f32)
    cosT = np.ascontiguousarray(np.cos(freqs).astype(f32).T)
    sinT = np.ascontiguousarray(np.sin(freqs).astype(f32).T)

    in_maps = []
    for c in range(C):
        m = dict(shared)
        m["x"] = np.ascontiguousarray(x[c * T:(c + 1) * T])
        m["xT"] = np.ascontiguousarray(x[c * T:(c + 1) * T].T)
        m["cosT"] = np.ascontiguousarray(cosT[:, c * T:(c + 1) * T])
        m["sinT"] = np.ascontiguousarray(sinT[:, c * T:(c + 1) * T])
        in_maps.append(m)
    return in_maps


def kernel(**inputs):
    cfg = CFG
    nc = build(cfg)
    in_maps = _host_inputs(cfg, inputs)
    res = run_bass_kernel_spmd(nc, in_maps, core_ids=list(range(cfg.C)))
    out = np.concatenate(
        [np.ascontiguousarray(np.asarray(r["out"], dtype=np.float32).T) for r in res.results],
        axis=0)
    return out.reshape(1, cfg.S, cfg.D)


# version 31
# speedup vs baseline: 1.0467x; 1.0270x over previous
import numpy as np
import ml_dtypes

import concourse.bass as bass
import concourse.mybir as mybir
from concourse.bass_utils import run_bass_kernel_spmd

F32 = mybir.dt.float32
BF16 = mybir.dt.bfloat16
U8 = mybir.dt.uint8
AF = mybir.ActivationFunctionType
ALU = mybir.AluOpType
AXX = mybir.AxisListType.X
BF = ml_dtypes.bfloat16


class Cfg:
    def __init__(self, S=8192, D=2048, L=512, IS=8192, IR=1024, NE=8, C=8,
                 eps=1e-5, base=10000.0):
        self.S, self.D, self.L, self.IS, self.IR, self.NE, self.C = S, D, L, IS, IR, NE, C
        self.eps, self.base = eps, base
        self.T = S // C
        self.QT = self.T // 128
        self.DC = D // 128
        self.LC = L // 128
        self.NH = D // 128
        self.NC2 = self.DC // 2
        self.JH = (D // 4) // 128
        self.KC = S // 128
        self.ISC = IS // 128
        self.IRC = IR // 128
        self.IG = 4
        self.NF = max(1, self.T // 512)
        self.NW = min(512, self.T)


CFG = Cfg()


def _groups(nchunks, gsz):
    out, i = [], 0
    while i < nchunks:
        out.append((i, min(gsz, nchunks - i)))
        i += gsz
    return out


def build(cfg):
    S, D, L, T = cfg.S, cfg.D, cfg.L, cfg.T
    DC, LC, QT, NH, KC = cfg.DC, cfg.LC, cfg.QT, cfg.NH, cfg.KC
    JH, NC2 = cfg.JH, cfg.NC2
    NE, NF, NW = cfg.NE, cfg.NF, cfg.NW
    C = cfg.C
    scale = float(np.float32(1.0 / np.sqrt(np.float32(128.0))))

    nc = bass.Bass("TRN2", target_bir_lowering=False)

    x_d = nc.dram_tensor("x", [T, D], F32, kind="ExternalInput")
    cos_d = nc.dram_tensor("cosT", [JH * 128, T], F32, kind="ExternalInput")
    sin_d = nc.dram_tensor("sinT", [JH * 128, T], F32, kind="ExternalInput")
    wqd_d = nc.dram_tensor("wqdT", [D, L], BF16, kind="ExternalInput")
    wkvd_d = nc.dram_tensor("wkvdT", [D, L], BF16, kind="ExternalInput")
    wqu_d = nc.dram_tensor("wquT", [L, D], BF16, kind="ExternalInput")
    wkn_d = nc.dram_tensor("wknT", [L, D // 2], BF16, kind="ExternalInput")
    wkr_d = nc.dram_tensor("wkrT", [L, D // 2], BF16, kind="ExternalInput")
    wvu_d = nc.dram_tensor("wvuT", [L, D], BF16, kind="ExternalInput")
    wo_d = nc.dram_tensor("woT", [D, D], BF16, kind="ExternalInput")
    rt_d = nc.dram_tensor("routerT", [D, NE], BF16, kind="ExternalInput")
    wg_d = nc.dram_tensor("wgT", [D, cfg.IS], BF16, kind="ExternalInput")
    wu_d = nc.dram_tensor("wuT", [D, cfg.IS], BF16, kind="ExternalInput")
    wd_d = nc.dram_tensor("wdT", [cfg.IS, D], BF16, kind="ExternalInput")
    rg_d = nc.dram_tensor("rgT", [D, cfg.IR], BF16, kind="ExternalInput")
    ru_d = nc.dram_tensor("ruT", [D, cfg.IR], BF16, kind="ExternalInput")
    rd_d = nc.dram_tensor("rdT", [cfg.IR, D], BF16, kind="ExternalInput")
    bqd_d = nc.dram_tensor("bqd2", [128, LC], F32, kind="ExternalInput")
    bkv_d = nc.dram_tensor("bkv2", [128, LC], F32, kind="ExternalInput")
    bqu_d = nc.dram_tensor("bqu2", [128, DC], F32, kind="ExternalInput")
    bkn_d = nc.dram_tensor("bkn2", [128, NC2], F32, kind="ExternalInput")
    bkr_d = nc.dram_tensor("bkr2", [128, 2 * JH], F32, kind="ExternalInput")
    bvu_d = nc.dram_tensor("bvu2", [128, DC], F32, kind="ExternalInput")
    bo_d = nc.dram_tensor("bo2", [128, DC], F32, kind="ExternalInput")
    xT_d = nc.dram_tensor("xT", [D, T], F32, kind="ExternalInput")
    out_d = nc.dram_tensor("out", [D, T], F32, kind="ExternalOutput")

    id_bf_d = nc.inline_tensor(np.eye(128, dtype=BF), "id_bf")
    id_f32_d = nc.inline_tensor(np.eye(128, dtype=np.float32), "id_f32")

    NQ = 4
    kT_locQ = [nc.dram_tensor(f"kT_loc{q}", [D // NQ, T], BF16) for q in range(NQ)]
    v_locQ = [nc.dram_tensor(f"v_loc{q}", [T, D // NQ], BF16) for q in range(NQ)]
    kT_gatQ = [nc.dram_tensor(f"kT_gat{q}", [C * (D // NQ), T], BF16) for q in range(NQ)]
    v_gatQ = [nc.dram_tensor(f"v_gat{q}", [C * T, D // NQ], BF16) for q in range(NQ)]
    kn_loc = nc.dram_tensor("kn_loc", [1, NH], F32)
    kn_gat = nc.dram_tensor("kn_gat", [1, NH], F32)
    comb_dr = nc.dram_tensor("comb_dr", [QT, 128], F32)
    cneg_dr = nc.dram_tensor("cneg_dr", [1, NH], F32)
    yffn_dr = nc.dram_tensor("yffn_dr", [128, DC * T], F32)

    o = 0

    def take(nbytes, align=256):
        nonlocal o
        o = (o + align - 1) // align * align
        r = o
        o += nbytes
        return r

    O_IDBF = take(256)
    O_IDF32 = take(512)
    O_ONES = take(256)
    O_BQD = take(4 * LC); O_BKV = take(4 * LC); O_BQU = take(4 * DC)
    O_BKN = take(4 * NC2); O_BKR = take(4 * 2 * JH); O_BVU = take(4 * DC)
    O_BO = take(4 * DC)
    O_COMB = take(2 * T)
    O_CNEG = take(4 * NH)
    O_KN2 = take(4 * NH)
    O_QN2 = take(4 * NH)
    O_KNG = take(4 * NH)
    O_CROW = take(4 * NH)
    O_CROW2 = take(4 * NH)
    O_LGS = take(4 * NE * QT)
    O_COMBT = take(4 * 128)
    O_MSK = take(4 * NE * QT)
    O_M1 = take(4 * QT)
    O_M2 = take(4 * QT)
    O_RTR = take(2 * DC * NE)
    O_EPS = take(64)
    O_SMALL = take(64 * 16)
    FIXED_END = take(0, 512)

    def small(i, n=1):
        return (O_SMALL + i * 64, n)

    O_XNT = FIXED_END
    O_QTT = O_XNT + 2 * DC * T
    BIG = O_QTT + 2 * DC * T

    D4 = D * 4
    A_X = [BIG, BIG + D4, BIG + 2 * D4]
    A_XSQ = BIG + 3 * D4
    A_XNB = [A_XSQ + D4, A_XSQ + D4 + 2 * D]
    A_END = A_XSQ + D4 + 4 * D
    B_COS = BIG
    B_SIN = B_COS + 4 * JH * T
    B_KVL = B_SIN + 4 * JH * T
    B_W = B_KVL + 2 * LC * T
    B_WSZ = max(2 * DC * L, 2 * LC * D)
    B_KT = B_W + B_WSZ
    B_KR = B_KT + 2 * DC * T
    B_VSB = B_KR
    B_RT1 = B_KR + 4 * 2 * JH * T
    B_RT2 = B_RT1 + 4 * T
    B_SQ = B_RT2 + 4 * T
    B_VT = [B_SQ + 2 * T, B_SQ + 4 * T]
    B_QL = B_KVL
    B_END = B_VT[1] + 2 * T
    C_WG = BIG
    C_WU = C_WG + 2 * DC * cfg.IG * 128
    C_WD = C_WU + 2 * DC * cfg.IG * 128
    C_Y = C_WD + 2 * cfg.IG * D
    C_H = [C_Y + 4 * DC * T, C_Y + 4 * DC * T + 2 * cfg.IG * T]
    C_SG = C_H[1] + 2 * cfg.IG * T
    C_T2 = C_SG + 2 * T
    C_END = C_T2 + 2 * T
    E_KH = [BIG, BIG + 2 * S]
    E_VH = [BIG + 4 * S, BIG + 6 * S]
    E_E = [BIG + 8 * S + i * 2 * T for i in range(4)]
    E_PB = [BIG + 8 * S + 8 * T, BIG + 8 * S + 10 * T]
    E_EA = BIG + 8 * S + 12 * T
    E_EA2 = E_EA + 4 * T
    E_AO = E_EA2 + 4 * T
    E_END = E_AO + 2 * DC * T
    F_YB = [BIG, BIG + 4 * T]
    F_XR = [BIG + 8 * T, BIG + 16 * T]
    F_OS = [BIG + 24 * T, BIG + 32 * T]
    F_WO = [E_END, E_END + 2 * DC * 256]
    F_END = F_WO[1] + 2 * DC * 256

    ARENA = max(A_END, B_END, C_END, E_END, F_END)
    ARENA = (ARENA + 511) // 512 * 512
    O_QL = B_RT1
    O_ONESF = B_VT[1]
    O_RCPA = B_RT2
    O_RCPB = C_SG
    O_NSQ = B_VSB
    arena = nc.alloc_sbuf_tensor("arena", [128, ARENA], U8)

    def bb(off, n, parts=128, p0=0):
        return arena[p0:p0 + parts, off:off + 2 * n].bitcast(BF16)

    def bf(off, n, parts=128, p0=0):
        return arena[p0:p0 + parts, off:off + 4 * n].bitcast(F32)

    PT = max(512, T)
    P = [nc.alloc_psum_tensor(f"P{i}", [128, PT], F32) for i in range(4)]

    class Sem:
        def __init__(self, name):
            self.name = name
            self.h = nc.semaphore(name).__enter__()
            self.n = 0

        def bump(self, k=1):
            self.n += k
            return self.n

    s_xa = [Sem("s_xa0"), Sem("s_xa1"), Sem("s_xa2")]
    g_cst = Sem("g_cst")
    g_cs = Sem("g_cs")
    g_wB = Sem("g_wB")
    g_slab = [Sem("g_sg"), Sem("g_su"), Sem("g_sd")]
    g_wo = [Sem("g_wo0"), Sem("g_wo1")]
    g_kn = Sem("g_kn")
    g_cb = Sem("g_cb")
    g_ms = Sem("g_ms")
    g_cc = Sem("g_cc")
    s_kst = Sem("s_kst")
    s_kn2 = Sem("s_kn2")
    s_vst = Sem("s_vst")
    s_cd1 = Sem("s_cd1")
    s_cd2 = Sem("s_cd2")
    s_kv = [Sem("s_kv0"), Sem("s_kv1")]
    s_ysp = [Sem("s_yspa"), Sem("s_yspb"), Sem("s_yspc")]
    s_yb = [Sem("s_yb0"), Sem("s_yb1")]
    s_xr = [Sem("s_xr0"), Sem("s_xr1")]
    s_out = [Sem("s_out0"), Sem("s_out1"), Sem("s_out2"), Sem("s_out3")]
    scsem = Sem("scsem")
    expsem = Sem("expsem")
    pvsem = Sem("pvsem")
    dacc = Sem("dacc")
    pacc = Sem("pacc")
    vfin = Sem("vfin")
    TE = Sem("TE")
    V = Sem("V")
    SC = Sem("SC")

    ops = {k: [] for k in ("sync", "gpsimd", "tensor", "scalar", "vector")}
    max_waited = {}

    def em(eng, fn):
        ops[eng].append(fn)

    def wait(eng, sem, v):
        if v > 0:
            max_waited[sem] = max(max_waited.get(sem, 0), v)
            em(eng, lambda e, h=sem.h, vv=v: e.wait_ge(h, vv))

    def dma(eng, sem, out, in_):
        tgt = sem.bump(16)
        em(eng, lambda e, oo=out, ii=in_, h=sem.h: e.dma_start(out=oo, in_=ii).then_inc(h, 16))
        return tgt

    def mm(pp, w, r, st, sp, inc=None, inc_amt=1):
        if inc is None:
            em("tensor", lambda e, o=pp, ww=w, rr=r, s1=st, s2=sp:
               e.matmul(o, ww, rr, start=s1, stop=s2, skip_group_check=True))
        else:
            em("tensor", lambda e, o=pp, ww=w, rr=r, s1=st, s2=sp, h=inc.h, a=inc_amt:
               e.matmul(o, ww, rr, start=s1, stop=s2, skip_group_check=True).then_inc(h, a))

    ident_bf = bb(O_IDBF, 128)
    ident_f32 = bf(O_IDF32, 128)
    ones_bf = bb(O_ONES, 128)
    ones_f32 = bf(O_ONESF, 128)

    dma("gpsimd", g_cst, bb(O_IDBF, 128), id_bf_d[:, :])
    dma("gpsimd", g_cst, bf(O_IDF32, 128), id_f32_d[:, :])
    for od, dd in ((O_BQD, bqd_d), (O_BKV, bkv_d), (O_BQU, bqu_d),
                   (O_BKN, bkn_d), (O_BKR, bkr_d), (O_BVU, bvu_d), (O_BO, bo_d)):
        dma("gpsimd", g_cst, bf(od, dd.shape[1]), dd[:, :])
    dma("gpsimd", g_cst, bb(O_RTR, DC * NE).rearrange("p (dc e) -> p dc e", dc=DC),
        rt_d.ap().rearrange("(dc p) e -> p dc e", p=128))
    CST_ALL = g_cst.n
    em("gpsimd", lambda e: e.memset(bb(O_ONES, 128), 1.0).then_inc(g_ms.h, 1))
    ONES_OK = g_ms.bump()
    em("gpsimd", lambda e, v=float(cfg.eps): e.memset(bf(O_EPS, 1), v).then_inc(g_ms.h, 1))
    EPS_OK = g_ms.bump()

    xnT = bb(O_XNT, DC * T)
    A_VDONE = {}
    A_TCOPY = {}
    A_TTR = {}
    A_SCEND = {}
    for qt in range(QT):
        pr = qt % 3
        p2 = qt % 2
        sb = 8 * p2
        if qt >= 3:
            wait("sync", V, A_VDONE[qt - 3])
        xld = dma("sync", s_xa[pr], bf(A_X[pr], D), x_d[qt * 128:(qt + 1) * 128, :])
        xap = bf(A_X[pr], D)

        wait("scalar", s_xa[pr], xld)
        if qt >= 2:
            wait("scalar", SC, A_SCEND[qt - 2])
            wait("scalar", V, A_VDONE[qt - 2])
        em("scalar", lambda e, x=xap, oo=bf(A_XSQ, D), a=bf(*small(sb + 0)):
           e.activation(oo, x, AF.Copy, accum_out=a).then_inc(SC.h, 1))
        sc_sum = SC.bump()
        wait("scalar", SC, sc_sum)
        em("scalar", lambda e, x=xap, oo=bf(A_XSQ, D), a=bf(*small(sb + 2)):
           e.activation(oo, x, AF.Square, accum_out=a).then_inc(SC.h, 1))
        sc_sq = SC.bump()
        wait("vector", SC, sc_sum)
        if qt >= 2:
            wait("vector", V, A_VDONE[qt - 2])
        em("vector", lambda e, i=bf(*small(sb + 0)), oo=bf(*small(sb + 1)):
           e.tensor_scalar_mul(oo, i, 1.0 / D).then_inc(V.h, 1))
        v_mean = V.bump()
        wait("vector", V, v_mean)
        em("vector", lambda e, m=bf(*small(sb + 1)), oo=bf(*small(sb + 3)):
           e.tensor_mul(oo, m, m).then_inc(V.h, 1))
        v_m2 = V.bump()
        wait("vector", SC, sc_sq)
        wait("vector", V, v_m2)
        em("vector", lambda e, sq=bf(*small(sb + 2)), m2=bf(*small(sb + 3)), oo=bf(*small(sb + 4)):
           e.scalar_tensor_tensor(oo, sq, 1.0 / D, m2, ALU.mult, ALU.subtract).then_inc(V.h, 1))
        v_var = V.bump()
        wait("scalar", V, v_var)
        if qt == 0:
            wait("scalar", g_ms, EPS_OK)
        em("scalar", lambda e, i=bf(*small(sb + 4)), oo=bf(*small(sb + 5)), b=bf(O_EPS, 1):
           e.activation(oo, i, AF.Ln, bias=b).then_inc(SC.h, 1))
        sc_ln = SC.bump()
        wait("scalar", SC, sc_ln)
        em("scalar", lambda e, i=bf(*small(sb + 5)), oo=bf(*small(sb + 6)):
           e.activation(oo, i, AF.Exp, scale=-0.5).then_inc(SC.h, 1))
        sc_rs = SC.bump()
        wait("vector", SC, sc_rs)
        if qt >= 2:
            wait("vector", TE, A_TTR[qt - 2])
        em("vector", lambda e, x=xap, m=bf(*small(sb + 1)), r=bf(*small(sb + 6)), oo=bb(A_XNB[p2], D):
           e.tensor_scalar(oo, x, m, r, ALU.subtract, ALU.mult).then_inc(V.h, 1))
        v_xn = V.bump()
        A_VDONE[qt] = v_xn
        A_SCEND[qt] = SC.n

        ppr = P[(qt % 2) + (2 if qt >= 4 else 0)]
        pview = ppr[:, 0:(DC * 128) // 2].bitcast(BF16)
        if qt == 0:
            wait("tensor", g_cst, CST_ALL)
        wait("tensor", V, v_xn)
        if qt not in (0, 1, 4, 5):
            wait("tensor", V, A_TCOPY[qt - 2])
        for dc in range(DC):
            if dc == DC - 1:
                em("tensor", lambda e, oo=pview[:, dc * 128:(dc + 1) * 128],
                   i=bb(A_XNB[p2] + dc * 256, 128), idt=ident_bf:
                   e.transpose(oo, i, idt).then_inc(TE.h, 1))
            else:
                em("tensor", lambda e, oo=pview[:, dc * 128:(dc + 1) * 128],
                   i=bb(A_XNB[p2] + dc * 256, 128), idt=ident_bf:
                   e.transpose(oo, i, idt))
        A_TTR[qt] = TE.bump()
        wait("vector", TE, A_TTR[qt])
        dst = xnT.rearrange("p (dc t) -> p dc t", dc=DC)[:, :, qt * 128:(qt + 1) * 128]
        src = pview.rearrange("p (dc i) -> p dc i", dc=DC)
        em("vector", lambda e, oo=dst, i=src: e.tensor_copy(oo, i).then_inc(V.h, 1))
        A_TCOPY[qt] = V.bump()

    A_DONE_V = V.n
    A_DONE_SC = SC.n

    def mm_group(pp, tiles, inc_sem):
        n = len(tiles)
        for i, (po, w, r, st, sp) in enumerate(tiles):
            if i == n - 1:
                em("tensor", lambda e, o=po, ww=w, rr=r, s1=st, s2=sp, h=inc_sem.h:
                   e.matmul(o, ww, rr, start=s1, stop=s2, skip_group_check=True).then_inc(h, 1))
            else:
                em("tensor", lambda e, o=po, ww=w, rr=r, s1=st, s2=sp:
                   e.matmul(o, ww, rr, start=s1, stop=s2, skip_group_check=True))
        return inc_sem.bump()

    def mm_proj2(rhs_off, rhs_nchunks, w_off, nout, dst_writer,
                 pre_tensor_wait=None, psel=None, first_wait=None, f_waits=None):
        first_v = V.n if first_wait is None else first_wait
        psel = psel or [P[0], P[1]]
        evac_v = {}
        for oc in range(nout):
            pp = psel[oc % len(psel)]
            if oc == 0 and pre_tensor_wait is not None:
                pre_tensor_wait()
            if oc < len(psel):
                wait("tensor", V, first_v)
            else:
                wait("tensor", V, evac_v[oc - len(psel)])
            if f_waits is None:
                tiles = []
                for c in range(rhs_nchunks):
                    w = bb(w_off + 2 * (c * nout + oc) * 128, 128)
                    for f in range(NF):
                        tiles.append((pp[:, f * NW:(f + 1) * NW], w,
                                      bb(rhs_off + 2 * (c * T + f * NW), NW),
                                      c == 0, c == rhs_nchunks - 1))
                t_mm = mm_group(pp, tiles, TE)
            else:
                for f in range(NF):
                    if oc == 0 and f in f_waits:
                        wait("tensor", f_waits[f][0], f_waits[f][1])
                    tiles = [(pp[:, f * NW:(f + 1) * NW],
                              bb(w_off + 2 * (c * nout + oc) * 128, 128),
                              bb(rhs_off + 2 * (c * T + f * NW), NW),
                              c == 0, c == rhs_nchunks - 1)
                             for c in range(rhs_nchunks)]
                    t_mm = mm_group(pp, tiles, TE)
            wait("vector", TE, t_mm)
            evac_v[oc] = dst_writer(oc, pp)
        return evac_v

    def evac_bias(dst_off, bias_off, f32out=False, first_wait=None):
        def w(oc, pp):
            if oc == 0 and first_wait is not None:
                wait("vector", first_wait[0], first_wait[1])
            bias = bf(bias_off + 4 * oc, 1)
            if f32out:
                oap = bf(dst_off + 4 * oc * T, T)
            else:
                oap = bb(dst_off + 2 * oc * T, T)
            em("vector", lambda e, oo=oap, i=pp[:, 0:T], b=bias:
               e.tensor_scalar(oo, i, b, None, ALU.add).then_inc(V.h, 1))
            return V.bump()
        return w

    wait("sync", V, A_DONE_V)
    wait("sync", SC, A_DONE_SC)
    dma("sync", g_cs, bf(B_COS, JH * T).rearrange("p (j t) -> p j t", j=JH),
        cos_d.ap().rearrange("(j p) t -> p j t", p=128))
    dma("sync", g_cs, bf(B_SIN, JH * T).rearrange("p (j t) -> p j t", j=JH),
        sin_d.ap().rearrange("(j p) t -> p j t", p=128))
    CS_ALL = g_cs.n

    def loadB(src_ap, te_wait=None):
        wait("gpsimd", TE, TE.n if te_wait is None else te_wait)
        wait("gpsimd", g_wB, g_wB.n)
        nch = src_ap.shape[0] // 128
        return dma("gpsimd", g_wB,
                   bb(B_W, nch * src_ap.shape[1]).rearrange("p (c n) -> p c n", c=nch),
                   src_ap.rearrange("(c p) n -> p c n", p=128))

    if B_W >= A_END:
        ld = loadB(wkvd_d.ap(), te_wait=0)
    else:
        wait("gpsimd", V, A_DONE_V)
        wait("gpsimd", SC, A_DONE_SC)
        ld = loadB(wkvd_d.ap())

    def pre_kv():
        wait("tensor", g_wB, ld)
    mm_proj2(O_XNT, DC, B_W, LC, evac_bias(B_KVL, O_BKV), pre_tensor_wait=pre_kv,
             first_wait=A_TCOPY[QT // NF - 1],
             f_waits={f: (V, A_TCOPY[(f + 1) * (QT // NF) - 1]) for f in range(NF)})

    ldn = loadB(wkn_d.ap())
    mm_proj2(B_KVL, LC, B_W, NC2, evac_bias(B_KT, O_BKN),
             pre_tensor_wait=lambda: wait("tensor", g_wB, ldn))

    ldr = loadB(wkr_d.ap())
    KR_EV = mm_proj2(B_KVL, LC, B_W, 2 * JH, evac_bias(B_KR, O_BKR, f32out=True),
                     pre_tensor_wait=lambda: wait("tensor", g_wB, ldr))

    wait("vector", g_cs, CS_ALL)
    wait("vector", V, KR_EV[2 * JH - 1])
    for jc in range(JH):
        x1 = bf(B_KR + 4 * jc * T, T)
        x2 = bf(B_KR + 4 * (JH + jc) * T, T)
        co = bf(B_COS + 4 * jc * T, T)
        si = bf(B_SIN + 4 * jc * T, T)
        em("vector", lambda e, oo=bf(B_RT1, T), a=x1, b=co: e.tensor_mul(oo, a, b).then_inc(V.h, 1))
        V.bump()
        em("vector", lambda e, oo=bf(B_RT2, T), a=x2, b=si: e.tensor_mul(oo, a, b).then_inc(V.h, 1))
        v2 = V.bump()
        wait("vector", V, v2)
        em("vector", lambda e, oo=bb(B_KT + 2 * (NC2 + jc) * T, T), a=bf(B_RT1, T), b=bf(B_RT2, T):
           e.tensor_sub(oo, a, b).then_inc(V.h, 1))
        v3 = V.bump()
        wait("vector", V, v3)
        em("vector", lambda e, oo=bf(B_RT1, T), a=x1, b=si: e.tensor_mul(oo, a, b).then_inc(V.h, 1))
        V.bump()
        em("vector", lambda e, oo=bf(B_RT2, T), a=x2, b=co: e.tensor_mul(oo, a, b).then_inc(V.h, 1))
        v5 = V.bump()
        wait("vector", V, v5)
        em("vector", lambda e, oo=bb(B_KT + 2 * (NC2 + JH + jc) * T, T), a=bf(B_RT1, T), b=bf(B_RT2, T):
           e.tensor_add(oo, a, b).then_inc(V.h, 1))
        v6 = V.bump()
        if jc < JH - 1:
            wait("vector", V, v6)
    KT_DONE_V = V.n

    wait("sync", V, KT_DONE_V)
    for q in range(NQ):
        kst = dma("sync", s_kst, kT_locQ[q].ap().rearrange("(dc p) t -> p dc t", p=128),
                  bb(B_KT + q * (DC // NQ) * 2 * T, (DC // NQ) * T)
                  .rearrange("p (dc t) -> p dc t", dc=DC // NQ))
    KST_ALL = kst

    ldq = loadB(wqd_d.ap())
    QD_EV = mm_proj2(O_XNT, DC, B_W, LC,
                     evac_bias(O_QL, O_BQD, first_wait=(V, KT_DONE_V)),
                     pre_tensor_wait=lambda: wait("tensor", g_wB, ldq),
                     first_wait=KR_EV[2 * JH - 1])
    QD_TE = TE.n

    wait("gpsimd", V, KT_DONE_V)
    wait("gpsimd", g_wB, g_wB.n)
    ldv = dma("gpsimd", g_wB, bb(C_WG, LC * D).rearrange("p (c n) -> p c n", c=LC),
              wvu_d.ap().rearrange("(c p) n -> p c n", p=128))
    wait("gpsimd", g_wB, g_wB.n)
    ldqu = dma("gpsimd", g_wB, bb(C_WU, LC * D).rearrange("p (c n) -> p c n", c=LC),
               wqu_d.ap().rearrange("(c p) n -> p c n", p=128))

    KSQ_SC = {}
    KN_MM = {}
    KN_V = {}
    for h in range(NH):
        half = h // 8
        if h % 8 == 0:
            wait("scalar", V, KT_DONE_V)
            if half == 1:
                wait("scalar", TE, KN_MM[7])
            em("scalar", lambda e, oo=bb(O_NSQ, 8 * T), i=bb(B_KT + 2 * half * 8 * T, 8 * T):
               e.activation(oo, i, AF.Square).then_inc(SC.h, 1))
            KSQ_SC[half] = SC.bump()
        if h == 0:
            wait("tensor", g_ms, ONES_OK)
        wait("tensor", SC, KSQ_SC[half])
        if h >= 2:
            wait("tensor", V, KN_V[h - 2])
        pp = P[2 + (h % 2)]
        tiles = [(pp[:, f * NW:(f + 1) * NW], ones_bf,
                  bb(O_NSQ + 2 * ((h % 8) * T + f * NW), NW), True, True)
                 for f in range(NF)]
        KN_MM[h] = mm_group(pp, tiles, TE)
        wait("vector", TE, KN_MM[h])
        em("vector", lambda e, oo=bf(O_KN2 + 4 * h, 1), i=pp[:, 0:T]:
           e.reduce_max(oo, i, axis=AXX).then_inc(V.h, 1))
        KN_V[h] = V.bump()
    KN_DONE_V = V.n

    wait("sync", V, KN_DONE_V)
    knst = dma("sync", s_kn2, kn_loc[0:1, :], bf(O_KN2, NH, parts=1))

    vsb = bb(B_VSB, QT * D)
    VT_COPY = {}
    VT_TR = {}

    def evac_v_writer(dc, pp):
        pr = dc % 2
        bias = bf(O_BVU + 4 * dc, 1)
        if dc >= 2:
            wait("vector", TE, VT_TR[dc - 2])
        em("vector", lambda e, oo=bb(B_VT[pr], T), i=pp[:, 0:T], b=bias:
           e.tensor_scalar(oo, i, b, None, ALU.add).then_inc(V.h, 1))
        v_e = V.bump()
        wait("tensor", V, v_e)
        ptr = P[2 + pr]
        pview = ptr[:, 0:(QT * 128) // 2].bitcast(BF16)
        if dc < 2:
            wait("tensor", V, KN_V[NH - 2 + dc])
        else:
            wait("tensor", V, VT_COPY[dc - 2])
        for qt in range(QT):
            if qt == QT - 1:
                em("tensor", lambda e, oo=pview[:, qt * 128:(qt + 1) * 128],
                   i=bb(B_VT[pr] + 2 * qt * 128, 128), idt=ident_bf:
                   e.transpose(oo, i, idt).then_inc(TE.h, 1))
            else:
                em("tensor", lambda e, oo=pview[:, qt * 128:(qt + 1) * 128],
                   i=bb(B_VT[pr] + 2 * qt * 128, 128), idt=ident_bf:
                   e.transpose(oo, i, idt))
        VT_TR[dc] = TE.bump()
        wait("vector", TE, VT_TR[dc])
        dst = vsb.rearrange("p (qt d) -> p qt d", qt=QT)[:, :, dc * 128:(dc + 1) * 128]
        src = pview.rearrange("p (qt i) -> p qt i", qt=QT)
        em("vector", lambda e, oo=dst, i=src: e.tensor_copy(oo, i).then_inc(V.h, 1))
        VT_COPY[dc] = V.bump()
        return VT_COPY[dc]

    def pre_v():
        wait("tensor", g_wB, ldv)
        wait("tensor", V, KT_DONE_V)
    mm_proj2(B_KVL, LC, C_WG, DC, evac_v_writer, pre_tensor_wait=pre_v,
             first_wait=QD_EV[LC - 1])
    V_TE = TE.n

    wait("sync", V, VT_COPY[DC - 1])
    for q in range(NQ):
        vst = dma("sync", s_vst, v_locQ[q].ap().rearrange("(qt p) d -> p qt d", p=128),
                  vsb.rearrange("p (qt d) -> p qt d", qt=QT)[:, :, q * (D // NQ):(q + 1) * (D // NQ)])
    VST_ALL = vst

    QT_EV = mm_proj2(O_QL, LC, C_WU, DC, evac_bias(O_QTT, O_BQU),
                     pre_tensor_wait=lambda: wait("tensor", g_wB, ldqu))
    QU_TE = TE.n

    sgroups = [(0, 1), (1, 1), (2, 2)] + [(g0 + 4, gs) for g0, gs in _groups(cfg.ISC - 4, cfg.IG)]
    rgroups = _groups(cfg.IRC, cfg.IG)
    allgroups = [("s", g0, gs) for g0, gs in sgroups] + [("r", g0, gs) for g0, gs in rgroups]
    g0_gs = allgroups[0][2]
    W0_G, W0_U, W0_D = B_W, B_KT, C_Y + 4 * 15 * T
    wait("gpsimd", TE, QD_TE)
    wait("gpsimd", g_slab[0], g_slab[0].n)
    lgld0 = dma("gpsimd", g_slab[0],
                bb(W0_G, DC * g0_gs * 128).rearrange("p (dc i) -> p dc i", dc=DC),
                wg_d.ap()[:, 0:g0_gs * 128].rearrange("(dc p) i -> p dc i", p=128))
    wait("gpsimd", SC, KSQ_SC[1])
    wait("gpsimd", s_kst, KST_ALL)
    wait("gpsimd", g_slab[1], g_slab[1].n)
    luld0 = dma("gpsimd", g_slab[1],
                bb(W0_U, DC * g0_gs * 128).rearrange("p (dc i) -> p dc i", dc=DC),
                wu_d.ap()[:, 0:g0_gs * 128].rearrange("(dc p) i -> p dc i", p=128))
    wait("gpsimd", s_vst, VST_ALL)
    wait("gpsimd", g_slab[2], g_slab[2].n)
    ldld0 = dma("gpsimd", g_slab[2],
                bb(W0_D, g0_gs * D).rearrange("p (il c) -> p il c", il=g0_gs),
                wd_d.ap()[0:g0_gs * 128, :].rearrange("(il p) c -> p il c", p=128))

    CC_K = {}
    CC_V = {}

    def emit_cc_k(q):
        wait("gpsimd", s_kst, KST_ALL)
        em("gpsimd", lambda e: e.collective_compute(
            "AllGather", ALU.bypass, replica_groups=[list(range(C))],
            ins=[kT_locQ[q].ap().opt()], outs=[kT_gatQ[q].ap().opt()]).then_inc(g_cc.h, 1))
        CC_K[q] = g_cc.bump()

    def emit_cc_v(q):
        wait("gpsimd", s_vst, VST_ALL)
        em("gpsimd", lambda e: e.collective_compute(
            "AllGather", ALU.bypass, replica_groups=[list(range(C))],
            ins=[v_locQ[q].ap().opt()], outs=[v_gatQ[q].ap().opt()]).then_inc(g_cc.h, 1))
        CC_V[q] = g_cc.bump()

    emit_cc_k(0)
    wait("gpsimd", s_kn2, knst)
    em("gpsimd", lambda e: e.collective_compute(
        "AllReduce", ALU.max, replica_groups=[list(range(C))],
        ins=[kn_loc.ap().opt()], outs=[kn_gat.ap().opt()]).then_inc(g_cc.h, 1))
    CC_KN = g_cc.bump()

    QSQ_SC = {}
    QN_MM = {}
    QN_V = {}
    for h in range(NH):
        half = h // 8
        if h % 8 == 0:
            wait("scalar", V, QT_EV[DC - 1])
            wait("scalar", s_vst, VST_ALL)
            wait("scalar", TE, KN_MM[15] if half == 0 else QN_MM[7])
            em("scalar", lambda e, oo=bb(O_NSQ, 8 * T), i=bb(O_QTT + 2 * half * 8 * T, 8 * T):
               e.activation(oo, i, AF.Square).then_inc(SC.h, 1))
            QSQ_SC[half] = SC.bump()
        wait("tensor", SC, QSQ_SC[half])
        if h < 2:
            wait("tensor", V, VT_COPY[DC - 2 + h])
        else:
            wait("tensor", V, QN_V[h - 2])
        pp = P[2 + (h % 2)]
        tiles = [(pp[:, f * NW:(f + 1) * NW], ones_bf,
                  bb(O_NSQ + 2 * ((h % 8) * T + f * NW), NW), True, True)
                 for f in range(NF)]
        QN_MM[h] = mm_group(pp, tiles, TE)
        wait("vector", TE, QN_MM[h])
        em("vector", lambda e, oo=bf(O_QN2 + 4 * h, 1), i=pp[:, 0:T]:
           e.reduce_max(oo, i, axis=AXX).then_inc(V.h, 1))
        QN_V[h] = V.bump()
    QN_DONE_V = V.n

    wait("tensor", V, QN_V[NH - 1])
    lg_tiles = []
    for qt in range(QT):
        for dc in range(DC):
            lg_tiles.append((P[3][:, qt * NE:(qt + 1) * NE],
                             bb(O_XNT + 2 * (dc * T + qt * 128), 128),
                             bb(O_RTR + 2 * dc * NE, NE),
                             dc == 0, dc == DC - 1))
    t_lg = mm_group(None, lg_tiles, TE)
    wait("vector", TE, t_lg)
    em("vector", lambda e, oo=bf(O_LGS, NE * QT), i=P[3][:, 0:NE * QT]:
       e.tensor_copy(oo, i).then_inc(V.h, 1))
    v_lg = V.bump()
    wait("vector", V, v_lg)
    em("vector", lambda e, oo=bf(O_M1, QT),
       i=bf(O_LGS, NE * QT).rearrange("p (q n) -> p q n", q=QT):
       e.reduce_max(oo, i, axis=AXX).then_inc(V.h, 1))
    v_m1 = V.bump()
    wait("vector", V, v_m1)
    for qt in range(QT):
        em("vector", lambda e, oo=bf(O_MSK + 4 * qt * NE, NE), i=bf(O_LGS + 4 * qt * NE, NE),
           m=bf(O_M1 + 4 * qt, 1):
           e.tensor_scalar(oo, i, m, None, ALU.is_ge).then_inc(V.h, 1))
        V.bump()
    v_eq = V.n
    wait("vector", V, v_eq)
    for qt in range(QT):
        em("vector", lambda e, oo=bf(O_MSK + 4 * qt * NE, NE), i=bf(O_MSK + 4 * qt * NE, NE),
           l=bf(O_LGS + 4 * qt * NE, NE):
           e.scalar_tensor_tensor(oo, i, -1e30, l, ALU.mult, ALU.add).then_inc(V.h, 1))
        V.bump()
    v_mk = V.n
    wait("vector", V, v_mk)
    em("vector", lambda e, oo=bf(O_M2, QT),
       i=bf(O_MSK, NE * QT).rearrange("p (q n) -> p q n", q=QT):
       e.reduce_max(oo, i, axis=AXX).then_inc(V.h, 1))
    v_m2 = V.bump()
    wait("vector", V, v_m2)
    em("vector", lambda e, oo=bf(O_M2, QT), a=bf(O_M2, QT), b=bf(O_M1, QT):
       e.tensor_sub(oo, a, b).then_inc(V.h, 1))
    v_d = V.bump()
    wait("scalar", V, v_d)
    em("scalar", lambda e, oo=bf(O_M2, QT), i=bf(O_M2, QT):
       e.activation(oo, i, AF.Exp).then_inc(SC.h, 1))
    sc_t = SC.bump()
    wait("vector", SC, sc_t)
    em("vector", lambda e, oo=bf(O_M1, QT), i=bf(O_M2, QT):
       e.tensor_scalar_add(oo, i, 1.0).then_inc(V.h, 1))
    v_o = V.bump()
    wait("vector", V, v_o)
    em("vector", lambda e, oo=bf(O_M2, QT), i=bf(O_M1, QT):
       e.reciprocal(oo, i).then_inc(V.h, 1))
    v_r = V.bump()
    wait("vector", V, v_r)
    em("vector", lambda e, oo=bf(O_COMBT, QT), a=bf(O_M1, QT), b=bf(O_M2, QT):
       e.tensor_mul(oo, a, b).then_inc(V.h, 1))
    v_combtok = V.bump()
    wait("tensor", V, v_combtok)
    em("tensor", lambda e, oo=P[2][0:max(QT, 1), 0:128], i=bf(O_COMBT, QT), idt=ident_f32:
       e.transpose(oo, i, idt).then_inc(TE.h, 1))
    t_ct = TE.bump()
    wait("vector", TE, t_ct)
    em("vector", lambda e, oo=bf(O_COMBT, 128, parts=QT), i=P[2][0:QT, 0:128]:
       e.tensor_copy(oo, i).then_inc(V.h, 1))
    v_ctc = V.bump()
    wait("sync", V, v_ctc)
    cbst = dma("sync", s_cd2, comb_dr[:, :], bf(O_COMBT, 128, parts=QT))

    yacc = bf(C_Y, DC * T)
    NSL = (DC + 1) // 2

    def load_wo(sidx):
        pr = sidx % 2
        ncc = min(2, DC - sidx * 2)
        wait("gpsimd", g_wo[pr], g_wo[pr].n)
        dma("gpsimd", g_wo[pr],
            bb(F_WO[pr], DC * ncc * 128).rearrange("p (dc c) -> p dc c", dc=DC),
            wo_d.ap()[:, sidx * 256:sidx * 256 + ncc * 128]
            .rearrange("(dc p) c -> p dc c", p=128))

    C_SILU = {}
    C_HM = {}
    C_YACC_V = {}
    C_TDOWN = {}
    C_GLAST = {}
    C_ULAST = {}

    for gi, (kind, g0, gs) in enumerate(allgroups):
        gsrc = (wg_d, wu_d, wd_d) if kind == "s" else (rg_d, ru_d, rd_d)
        if gi == 0:
            lgld, luld, ldld = lgld0, luld0, ldld0
        else:
            wait("gpsimd", TE, C_GLAST[gi - 1])
            wait("gpsimd", g_slab[0], g_slab[0].n)
            lgld = dma("gpsimd", g_slab[0],
                       bb(C_WG, DC * gs * 128).rearrange("p (dc i) -> p dc i", dc=DC),
                       gsrc[0].ap()[:, g0 * 128:(g0 + gs) * 128].rearrange("(dc p) i -> p dc i", p=128))
            wait("gpsimd", TE, C_ULAST[gi - 1])
            wait("gpsimd", g_slab[1], g_slab[1].n)
            luld = dma("gpsimd", g_slab[1],
                       bb(C_WU, DC * gs * 128).rearrange("p (dc i) -> p dc i", dc=DC),
                       gsrc[1].ap()[:, g0 * 128:(g0 + gs) * 128].rearrange("(dc p) i -> p dc i", p=128))
            wait("gpsimd", TE, C_TDOWN[gi - 1])
            wait("gpsimd", g_slab[2], g_slab[2].n)
            ldld = dma("gpsimd", g_slab[2],
                       bb(C_WD, gs * D).rearrange("p (il c) -> p il c", il=gs),
                       gsrc[2].ap()[g0 * 128:(g0 + gs) * 128, :].rearrange("(il p) c -> p il c", p=128))
        if gi == 2:
            emit_cc_v(0)
        if gi == 4:
            emit_cc_k(1)
        if gi == 6:
            emit_cc_v(1)
        if gi == 8:
            emit_cc_k(2)
        if gi == 11:
            emit_cc_v(2)
        if gi == 13:
            emit_cc_k(3)
        if gi == 15:
            emit_cc_v(3)
        if gi == 14:
            em("gpsimd", lambda e: e.memset(bf(O_ONESF, 128), 1.0).then_inc(g_ms.h, 1))
            ONESF_OK = g_ms.bump()
        if gi == 10:
            wait("gpsimd", g_cc, CC_KN)
            knld = dma("gpsimd", g_kn, bf(O_KNG, NH, parts=1), kn_gat[0:1, :])
        if gi == 8:
            wait("gpsimd", s_cd2, cbst)
            cbld = dma("gpsimd", g_cb, bb(O_COMB, T), bass.AP(comb_dr, 0, [[0, 128], [1, T]]))
        if gi == 12:
            wait("vector", g_kn, knld)
            em("vector", lambda e, oo=bf(O_CROW, NH, parts=1), a=bf(O_QN2, NH, parts=1),
               b=bf(O_KNG, NH, parts=1):
               e.tensor_mul(oo, a, b).then_inc(V.h, 1))
            v_c1 = V.bump()
            wait("scalar", V, v_c1)
            em("scalar", lambda e, oo=bf(O_CROW2, NH, parts=1), i=bf(O_CROW, NH, parts=1):
               e.activation(oo, i, AF.Sqrt).then_inc(SC.h, 1))
            sc_cs = SC.bump()
            wait("vector", SC, sc_cs)
            em("vector", lambda e, oo=bf(O_CROW, NH, parts=1), i=bf(O_CROW2, NH, parts=1):
               e.tensor_scalar_mul(oo, i, -1.05 * scale).then_inc(V.h, 1))
            v_cn = V.bump()
            wait("sync", V, v_cn)
            cst2 = dma("sync", s_cd1, cneg_dr[0:1, :], bf(O_CROW, NH, parts=1))
        for il in range(gs):
            if gi == 0 and il == 0:
                wait("tensor", V, QT_EV[DC - 1])
            wait("tensor", g_slab[0], lgld)
            prev_silu = C_SILU.get((gi, il - 1)) or (C_SILU.get((gi - 1, allgroups[gi - 1][2] - 1)) if gi > 0 else 0)
            wait("tensor", SC, prev_silu or 0)
            tiles = []
            for dc in range(DC):
                w = bb((W0_G if gi == 0 else C_WG) + 2 * (dc * gs * 128 + il * 128), 128)
                for f in range(NF):
                    tiles.append((P[0][:, f * NW:(f + 1) * NW], w,
                                  bb(O_XNT + 2 * (dc * T + f * NW), NW),
                                  dc == 0, dc == DC - 1))
            t_g = mm_group(None, tiles, TE)
            wait("tensor", g_slab[1], luld)
            prev_hm = C_HM.get((gi, il - 1)) or (C_HM.get((gi - 1, allgroups[gi - 1][2] - 1)) if gi > 0 else 0)
            wait("tensor", V, prev_hm or 0)
            tiles = []
            for dc in range(DC):
                w = bb((W0_U if gi == 0 else C_WU) + 2 * (dc * gs * 128 + il * 128), 128)
                for f in range(NF):
                    tiles.append((P[1][:, f * NW:(f + 1) * NW], w,
                                  bb(O_XNT + 2 * (dc * T + f * NW), NW),
                                  dc == 0, dc == DC - 1))
            t_u = mm_group(None, tiles, TE)
            C_GLAST[gi] = t_g
            C_ULAST[gi] = t_u
            wait("scalar", TE, t_g)
            wait("scalar", V, prev_hm or 0)
            em("scalar", lambda e, oo=bb(C_SG, T), i=P[0][:, 0:T]:
               e.activation(oo, i, AF.Silu).then_inc(SC.h, 1))
            C_SILU[gi, il] = SC.bump()
            wait("vector", SC, C_SILU[gi, il])
            wait("vector", TE, t_u)
            if gi >= 2:
                wait("vector", TE, C_TDOWN[gi - 2])
            if kind == "s":
                em("vector", lambda e, oo=bb(C_H[gi % 2] + 2 * il * T, T), a=bb(C_SG, T),
                   b=P[1][:, 0:T]: e.tensor_mul(oo, a, b).then_inc(V.h, 1))
                C_HM[gi, il] = V.bump()
            else:
                em("vector", lambda e, oo=bb(C_T2, T), a=bb(C_SG, T), b=P[1][:, 0:T]:
                   e.tensor_mul(oo, a, b).then_inc(V.h, 1))
                v_t2 = V.bump()
                wait("vector", V, v_t2)
                wait("vector", g_cb, cbld)
                em("vector", lambda e, oo=bb(C_H[gi % 2] + 2 * il * T, T), a=bb(C_T2, T),
                   b=bb(O_COMB, T): e.tensor_mul(oo, a, b).then_inc(V.h, 1))
                C_HM[gi, il] = V.bump()
        wait("tensor", g_slab[2], ldld)
        wait("tensor", V, C_HM[gi, gs - 1])
        for cc in range(DC):
            pp = P[2 + (cc % 2)]
            if cc >= 2:
                wait("tensor", V, C_YACC_V[gi, cc - 2])
            elif gi > 0:
                wait("tensor", V, C_YACC_V[gi - 1, DC - 2 + cc])
            else:
                wait("tensor", V, v_ctc)
            tiles = []
            for il in range(gs):
                w = bb((W0_D if gi == 0 else C_WD) + 2 * (il * D + cc * 128), 128)
                for f in range(NF):
                    tiles.append((pp[:, f * NW:(f + 1) * NW], w,
                                  bb(C_H[gi % 2] + 2 * (il * T + f * NW), NW),
                                  il == 0, il == gs - 1))
            t_d = mm_group(None, tiles, TE)
            wait("vector", TE, t_d)
            if gi == 0:
                if cc == 0:
                    wait("vector", s_kst, KST_ALL)
                    wait("vector", s_vst, VST_ALL)
                em("vector", lambda e, oo=bf(C_Y + 4 * cc * T, T), i=pp[:, 0:T]:
                   e.tensor_copy(oo, i).then_inc(V.h, 1))
            else:
                wait("vector", V, C_YACC_V[gi - 1, cc])
                em("vector", lambda e, oo=bf(C_Y + 4 * cc * T, T), i=pp[:, 0:T],
                   a=bf(C_Y + 4 * cc * T, T):
                   e.tensor_add(oo, a, i).then_inc(V.h, 1))
            C_YACC_V[gi, cc] = V.bump()
        C_TDOWN[gi] = TE.n

    FFN_LAST_V = C_YACC_V[len(allgroups) - 1, DC - 1]
    C_END_SC = SC.n

    wait("gpsimd", s_cd1, cst2)
    cngld = dma("gpsimd", g_kn, bf(O_CNEG, NH),
                bass.AP(cneg_dr, 0, [[0, 128], [1, NH]]))

    wait("gpsimd", TE, C_TDOWN[len(allgroups) - 1])
    load_wo(0)

    LG = len(allgroups) - 1
    wait("scalar", V, C_YACC_V[LG, 3])
    ysp_a = dma("scalar", s_ysp[0], yffn_dr[:, 0:4 * T], bf(C_Y, 4 * T))
    wait("scalar", V, C_YACC_V[LG, 9])
    ysp_b = dma("scalar", s_ysp[1], yffn_dr[:, 4 * T:10 * T], bf(C_Y + 16 * T, 6 * T))
    wait("scalar", V, C_YACC_V[LG, 15])
    ysp_c = dma("scalar", s_ysp[2], yffn_dr[:, 10 * T:16 * T], bf(C_Y + 40 * T, 6 * T))

    for h in range(NH):
        pr = h % 2
        vpr = (h + 1) % 2
        if h == 0:
            wait("sync", TE, C_GLAST[LG])
            wait("sync", g_cc, CC_V[0])
        if h % 4 == 0 and h > 0:
            wait("sync", g_cc, CC_V[h // 4])
        if h >= 2:
            wait("sync", pvsem, (h - 1) * KC)
        ktg = kT_gatQ[h // 4]
        vg = v_gatQ[h // 4]
        hh = h % 4
        if h == 1:
            wait("sync", TE, C_ULAST[LG])
        for r in range(C):
            dma("sync", s_kv[pr], bb(E_KH[pr] + 2 * r * T, T),
                ktg[r * (D // NQ) + hh * 128:r * (D // NQ) + (hh + 1) * 128, :])
        if h == 0:
            wait("sync", s_ysp[0], ysp_a)
        if h == 1:
            wait("sync", TE, C_TDOWN[LG])
        for r in range(C):
            dma("sync", s_kv[pr],
                bb(E_VH[vpr] + 2 * r * T, T).rearrange("p (kc d) -> p kc d", d=128),
                vg[r * T:(r + 1) * T, hh * 128:(hh + 1) * 128]
                .rearrange("(kc p) d -> p kc d", p=128))

    slot_wait = [(expsem, 0), (expsem, 0), (expsem, 0)]
    eslot = [(dacc, 0)] * 4
    pb_rd = [(dacc, 0), (pacc, 0)]
    FIN_V = {}
    COMB_V = {}
    PACC_AT = {}
    DACC_AT = {}

    def emit_combine(fh):
        wait("vector", pacc, PACC_AT[fh])
        wait("vector", dacc, DACC_AT[fh])
        em("vector", lambda e, oo=bf(E_EA, T), a=bf(E_EA, T), b=bf(E_EA2, T):
           e.tensor_tensor(oo, a, b, ALU.add).then_inc(V.h, 1))
        COMB_V[fh] = V.bump()

    DEN_T = {}
    DEN_SLOT = {}

    def emit_den(fh, slot):
        wsem, wval = slot_wait[slot]
        wait("tensor", wsem, wval)
        wait("tensor", V, COMB_V[fh])
        ppD = P[slot]
        tiles = [(ppD[:, f * NW:(f + 1) * NW], ones_f32,
                  bf(E_EA + 4 * f * NW, NW), True, True) for f in range(NF)]
        DEN_T[fh] = mm_group(None, tiles, TE)
        DEN_SLOT[fh] = slot

    def emit_rcp(fh):
        slot = DEN_SLOT[fh]
        wait("scalar", TE, DEN_T[fh])
        if fh >= 1:
            wait("scalar", V, FIN_V[fh - 1])
        em("scalar", lambda e, oo=bf(O_RCPA, T), i=P[slot][:, 0:T]:
           e.activation(oo, i, AF.Ln).then_inc(SC.h, 1))
        sc_ln = SC.bump()
        slot_wait[slot] = (SC, sc_ln)
        wait("scalar", SC, sc_ln)
        em("scalar", lambda e, oo=bf(O_RCPB, T), i=bf(O_RCPA, T):
           e.activation(oo, i, AF.Exp, scale=-1.0).then_inc(SC.h, 1))
        sc_rcp = SC.bump()
        wait("vector", SC, sc_rcp)
        wait("vector", vfin, fh + 1)
        em("vector", lambda e, oo=bb(E_AO + 2 * fh * T, T), a=bb(E_AO + 2 * fh * T, T),
           b=bf(O_RCPB, T): e.tensor_mul(oo, a, b).then_inc(V.h, 1))
        FIN_V[fh] = V.bump()

    for h in range(NH):
        pr = h % 2
        g_end_prev = h * KC
        kv_target = 16 * (2 * C) * (h // 2 + 1)
        wait("tensor", s_kv[pr], kv_target)
        if h == 0:
            wait("tensor", V, FFN_LAST_V)
            wait("tensor", SC, C_END_SC)
            wait("tensor", g_ms, ONESF_OK)
            wait("scalar", g_kn, cngld)
            wait("scalar", s_ysp[1], ysp_b)
            wait("vector", s_ysp[1], ysp_b)
            wait("gpsimd", s_ysp[1], ysp_b)
        for kc in range(KC):
            g = h * KC + kc
            wsem, wval = slot_wait[g % 3]
            wait("tensor", wsem, wval)
            pp = P[g % 3]
            tiles = [(pp[:, f * NW:(f + 1) * NW],
                      bb(E_KH[pr] + 2 * kc * 128, 128),
                      bb(O_QTT + 2 * (h * T + f * NW), NW), True, True)
                     for f in range(NF)]
            mm_group(None, tiles, scsem)
            slot_wait[g % 3] = (expsem, g + 1)
            wait("scalar", scsem, g + 1)
            wait("scalar", pvsem, g - 3)
            if g >= 4:
                esem, ev = eslot[g % 4]
                wait("scalar", esem, ev)
            em("scalar", lambda e, oo=bb(E_E[g % 4], T), i=pp[:, 0:T],
               b=bf(O_CNEG + 4 * h, 1), ssc=scale:
               e.activation(oo, i, AF.Exp, bias=b, scale=ssc).then_inc(expsem.h, 1))
            expsem.bump()
            if h >= 1 and kc == 1:
                emit_combine(h - 1)
            if h >= 1 and kc == 3:
                emit_den(h - 1, g_end_prev % 3)
            if h >= 1 and kc == 5:
                emit_rcp(h - 1)
            if kc % 2 == 1:
                j = kc // 2
                on_pool = j % 4 == 0
                wait("vector", expsem, g + 1)
                if on_pool and j >= 4:
                    wait("vector", pacc, pb_rd[0][1])
                if not on_pool and j >= 2:
                    wait("vector", dacc, pb_rd[1][1])
                em("vector", lambda e, oo=bb(E_PB[0 if on_pool else 1], T),
                   a=bb(E_E[(g - 1) % 4], T),
                   bbb=bb(E_E[g % 4], T), hh=dacc.h:
                   e.tensor_tensor(oo, a, bbb, ALU.add).then_inc(hh, 1))
                pj = dacc.bump()
                eslot[(g - 1) % 4] = (dacc, pj)
                eslot[g % 4] = (dacc, pj)
                if not on_pool:
                    wait("vector", dacc, pj)
                    if j == 1:
                        if h > 0:
                            wait("vector", TE, DEN_T[h - 1])
                        em("vector", lambda e, oo=bf(E_EA, T), i=bb(E_PB[1], T), hh=dacc.h:
                           e.tensor_copy(oo, i).then_inc(hh, 1))
                    else:
                        em("vector", lambda e, oo=bf(E_EA, T), a=bf(E_EA, T),
                           i=bb(E_PB[1], T), hh=dacc.h:
                           e.tensor_tensor(oo, a, i, ALU.add).then_inc(hh, 1))
                    pb_rd[1] = (dacc, dacc.bump())
                else:
                    wait("gpsimd", dacc, pj)
                    if j == 0:
                        if h > 0:
                            wait("gpsimd", V, COMB_V[h - 1])
                        em("gpsimd", lambda e, oo=bf(E_EA2, T), i=bb(E_PB[0], T), hh=pacc.h:
                           e.tensor_copy(oo, i).then_inc(hh, 1))
                    else:
                        em("gpsimd", lambda e, oo=bf(E_EA2, T), a=bf(E_EA2, T),
                           i=bb(E_PB[0], T), hh=pacc.h:
                           e.tensor_tensor(oo, a, i, ALU.add).then_inc(hh, 1))
                    pb_rd[0] = (pacc, pacc.bump())
            def emit_pv(gg, kk):
                wait("tensor", expsem, gg + 1)
                if kk == 0:
                    wait("tensor", vfin, h)
                tiles = [(P[3][:, f * NW:(f + 1) * NW],
                          bb(E_VH[(h + 1) % 2] + 2 * kk * 128, 128),
                          bb(E_E[gg % 4] + 2 * f * NW, NW),
                          kk == 0, kk == KC - 1) for f in range(NF)]
                mm_group(None, tiles, pvsem)
            if kc >= 2:
                emit_pv(g - 2, kc - 2)
        emit_pv(h * KC + KC - 2, KC - 2)
        emit_pv(h * KC + KC - 1, KC - 1)
        PACC_AT[h] = pacc.n
        DACC_AT[h] = dacc.n
        wait("vector", pvsem, (h + 1) * KC)
        if h == 0:
            wait("vector", s_ysp[2], ysp_c)
        em("vector", lambda e, oo=bb(E_AO + 2 * h * T, T), i=P[3][:, 0:T]:
           e.tensor_copy(oo, i).then_inc(vfin.h, 1))
        vfin.bump()
        if h == NH - 1:
            emit_combine(h)
            emit_den(h, ((h + 1) * KC) % 3)
            emit_rcp(h)
    ATT_DONE = NH * KC

    wait("gpsimd", V, FIN_V[NH - 1])
    load_wo(1)

    F_EV = {}
    XLD = {}
    PADD_V = {}
    for cc in range(DC):
        sidx = cc // 2
        pr = sidx % 2
        if cc == 0:
            wait("sync", pvsem, ATT_DONE)
            wait("sync", V, FIN_V[NH - 1])
            wait("sync", s_ysp[2], ysp_c)
        if cc >= 2:
            wait("sync", V, F_EV[cc - 2])
        ybl = dma("sync", s_yb[cc % 2], bf(F_YB[cc % 2], T), yffn_dr[:, cc * T:(cc + 1) * T])
        if cc >= 2:
            wait("scalar", V, PADD_V[cc - 2])
        if cc >= 4:
            wait("scalar", V, F_EV[cc - 4])
        XLD[cc] = dma("scalar", s_xr[cc % 2], bf(F_XR[cc % 2] + 4 * ((cc // 2) % 2) * T, T),
                      xT_d[cc * 128:(cc + 1) * 128, :])
        wait("tensor", g_wo[pr], 16 * (sidx // 2 + 1))
        if cc == 0:
            wait("tensor", V, FIN_V[NH - 1])
        if cc >= 2:
            wait("tensor", V, F_EV[cc - 2])
        pp = P[2 + (cc % 2)]
        tiles = []
        for dc in range(DC):
            w = bb(F_WO[pr] + 2 * (dc * 2 * 128 + (cc - sidx * 2) * 128), 128)
            for f in range(NF):
                tiles.append((pp[:, f * NW:(f + 1) * NW], w,
                              bb(E_AO + 2 * (dc * T + f * NW), NW),
                              dc == 0, dc == DC - 1))
        t_wo = mm_group(None, tiles, TE)
        if (cc % 2 == 1 or cc == DC - 1) and sidx + 2 < NSL:
            wait("gpsimd", TE, t_wo)
            load_wo(sidx + 2)
        wait("vector", s_yb[cc % 2], ybl)
        wait("vector", s_xr[cc % 2], XLD[cc])
        em("vector", lambda e, oo=bf(F_YB[cc % 2], T), a=bf(F_YB[cc % 2], T),
           b=bf(F_XR[cc % 2] + 4 * ((cc // 2) % 2) * T, T):
           e.tensor_add(oo, a, b).then_inc(V.h, 1))
        PADD_V[cc] = V.bump()
        wait("vector", TE, t_wo)
        wait("vector", V, PADD_V[cc])
        if cc >= 4:
            wait("vector", s_out[cc % 4], 16 * (cc // 4))
        em("vector", lambda e, oo=bf(F_OS[cc % 2] + 4 * ((cc // 2) % 2) * T, T),
           i=pp[:, 0:T], b=bf(O_BO + 4 * cc, 1), y=bf(F_YB[cc % 2], T):
           e.scalar_tensor_tensor(oo, i, b, y, ALU.add, ALU.add).then_inc(V.h, 1))
        F_EV[cc] = V.bump()
        wait("sync", V, F_EV[cc])
        dma("sync", s_out[cc % 4], out_d[cc * 128:(cc + 1) * 128, :],
            bf(F_OS[cc % 2] + 4 * ((cc // 2) % 2) * T, T))

    for so in s_out:
        wait("sync", so, so.n)

    for s, v in max_waited.items():
        assert v <= s.n, f"wait beyond final count on sem {s.name}: waited {v}, final {s.n}"

    with nc.Block() as block:
        @block.sync
        def _(e):
            for f in ops["sync"]:
                f(e)

        @block.gpsimd
        def _(e):
            for f in ops["gpsimd"]:
                f(e)

        @block.tensor
        def _(e):
            for f in ops["tensor"]:
                f(e)

        @block.scalar
        def _(e):
            for f in ops["scalar"]:
                f(e)

        @block.vector
        def _(e):
            for f in ops["vector"]:
                f(e)

    return nc


def _host_inputs(cfg, inputs):
    S, D, T, C = cfg.S, cfg.D, cfg.T, cfg.C
    f32 = np.float32

    def t2(a):
        return np.ascontiguousarray(np.asarray(a, dtype=f32).T.astype(BF))

    def b2(b, nch):
        return np.ascontiguousarray(np.asarray(b, dtype=f32).reshape(nch, 128).T)

    x = np.ascontiguousarray(np.asarray(inputs["L0_attn_hidden_states"], dtype=f32).reshape(S, D))

    shared = dict(
        wqdT=t2(inputs["L0_attn_Wq_down"]), bqd2=b2(inputs["L0_attn_bq_down"], cfg.LC),
        wkvdT=t2(inputs["L0_attn_Wkv_down"]), bkv2=b2(inputs["L0_attn_bkv_down"], cfg.LC),
        wquT=t2(inputs["L0_attn_Wq_up"]), bqu2=b2(inputs["L0_attn_bq_up"], cfg.DC),
        wknT=t2(inputs["L0_attn_Wk_nope"]), bkn2=b2(inputs["L0_attn_bk_nope"], cfg.NC2),
        wkrT=t2(inputs["L0_attn_Wk_rope"]), bkr2=b2(inputs["L0_attn_bk_rope"], 2 * cfg.JH),
        wvuT=t2(inputs["L0_attn_Wv_up"]), bvu2=b2(inputs["L0_attn_bv_up"], cfg.DC),
        woT=t2(inputs["L0_attn_Wo"]), bo2=b2(inputs["L0_attn_bo"], cfg.DC),
        routerT=t2(inputs["L0_ffn_router_weight"]),
        wgT=t2(inputs["L0_ffn_sh_gate"]), wuT=t2(inputs["L0_ffn_sh_up"]),
        wdT=t2(inputs["L0_ffn_sh_down"]),
        rgT=t2(inputs["L0_ffn_r_gate"]), ruT=t2(inputs["L0_ffn_r_up"]),
        rdT=t2(inputs["L0_ffn_r_down"]),
    )

    ROT = D // 2
    inv = (1.0 / (np.float32(cfg.base) ** (np.arange(0, ROT, 2, dtype=f32) / f32(ROT)))).astype(f32)
    tpos = np.arange(S, dtype=f32)
    freqs = np.outer(tpos, inv).astype(f32)
    cosT = np.ascontiguousarray(np.cos(freqs).astype(f32).T)
    sinT = np.ascontiguousarray(np.sin(freqs).astype(f32).T)

    in_maps = []
    for c in range(C):
        m = dict(shared)
        m["x"] = np.ascontiguousarray(x[c * T:(c + 1) * T])
        m["xT"] = np.ascontiguousarray(x[c * T:(c + 1) * T].T)
        m["cosT"] = np.ascontiguousarray(cosT[:, c * T:(c + 1) * T])
        m["sinT"] = np.ascontiguousarray(sinT[:, c * T:(c + 1) * T])
        in_maps.append(m)
    return in_maps


def kernel(**inputs):
    cfg = CFG
    nc = build(cfg)
    in_maps = _host_inputs(cfg, inputs)
    res = run_bass_kernel_spmd(nc, in_maps, core_ids=list(range(cfg.C)))
    out = np.concatenate(
        [np.ascontiguousarray(np.asarray(r["out"], dtype=np.float32).T) for r in res.results],
        axis=0)
    return out.reshape(1, cfg.S, cfg.D)


# version 34 (final)
# speedup vs baseline: 1.0592x; 1.0119x over previous
import numpy as np
import ml_dtypes

import concourse.bass as bass
import concourse.mybir as mybir
from concourse.bass_utils import run_bass_kernel_spmd

F32 = mybir.dt.float32
BF16 = mybir.dt.bfloat16
U8 = mybir.dt.uint8
AF = mybir.ActivationFunctionType
ALU = mybir.AluOpType
AXX = mybir.AxisListType.X
BF = ml_dtypes.bfloat16


class Cfg:
    def __init__(self, S=8192, D=2048, L=512, IS=8192, IR=1024, NE=8, C=8,
                 eps=1e-5, base=10000.0):
        self.S, self.D, self.L, self.IS, self.IR, self.NE, self.C = S, D, L, IS, IR, NE, C
        self.eps, self.base = eps, base
        self.T = S // C
        self.QT = self.T // 128
        self.DC = D // 128
        self.LC = L // 128
        self.NH = D // 128
        self.NC2 = self.DC // 2
        self.JH = (D // 4) // 128
        self.KC = S // 128
        self.ISC = IS // 128
        self.IRC = IR // 128
        self.IG = 4
        self.NF = max(1, self.T // 512)
        self.NW = min(512, self.T)


CFG = Cfg()


def _groups(nchunks, gsz):
    out, i = [], 0
    while i < nchunks:
        out.append((i, min(gsz, nchunks - i)))
        i += gsz
    return out


def build(cfg):
    S, D, L, T = cfg.S, cfg.D, cfg.L, cfg.T
    DC, LC, QT, NH, KC = cfg.DC, cfg.LC, cfg.QT, cfg.NH, cfg.KC
    JH, NC2 = cfg.JH, cfg.NC2
    NE, NF, NW = cfg.NE, cfg.NF, cfg.NW
    C = cfg.C
    scale = float(np.float32(1.0 / np.sqrt(np.float32(128.0))))

    nc = bass.Bass("TRN2", target_bir_lowering=False)

    x_d = nc.dram_tensor("x", [T, D], F32, kind="ExternalInput")
    cos_d = nc.dram_tensor("cosT", [JH * 128, T], F32, kind="ExternalInput")
    sin_d = nc.dram_tensor("sinT", [JH * 128, T], F32, kind="ExternalInput")
    wqd_d = nc.dram_tensor("wqdT", [D, L], BF16, kind="ExternalInput")
    wkvd_d = nc.dram_tensor("wkvdT", [D, L], BF16, kind="ExternalInput")
    wqu_d = nc.dram_tensor("wquT", [L, D], BF16, kind="ExternalInput")
    wkn_d = nc.dram_tensor("wknT", [L, D // 2], BF16, kind="ExternalInput")
    wkr_d = nc.dram_tensor("wkrT", [L, D // 2], BF16, kind="ExternalInput")
    wvu_d = nc.dram_tensor("wvuT", [L, D], BF16, kind="ExternalInput")
    wo_d = nc.dram_tensor("woT", [D, D], BF16, kind="ExternalInput")
    rt_d = nc.dram_tensor("routerT", [D, NE], BF16, kind="ExternalInput")
    wg_d = nc.dram_tensor("wgT", [D, cfg.IS], BF16, kind="ExternalInput")
    wu_d = nc.dram_tensor("wuT", [D, cfg.IS], BF16, kind="ExternalInput")
    wd_d = nc.dram_tensor("wdT", [cfg.IS, D], BF16, kind="ExternalInput")
    rg_d = nc.dram_tensor("rgT", [D, cfg.IR], BF16, kind="ExternalInput")
    ru_d = nc.dram_tensor("ruT", [D, cfg.IR], BF16, kind="ExternalInput")
    rd_d = nc.dram_tensor("rdT", [cfg.IR, D], BF16, kind="ExternalInput")
    bqd_d = nc.dram_tensor("bqd2", [128, LC], F32, kind="ExternalInput")
    bkv_d = nc.dram_tensor("bkv2", [128, LC], F32, kind="ExternalInput")
    bqu_d = nc.dram_tensor("bqu2", [128, DC], F32, kind="ExternalInput")
    bkn_d = nc.dram_tensor("bkn2", [128, NC2], F32, kind="ExternalInput")
    bkr_d = nc.dram_tensor("bkr2", [128, 2 * JH], F32, kind="ExternalInput")
    bvu_d = nc.dram_tensor("bvu2", [128, DC], F32, kind="ExternalInput")
    bo_d = nc.dram_tensor("bo2", [128, DC], F32, kind="ExternalInput")
    xT_d = nc.dram_tensor("xT", [D, T], F32, kind="ExternalInput")
    out_d = nc.dram_tensor("out", [D, T], F32, kind="ExternalOutput")

    id_bf_d = nc.inline_tensor(np.eye(128, dtype=BF), "id_bf")
    id_f32_d = nc.inline_tensor(np.eye(128, dtype=np.float32), "id_f32")

    NQ = 4
    kT_locQ = [nc.dram_tensor(f"kT_loc{q}", [D // NQ, T], BF16) for q in range(NQ)]
    v_locQ = [nc.dram_tensor(f"v_loc{q}", [T, D // NQ], BF16) for q in range(NQ)]
    kT_gatQ = [nc.dram_tensor(f"kT_gat{q}", [C * (D // NQ), T], BF16) for q in range(NQ)]
    v_gatQ = [nc.dram_tensor(f"v_gat{q}", [C * T, D // NQ], BF16) for q in range(NQ)]
    kn_loc = nc.dram_tensor("kn_loc", [1, NH], F32)
    kn_gat = nc.dram_tensor("kn_gat", [1, NH], F32)
    comb_dr = nc.dram_tensor("comb_dr", [QT, 128], F32)
    cneg_dr = nc.dram_tensor("cneg_dr", [1, NH], F32)
    yffn_dr = nc.dram_tensor("yffn_dr", [128, DC * T], F32)

    o = 0

    def take(nbytes, align=256):
        nonlocal o
        o = (o + align - 1) // align * align
        r = o
        o += nbytes
        return r

    O_IDBF = take(256)
    O_IDF32 = take(512)
    O_ONES = take(256)
    O_BQD = take(4 * LC); O_BKV = take(4 * LC); O_BQU = take(4 * DC)
    O_BKN = take(4 * NC2); O_BKR = take(4 * 2 * JH); O_BVU = take(4 * DC)
    O_BO = take(4 * DC)
    O_COMB = take(2 * T)
    O_CNEG = take(4 * NH)
    O_KN2 = take(4 * NH)
    O_QN2 = take(4 * NH)
    O_KNG = take(4 * NH)
    O_CROW = take(4 * NH)
    O_CROW2 = take(4 * NH)
    O_LGS = take(4 * NE * QT)
    O_COMBT = take(4 * 128)
    O_MSK = take(4 * NE * QT)
    O_M1 = take(4 * QT)
    O_M2 = take(4 * QT)
    O_RTR = take(2 * DC * NE)
    O_EPS = take(64)
    O_SMALL = take(64 * 16)
    FIXED_END = take(0, 512)

    def small(i, n=1):
        return (O_SMALL + i * 64, n)

    O_XNT = FIXED_END
    O_QTT = O_XNT + 2 * DC * T
    BIG = O_QTT + 2 * DC * T

    D4 = D * 4
    A_X = [BIG, BIG + D4, BIG + 2 * D4]
    A_XSQ = BIG + 3 * D4
    A_XNB = [A_XSQ + D4, A_XSQ + D4 + 2 * D]
    A_END = A_XSQ + D4 + 4 * D
    B_COS = BIG
    B_SIN = B_COS + 4 * JH * T
    B_KVL = B_SIN + 4 * JH * T
    B_W = B_KVL + 2 * LC * T
    B_WSZ = max(2 * DC * L, 2 * LC * D)
    B_KT = B_W + B_WSZ
    B_KR = B_KT + 2 * DC * T
    B_VSB = B_KR
    B_RT1 = B_KR + 4 * 2 * JH * T
    B_RT2 = B_RT1 + 4 * T
    B_SQ = B_RT2 + 4 * T
    B_VT = [B_SQ + 2 * T, B_SQ + 4 * T]
    B_QL = B_KVL
    B_END = B_VT[1] + 2 * T
    C_WG = BIG
    C_WU = C_WG + 2 * DC * cfg.IG * 128
    C_WD = C_WU + 2 * DC * cfg.IG * 128
    C_Y = C_WD + 2 * cfg.IG * D
    C_H = [C_Y + 4 * DC * T, C_Y + 4 * DC * T + 2 * cfg.IG * T]
    C_SG = C_H[1] + 2 * cfg.IG * T
    C_T2 = C_SG + 2 * T
    C_END = C_T2 + 2 * T
    E_KH = [BIG, BIG + 2 * S]
    E_VH = [BIG + 4 * S, BIG + 6 * S]
    E_E = [BIG + 8 * S + i * 2 * T for i in range(4)]
    E_PB = [BIG + 8 * S + 8 * T, BIG + 8 * S + 10 * T]
    E_EA = BIG + 8 * S + 12 * T
    E_EA2 = E_EA + 4 * T
    E_AO = E_EA2 + 4 * T
    E_END = E_AO + 2 * DC * T
    F_YB = [BIG, BIG + 4 * T]
    F_XR = [BIG + 8 * T, BIG + 16 * T]
    F_OS = [BIG + 24 * T, BIG + 32 * T]
    F_WO = [E_END, E_END + 2 * DC * 256]
    F_END = F_WO[1] + 2 * DC * 256

    ARENA = max(A_END, B_END, C_END, E_END, F_END)
    ARENA = (ARENA + 511) // 512 * 512
    O_QL = B_RT1
    O_ONESF = B_VT[1]
    O_RCPA = B_RT2
    O_RCPB = C_SG
    O_NSQ = B_VSB
    arena = nc.alloc_sbuf_tensor("arena", [128, ARENA], U8)

    def bb(off, n, parts=128, p0=0):
        return arena[p0:p0 + parts, off:off + 2 * n].bitcast(BF16)

    def bf(off, n, parts=128, p0=0):
        return arena[p0:p0 + parts, off:off + 4 * n].bitcast(F32)

    PT = max(512, T)
    P = [nc.alloc_psum_tensor(f"P{i}", [128, PT], F32) for i in range(4)]

    class Sem:
        def __init__(self, name):
            self.name = name
            self.h = nc.semaphore(name).__enter__()
            self.n = 0

        def bump(self, k=1):
            self.n += k
            return self.n

    s_xa = [Sem("s_xa0"), Sem("s_xa1"), Sem("s_xa2")]
    g_cst = Sem("g_cst")
    g_cs = Sem("g_cs")
    g_wB = Sem("g_wB")
    g_slab = [Sem("g_sg"), Sem("g_su"), Sem("g_sd")]
    g_wo = [Sem("g_wo0"), Sem("g_wo1")]
    g_kn = Sem("g_kn")
    g_cb = Sem("g_cb")
    g_ms = Sem("g_ms")
    g_cc = Sem("g_cc")
    s_kst = Sem("s_kst")
    s_kn2 = Sem("s_kn2")
    s_vst = Sem("s_vst")
    s_cd1 = Sem("s_cd1")
    s_cd2 = Sem("s_cd2")
    s_kv = [Sem("s_kv0"), Sem("s_kv1")]
    s_ysp = [Sem("s_yspa"), Sem("s_yspb"), Sem("s_yspc")]
    s_yb = [Sem("s_yb0"), Sem("s_yb1")]
    s_xr = [Sem("s_xr0"), Sem("s_xr1")]
    s_out = [Sem("s_out0"), Sem("s_out1"), Sem("s_out2"), Sem("s_out3")]
    scsem = Sem("scsem")
    expsem = Sem("expsem")
    pvsem = Sem("pvsem")
    dacc = Sem("dacc")
    pacc = Sem("pacc")
    vfin = Sem("vfin")
    TE = Sem("TE")
    V = Sem("V")
    SC = Sem("SC")

    ops = {k: [] for k in ("sync", "gpsimd", "tensor", "scalar", "vector")}
    max_waited = {}

    def em(eng, fn):
        ops[eng].append(fn)

    def wait(eng, sem, v):
        if v > 0:
            max_waited[sem] = max(max_waited.get(sem, 0), v)
            em(eng, lambda e, h=sem.h, vv=v: e.wait_ge(h, vv))

    def dma(eng, sem, out, in_):
        tgt = sem.bump(16)
        em(eng, lambda e, oo=out, ii=in_, h=sem.h: e.dma_start(out=oo, in_=ii).then_inc(h, 16))
        return tgt

    def mm(pp, w, r, st, sp, inc=None, inc_amt=1):
        if inc is None:
            em("tensor", lambda e, o=pp, ww=w, rr=r, s1=st, s2=sp:
               e.matmul(o, ww, rr, start=s1, stop=s2, skip_group_check=True))
        else:
            em("tensor", lambda e, o=pp, ww=w, rr=r, s1=st, s2=sp, h=inc.h, a=inc_amt:
               e.matmul(o, ww, rr, start=s1, stop=s2, skip_group_check=True).then_inc(h, a))

    ident_bf = bb(O_IDBF, 128)
    ident_f32 = bf(O_IDF32, 128)
    ones_bf = bb(O_ONES, 128)
    ones_f32 = bf(O_ONESF, 128)

    dma("gpsimd", g_cst, bb(O_IDBF, 128), id_bf_d[:, :])
    dma("gpsimd", g_cst, bf(O_IDF32, 128), id_f32_d[:, :])
    for od, dd in ((O_BQD, bqd_d), (O_BKV, bkv_d), (O_BQU, bqu_d),
                   (O_BKN, bkn_d), (O_BKR, bkr_d), (O_BVU, bvu_d), (O_BO, bo_d)):
        dma("gpsimd", g_cst, bf(od, dd.shape[1]), dd[:, :])
    dma("gpsimd", g_cst, bb(O_RTR, DC * NE).rearrange("p (dc e) -> p dc e", dc=DC),
        rt_d.ap().rearrange("(dc p) e -> p dc e", p=128))
    CST_ALL = g_cst.n
    em("gpsimd", lambda e: e.memset(bb(O_ONES, 128), 1.0).then_inc(g_ms.h, 1))
    ONES_OK = g_ms.bump()
    em("gpsimd", lambda e, v=float(cfg.eps): e.memset(bf(O_EPS, 1), v).then_inc(g_ms.h, 1))
    EPS_OK = g_ms.bump()

    xnT = bb(O_XNT, DC * T)
    A_VDONE = {}
    A_TCOPY = {}
    A_TTR = {}
    A_SCEND = {}
    for qt in range(QT):
        pr = qt % 3
        p2 = qt % 2
        sb = 8 * p2
        if qt >= 3:
            wait("sync", V, A_VDONE[qt - 3])
        xld = dma("sync", s_xa[pr], bf(A_X[pr], D), x_d[qt * 128:(qt + 1) * 128, :])
        xap = bf(A_X[pr], D)

        wait("scalar", s_xa[pr], xld)
        if qt >= 2:
            wait("scalar", SC, A_SCEND[qt - 2])
            wait("scalar", V, A_VDONE[qt - 2])
        em("scalar", lambda e, x=xap, oo=bf(A_XSQ, D), a=bf(*small(sb + 0)):
           e.activation(oo, x, AF.Copy, accum_out=a).then_inc(SC.h, 1))
        sc_sum = SC.bump()
        wait("scalar", SC, sc_sum)
        em("scalar", lambda e, x=xap, oo=bf(A_XSQ, D), a=bf(*small(sb + 2)):
           e.activation(oo, x, AF.Square, accum_out=a).then_inc(SC.h, 1))
        sc_sq = SC.bump()
        wait("vector", SC, sc_sum)
        if qt >= 2:
            wait("vector", V, A_VDONE[qt - 2])
        em("vector", lambda e, i=bf(*small(sb + 0)), oo=bf(*small(sb + 1)):
           e.tensor_scalar_mul(oo, i, 1.0 / D).then_inc(V.h, 1))
        v_mean = V.bump()
        wait("vector", V, v_mean)
        em("vector", lambda e, m=bf(*small(sb + 1)), oo=bf(*small(sb + 3)):
           e.tensor_mul(oo, m, m).then_inc(V.h, 1))
        v_m2 = V.bump()
        wait("vector", SC, sc_sq)
        wait("vector", V, v_m2)
        em("vector", lambda e, sq=bf(*small(sb + 2)), m2=bf(*small(sb + 3)), oo=bf(*small(sb + 4)):
           e.scalar_tensor_tensor(oo, sq, 1.0 / D, m2, ALU.mult, ALU.subtract).then_inc(V.h, 1))
        v_var = V.bump()
        wait("scalar", V, v_var)
        if qt == 0:
            wait("scalar", g_ms, EPS_OK)
        em("scalar", lambda e, i=bf(*small(sb + 4)), oo=bf(*small(sb + 5)), b=bf(O_EPS, 1):
           e.activation(oo, i, AF.Ln, bias=b).then_inc(SC.h, 1))
        sc_ln = SC.bump()
        wait("scalar", SC, sc_ln)
        em("scalar", lambda e, i=bf(*small(sb + 5)), oo=bf(*small(sb + 6)):
           e.activation(oo, i, AF.Exp, scale=-0.5).then_inc(SC.h, 1))
        sc_rs = SC.bump()
        wait("vector", SC, sc_rs)
        if qt >= 2:
            wait("vector", TE, A_TTR[qt - 2])
        em("vector", lambda e, x=xap, m=bf(*small(sb + 1)), r=bf(*small(sb + 6)), oo=bb(A_XNB[p2], D):
           e.tensor_scalar(oo, x, m, r, ALU.subtract, ALU.mult).then_inc(V.h, 1))
        v_xn = V.bump()
        A_VDONE[qt] = v_xn
        A_SCEND[qt] = SC.n

        ppr = P[(qt % 2) + (2 if qt >= 4 else 0)]
        pview = ppr[:, 0:(DC * 128) // 2].bitcast(BF16)
        if qt == 0:
            wait("tensor", g_cst, CST_ALL)
        wait("tensor", V, v_xn)
        if qt not in (0, 1, 4, 5):
            wait("tensor", V, A_TCOPY[qt - 2])
        for dc in range(DC):
            if dc == DC - 1:
                em("tensor", lambda e, oo=pview[:, dc * 128:(dc + 1) * 128],
                   i=bb(A_XNB[p2] + dc * 256, 128), idt=ident_bf:
                   e.transpose(oo, i, idt).then_inc(TE.h, 1))
            else:
                em("tensor", lambda e, oo=pview[:, dc * 128:(dc + 1) * 128],
                   i=bb(A_XNB[p2] + dc * 256, 128), idt=ident_bf:
                   e.transpose(oo, i, idt))
        A_TTR[qt] = TE.bump()
        wait("vector", TE, A_TTR[qt])
        dst = xnT.rearrange("p (dc t) -> p dc t", dc=DC)[:, :, qt * 128:(qt + 1) * 128]
        src = pview.rearrange("p (dc i) -> p dc i", dc=DC)
        em("vector", lambda e, oo=dst, i=src: e.tensor_copy(oo, i).then_inc(V.h, 1))
        A_TCOPY[qt] = V.bump()

    A_DONE_V = V.n
    A_DONE_SC = SC.n

    def mm_group(pp, tiles, inc_sem):
        n = len(tiles)
        for i, (po, w, r, st, sp) in enumerate(tiles):
            if i == n - 1:
                em("tensor", lambda e, o=po, ww=w, rr=r, s1=st, s2=sp, h=inc_sem.h:
                   e.matmul(o, ww, rr, start=s1, stop=s2, skip_group_check=True).then_inc(h, 1))
            else:
                em("tensor", lambda e, o=po, ww=w, rr=r, s1=st, s2=sp:
                   e.matmul(o, ww, rr, start=s1, stop=s2, skip_group_check=True))
        return inc_sem.bump()

    def mm_proj2(rhs_off, rhs_nchunks, w_off, nout, dst_writer,
                 pre_tensor_wait=None, psel=None, first_wait=None, f_waits=None):
        first_v = V.n if first_wait is None else first_wait
        psel = psel or [P[0], P[1]]
        evac_v = {}
        for oc in range(nout):
            pp = psel[oc % len(psel)]
            if oc == 0 and pre_tensor_wait is not None:
                pre_tensor_wait()
            if oc < len(psel):
                wait("tensor", V, first_v)
            else:
                wait("tensor", V, evac_v[oc - len(psel)])
            if f_waits is None:
                tiles = []
                for c in range(rhs_nchunks):
                    w = bb(w_off + 2 * (c * nout + oc) * 128, 128)
                    for f in range(NF):
                        tiles.append((pp[:, f * NW:(f + 1) * NW], w,
                                      bb(rhs_off + 2 * (c * T + f * NW), NW),
                                      c == 0, c == rhs_nchunks - 1))
                t_mm = mm_group(pp, tiles, TE)
            else:
                for f in range(NF):
                    if oc == 0 and f in f_waits:
                        wait("tensor", f_waits[f][0], f_waits[f][1])
                    tiles = [(pp[:, f * NW:(f + 1) * NW],
                              bb(w_off + 2 * (c * nout + oc) * 128, 128),
                              bb(rhs_off + 2 * (c * T + f * NW), NW),
                              c == 0, c == rhs_nchunks - 1)
                             for c in range(rhs_nchunks)]
                    t_mm = mm_group(pp, tiles, TE)
            wait("vector", TE, t_mm)
            evac_v[oc] = dst_writer(oc, pp)
        return evac_v

    def evac_bias(dst_off, bias_off, f32out=False, first_wait=None):
        def w(oc, pp):
            if oc == 0 and first_wait is not None:
                wait("vector", first_wait[0], first_wait[1])
            bias = bf(bias_off + 4 * oc, 1)
            if f32out:
                oap = bf(dst_off + 4 * oc * T, T)
            else:
                oap = bb(dst_off + 2 * oc * T, T)
            em("vector", lambda e, oo=oap, i=pp[:, 0:T], b=bias:
               e.tensor_scalar(oo, i, b, None, ALU.add).then_inc(V.h, 1))
            return V.bump()
        return w

    wait("sync", V, A_DONE_V)
    wait("sync", SC, A_DONE_SC)
    dma("sync", g_cs, bf(B_COS, JH * T).rearrange("p (j t) -> p j t", j=JH),
        cos_d.ap().rearrange("(j p) t -> p j t", p=128))
    dma("sync", g_cs, bf(B_SIN, JH * T).rearrange("p (j t) -> p j t", j=JH),
        sin_d.ap().rearrange("(j p) t -> p j t", p=128))
    CS_ALL = g_cs.n

    def loadB(src_ap, te_wait=None):
        wait("gpsimd", TE, TE.n if te_wait is None else te_wait)
        wait("gpsimd", g_wB, g_wB.n)
        nch = src_ap.shape[0] // 128
        return dma("gpsimd", g_wB,
                   bb(B_W, nch * src_ap.shape[1]).rearrange("p (c n) -> p c n", c=nch),
                   src_ap.rearrange("(c p) n -> p c n", p=128))

    ld = loadB(wkvd_d.ap(), te_wait=0)
    wait("gpsimd", g_wB, ld)
    ldn = dma("gpsimd", g_wB,
              bb(B_RT1, (L // 128) * (D // 2)).rearrange("p (c n) -> p c n", c=L // 128),
              wkn_d.ap().rearrange("(c p) n -> p c n", p=128))
    wait("gpsimd", g_wB, ldn)
    ldq = dma("gpsimd", g_wB, bb(O_QTT, DC * L).rearrange("p (c n) -> p c n", c=DC),
              wqd_d.ap().rearrange("(c p) n -> p c n", p=128))

    def pre_kv():
        wait("tensor", g_wB, ld)
    mm_proj2(O_XNT, DC, B_W, LC, evac_bias(B_KVL, O_BKV), pre_tensor_wait=pre_kv,
             first_wait=A_TCOPY[QT // NF - 1],
             f_waits={f: (V, A_TCOPY[(f + 1) * (QT // NF) - 1]) for f in range(NF)})
    KV_TE = TE.n

    wait("gpsimd", TE, KV_TE)
    wait("gpsimd", g_wB, ldq)
    ldr = dma("gpsimd", g_wB, bb(B_W, (L // 128) * (D // 2)).rearrange("p (c n) -> p c n", c=L // 128),
              wkr_d.ap().rearrange("(c p) n -> p c n", p=128))

    mm_proj2(B_KVL, LC, B_RT1, NC2, evac_bias(B_KT, O_BKN),
             pre_tensor_wait=lambda: wait("tensor", g_wB, ldn))

    KR_EV = mm_proj2(B_KVL, LC, B_W, 2 * JH, evac_bias(B_KR, O_BKR, f32out=True),
                     pre_tensor_wait=lambda: wait("tensor", g_wB, ldr))

    wait("vector", g_cs, CS_ALL)
    wait("vector", V, KR_EV[2 * JH - 1])
    for jc in range(JH):
        x1 = bf(B_KR + 4 * jc * T, T)
        x2 = bf(B_KR + 4 * (JH + jc) * T, T)
        co = bf(B_COS + 4 * jc * T, T)
        si = bf(B_SIN + 4 * jc * T, T)
        em("vector", lambda e, oo=bf(B_RT1, T), a=x1, b=co: e.tensor_mul(oo, a, b).then_inc(V.h, 1))
        V.bump()
        em("vector", lambda e, oo=bf(B_RT2, T), a=x2, b=si: e.tensor_mul(oo, a, b).then_inc(V.h, 1))
        v2 = V.bump()
        wait("vector", V, v2)
        em("vector", lambda e, oo=bb(B_KT + 2 * (NC2 + jc) * T, T), a=bf(B_RT1, T), b=bf(B_RT2, T):
           e.tensor_sub(oo, a, b).then_inc(V.h, 1))
        v3 = V.bump()
        wait("vector", V, v3)
        em("vector", lambda e, oo=bf(B_RT1, T), a=x1, b=si: e.tensor_mul(oo, a, b).then_inc(V.h, 1))
        V.bump()
        em("vector", lambda e, oo=bf(B_RT2, T), a=x2, b=co: e.tensor_mul(oo, a, b).then_inc(V.h, 1))
        v5 = V.bump()
        wait("vector", V, v5)
        em("vector", lambda e, oo=bb(B_KT + 2 * (NC2 + JH + jc) * T, T), a=bf(B_RT1, T), b=bf(B_RT2, T):
           e.tensor_add(oo, a, b).then_inc(V.h, 1))
        v6 = V.bump()
        if jc < JH - 1:
            wait("vector", V, v6)
    KT_DONE_V = V.n

    wait("sync", V, KT_DONE_V)
    for q in range(NQ):
        kst = dma("sync", s_kst, kT_locQ[q].ap().rearrange("(dc p) t -> p dc t", p=128),
                  bb(B_KT + q * (DC // NQ) * 2 * T, (DC // NQ) * T)
                  .rearrange("p (dc t) -> p dc t", dc=DC // NQ))
    KST_ALL = kst

    QD_EV = mm_proj2(O_XNT, DC, O_QTT, LC,
                     evac_bias(O_QL, O_BQD, first_wait=(V, KT_DONE_V)),
                     pre_tensor_wait=lambda: wait("tensor", g_wB, ldq),
                     first_wait=KR_EV[2 * JH - 1])
    QD_TE = TE.n

    wait("gpsimd", V, KT_DONE_V)
    wait("gpsimd", g_wB, ldr)
    ldv = dma("gpsimd", g_wB, bb(C_WG, LC * D).rearrange("p (c n) -> p c n", c=LC),
              wvu_d.ap().rearrange("(c p) n -> p c n", p=128))
    wait("gpsimd", g_wB, ldv)
    ldqu = dma("gpsimd", g_wB, bb(C_WU, LC * D).rearrange("p (c n) -> p c n", c=LC),
               wqu_d.ap().rearrange("(c p) n -> p c n", p=128))

    KSQ_SC = {}
    KN_MM = {}
    KN_V = {}
    for h in range(NH):
        half = h // 8
        if h % 8 == 0:
            wait("scalar", V, KT_DONE_V)
            if half == 1:
                wait("scalar", TE, KN_MM[7])
            em("scalar", lambda e, oo=bb(O_NSQ, 8 * T), i=bb(B_KT + 2 * half * 8 * T, 8 * T):
               e.activation(oo, i, AF.Square).then_inc(SC.h, 1))
            KSQ_SC[half] = SC.bump()
        if h == 0:
            wait("tensor", g_ms, ONES_OK)
        wait("tensor", SC, KSQ_SC[half])
        if h >= 2:
            wait("tensor", V, KN_V[h - 2])
        pp = P[2 + (h % 2)]
        tiles = [(pp[:, f * NW:(f + 1) * NW], ones_bf,
                  bb(O_NSQ + 2 * ((h % 8) * T + f * NW), NW), True, True)
                 for f in range(NF)]
        KN_MM[h] = mm_group(pp, tiles, TE)
        wait("vector", TE, KN_MM[h])
        em("vector", lambda e, oo=bf(O_KN2 + 4 * h, 1), i=pp[:, 0:T]:
           e.reduce_max(oo, i, axis=AXX).then_inc(V.h, 1))
        KN_V[h] = V.bump()
    KN_DONE_V = V.n

    wait("sync", V, KN_DONE_V)
    knst = dma("sync", s_kn2, kn_loc[0:1, :], bf(O_KN2, NH, parts=1))

    vsb = bb(B_VSB, QT * D)
    VT_COPY = {}
    VT_TR = {}

    def evac_v_writer(dc, pp):
        pr = dc % 2
        bias = bf(O_BVU + 4 * dc, 1)
        if dc >= 2:
            wait("vector", TE, VT_TR[dc - 2])
        em("vector", lambda e, oo=bb(B_VT[pr], T), i=pp[:, 0:T], b=bias:
           e.tensor_scalar(oo, i, b, None, ALU.add).then_inc(V.h, 1))
        v_e = V.bump()
        wait("tensor", V, v_e)
        ptr = P[2 + pr]
        pview = ptr[:, 0:(QT * 128) // 2].bitcast(BF16)
        if dc < 2:
            wait("tensor", V, KN_V[NH - 2 + dc])
        else:
            wait("tensor", V, VT_COPY[dc - 2])
        for qt in range(QT):
            if qt == QT - 1:
                em("tensor", lambda e, oo=pview[:, qt * 128:(qt + 1) * 128],
                   i=bb(B_VT[pr] + 2 * qt * 128, 128), idt=ident_bf:
                   e.transpose(oo, i, idt).then_inc(TE.h, 1))
            else:
                em("tensor", lambda e, oo=pview[:, qt * 128:(qt + 1) * 128],
                   i=bb(B_VT[pr] + 2 * qt * 128, 128), idt=ident_bf:
                   e.transpose(oo, i, idt))
        VT_TR[dc] = TE.bump()
        wait("vector", TE, VT_TR[dc])
        dst = vsb.rearrange("p (qt d) -> p qt d", qt=QT)[:, :, dc * 128:(dc + 1) * 128]
        src = pview.rearrange("p (qt i) -> p qt i", qt=QT)
        em("vector", lambda e, oo=dst, i=src: e.tensor_copy(oo, i).then_inc(V.h, 1))
        VT_COPY[dc] = V.bump()
        return VT_COPY[dc]

    def pre_v():
        wait("tensor", g_wB, ldv)
        wait("tensor", V, KT_DONE_V)
    mm_proj2(B_KVL, LC, C_WG, DC, evac_v_writer, pre_tensor_wait=pre_v,
             first_wait=QD_EV[LC - 1])
    V_TE = TE.n

    wait("sync", V, VT_COPY[DC - 1])
    for q in range(NQ):
        vst = dma("sync", s_vst, v_locQ[q].ap().rearrange("(qt p) d -> p qt d", p=128),
                  vsb.rearrange("p (qt d) -> p qt d", qt=QT)[:, :, q * (D // NQ):(q + 1) * (D // NQ)])
    VST_ALL = vst

    QT_EV = mm_proj2(O_QL, LC, C_WU, DC, evac_bias(O_QTT, O_BQU),
                     pre_tensor_wait=lambda: wait("tensor", g_wB, ldqu))
    QU_TE = TE.n

    sgroups = [(0, 1), (1, 1), (2, 2)] + [(g0 + 4, gs) for g0, gs in _groups(cfg.ISC - 4, cfg.IG)]
    rgroups = _groups(cfg.IRC, cfg.IG)
    allgroups = [("s", g0, gs) for g0, gs in sgroups] + [("r", g0, gs) for g0, gs in rgroups]
    g0_gs = allgroups[0][2]
    W0_G, W0_U, W0_D = B_W, B_KT, C_Y + 4 * 15 * T
    wait("gpsimd", TE, QD_TE)
    wait("gpsimd", g_slab[0], g_slab[0].n)
    lgld0 = dma("gpsimd", g_slab[0],
                bb(W0_G, DC * g0_gs * 128).rearrange("p (dc i) -> p dc i", dc=DC),
                wg_d.ap()[:, 0:g0_gs * 128].rearrange("(dc p) i -> p dc i", p=128))
    wait("gpsimd", SC, KSQ_SC[1])
    wait("gpsimd", s_kst, KST_ALL)
    wait("gpsimd", g_slab[1], g_slab[1].n)
    luld0 = dma("gpsimd", g_slab[1],
                bb(W0_U, DC * g0_gs * 128).rearrange("p (dc i) -> p dc i", dc=DC),
                wu_d.ap()[:, 0:g0_gs * 128].rearrange("(dc p) i -> p dc i", p=128))
    wait("gpsimd", s_vst, VST_ALL)
    wait("gpsimd", g_slab[2], g_slab[2].n)
    ldld0 = dma("gpsimd", g_slab[2],
                bb(W0_D, g0_gs * D).rearrange("p (il c) -> p il c", il=g0_gs),
                wd_d.ap()[0:g0_gs * 128, :].rearrange("(il p) c -> p il c", p=128))

    CC_K = {}
    CC_V = {}

    def emit_cc_k(q):
        wait("gpsimd", s_kst, KST_ALL)
        em("gpsimd", lambda e: e.collective_compute(
            "AllGather", ALU.bypass, replica_groups=[list(range(C))],
            ins=[kT_locQ[q].ap().opt()], outs=[kT_gatQ[q].ap().opt()]).then_inc(g_cc.h, 1))
        CC_K[q] = g_cc.bump()

    def emit_cc_v(q):
        wait("gpsimd", s_vst, VST_ALL)
        em("gpsimd", lambda e: e.collective_compute(
            "AllGather", ALU.bypass, replica_groups=[list(range(C))],
            ins=[v_locQ[q].ap().opt()], outs=[v_gatQ[q].ap().opt()]).then_inc(g_cc.h, 1))
        CC_V[q] = g_cc.bump()

    emit_cc_k(0)
    wait("gpsimd", s_kn2, knst)
    em("gpsimd", lambda e: e.collective_compute(
        "AllReduce", ALU.max, replica_groups=[list(range(C))],
        ins=[kn_loc.ap().opt()], outs=[kn_gat.ap().opt()]).then_inc(g_cc.h, 1))
    CC_KN = g_cc.bump()

    QSQ_SC = {}
    QN_MM = {}
    QN_V = {}
    for h in range(NH):
        half = h // 8
        if h % 8 == 0:
            wait("scalar", V, QT_EV[DC - 1])
            wait("scalar", s_vst, VST_ALL)
            wait("scalar", TE, KN_MM[15] if half == 0 else QN_MM[7])
            em("scalar", lambda e, oo=bb(O_NSQ, 8 * T), i=bb(O_QTT + 2 * half * 8 * T, 8 * T):
               e.activation(oo, i, AF.Square).then_inc(SC.h, 1))
            QSQ_SC[half] = SC.bump()
        wait("tensor", SC, QSQ_SC[half])
        if h < 2:
            wait("tensor", V, VT_COPY[DC - 2 + h])
        else:
            wait("tensor", V, QN_V[h - 2])
        pp = P[2 + (h % 2)]
        tiles = [(pp[:, f * NW:(f + 1) * NW], ones_bf,
                  bb(O_NSQ + 2 * ((h % 8) * T + f * NW), NW), True, True)
                 for f in range(NF)]
        QN_MM[h] = mm_group(pp, tiles, TE)
        wait("vector", TE, QN_MM[h])
        em("vector", lambda e, oo=bf(O_QN2 + 4 * h, 1), i=pp[:, 0:T]:
           e.reduce_max(oo, i, axis=AXX).then_inc(V.h, 1))
        QN_V[h] = V.bump()
    QN_DONE_V = V.n

    wait("tensor", V, QN_V[NH - 1])
    lg_tiles = []
    for qt in range(QT):
        for dc in range(DC):
            lg_tiles.append((P[3][:, qt * NE:(qt + 1) * NE],
                             bb(O_XNT + 2 * (dc * T + qt * 128), 128),
                             bb(O_RTR + 2 * dc * NE, NE),
                             dc == 0, dc == DC - 1))
    t_lg = mm_group(None, lg_tiles, TE)
    wait("vector", TE, t_lg)
    em("vector", lambda e, oo=bf(O_LGS, NE * QT), i=P[3][:, 0:NE * QT]:
       e.tensor_copy(oo, i).then_inc(V.h, 1))
    v_lg = V.bump()
    wait("vector", V, v_lg)
    em("vector", lambda e, oo=bf(O_M1, QT),
       i=bf(O_LGS, NE * QT).rearrange("p (q n) -> p q n", q=QT):
       e.reduce_max(oo, i, axis=AXX).then_inc(V.h, 1))
    v_m1 = V.bump()
    wait("vector", V, v_m1)
    for qt in range(QT):
        em("vector", lambda e, oo=bf(O_MSK + 4 * qt * NE, NE), i=bf(O_LGS + 4 * qt * NE, NE),
           m=bf(O_M1 + 4 * qt, 1):
           e.tensor_scalar(oo, i, m, None, ALU.is_ge).then_inc(V.h, 1))
        V.bump()
    v_eq = V.n
    wait("vector", V, v_eq)
    for qt in range(QT):
        em("vector", lambda e, oo=bf(O_MSK + 4 * qt * NE, NE), i=bf(O_MSK + 4 * qt * NE, NE),
           l=bf(O_LGS + 4 * qt * NE, NE):
           e.scalar_tensor_tensor(oo, i, -1e30, l, ALU.mult, ALU.add).then_inc(V.h, 1))
        V.bump()
    v_mk = V.n
    wait("vector", V, v_mk)
    em("vector", lambda e, oo=bf(O_M2, QT),
       i=bf(O_MSK, NE * QT).rearrange("p (q n) -> p q n", q=QT):
       e.reduce_max(oo, i, axis=AXX).then_inc(V.h, 1))
    v_m2 = V.bump()
    wait("vector", V, v_m2)
    em("vector", lambda e, oo=bf(O_M2, QT), a=bf(O_M2, QT), b=bf(O_M1, QT):
       e.tensor_sub(oo, a, b).then_inc(V.h, 1))
    v_d = V.bump()
    wait("scalar", V, v_d)
    em("scalar", lambda e, oo=bf(O_M2, QT), i=bf(O_M2, QT):
       e.activation(oo, i, AF.Exp).then_inc(SC.h, 1))
    sc_t = SC.bump()
    wait("vector", SC, sc_t)
    em("vector", lambda e, oo=bf(O_M1, QT), i=bf(O_M2, QT):
       e.tensor_scalar_add(oo, i, 1.0).then_inc(V.h, 1))
    v_o = V.bump()
    wait("vector", V, v_o)
    em("vector", lambda e, oo=bf(O_M2, QT), i=bf(O_M1, QT):
       e.reciprocal(oo, i).then_inc(V.h, 1))
    v_r = V.bump()
    wait("vector", V, v_r)
    em("vector", lambda e, oo=bf(O_COMBT, QT), a=bf(O_M1, QT), b=bf(O_M2, QT):
       e.tensor_mul(oo, a, b).then_inc(V.h, 1))
    v_combtok = V.bump()
    wait("tensor", V, v_combtok)
    em("tensor", lambda e, oo=P[2][0:max(QT, 1), 0:128], i=bf(O_COMBT, QT), idt=ident_f32:
       e.transpose(oo, i, idt).then_inc(TE.h, 1))
    t_ct = TE.bump()
    wait("vector", TE, t_ct)
    em("vector", lambda e, oo=bf(O_COMBT, 128, parts=QT), i=P[2][0:QT, 0:128]:
       e.tensor_copy(oo, i).then_inc(V.h, 1))
    v_ctc = V.bump()
    wait("sync", V, v_ctc)
    cbst = dma("sync", s_cd2, comb_dr[:, :], bf(O_COMBT, 128, parts=QT))

    yacc = bf(C_Y, DC * T)
    NSL = (DC + 1) // 2

    def load_wo(sidx):
        pr = sidx % 2
        ncc = min(2, DC - sidx * 2)
        wait("gpsimd", g_wo[pr], g_wo[pr].n)
        dma("gpsimd", g_wo[pr],
            bb(F_WO[pr], DC * ncc * 128).rearrange("p (dc c) -> p dc c", dc=DC),
            wo_d.ap()[:, sidx * 256:sidx * 256 + ncc * 128]
            .rearrange("(dc p) c -> p dc c", p=128))

    C_SILU = {}
    C_HM = {}
    C_YACC_V = {}
    C_TDOWN = {}
    C_GLAST = {}
    C_ULAST = {}

    for gi, (kind, g0, gs) in enumerate(allgroups):
        gsrc = (wg_d, wu_d, wd_d) if kind == "s" else (rg_d, ru_d, rd_d)
        if gi == 0:
            lgld, luld, ldld = lgld0, luld0, ldld0
        else:
            wait("gpsimd", TE, C_GLAST[gi - 1])
            wait("gpsimd", g_slab[0], g_slab[0].n)
            lgld = dma("gpsimd", g_slab[0],
                       bb(C_WG, DC * gs * 128).rearrange("p (dc i) -> p dc i", dc=DC),
                       gsrc[0].ap()[:, g0 * 128:(g0 + gs) * 128].rearrange("(dc p) i -> p dc i", p=128))
            wait("gpsimd", TE, C_ULAST[gi - 1])
            wait("gpsimd", g_slab[1], g_slab[1].n)
            luld = dma("gpsimd", g_slab[1],
                       bb(C_WU, DC * gs * 128).rearrange("p (dc i) -> p dc i", dc=DC),
                       gsrc[1].ap()[:, g0 * 128:(g0 + gs) * 128].rearrange("(dc p) i -> p dc i", p=128))
            wait("gpsimd", TE, C_TDOWN[gi - 1])
            wait("gpsimd", g_slab[2], g_slab[2].n)
            ldld = dma("gpsimd", g_slab[2],
                       bb(C_WD, gs * D).rearrange("p (il c) -> p il c", il=gs),
                       gsrc[2].ap()[g0 * 128:(g0 + gs) * 128, :].rearrange("(il p) c -> p il c", p=128))
        if gi == 2:
            emit_cc_v(0)
        if gi == 4:
            emit_cc_k(1)
        if gi == 6:
            emit_cc_v(1)
        if gi == 8:
            emit_cc_k(2)
        if gi == 11:
            emit_cc_v(2)
        if gi == 13:
            emit_cc_k(3)
        if gi == 15:
            emit_cc_v(3)
        if gi == 14:
            em("gpsimd", lambda e: e.memset(bf(O_ONESF, 128), 1.0).then_inc(g_ms.h, 1))
            ONESF_OK = g_ms.bump()
        if gi == 10:
            wait("gpsimd", g_cc, CC_KN)
            knld = dma("gpsimd", g_kn, bf(O_KNG, NH, parts=1), kn_gat[0:1, :])
        if gi == 8:
            wait("gpsimd", s_cd2, cbst)
            cbld = dma("gpsimd", g_cb, bb(O_COMB, T), bass.AP(comb_dr, 0, [[0, 128], [1, T]]))
        if gi == 12:
            wait("vector", g_kn, knld)
            em("vector", lambda e, oo=bf(O_CROW, NH, parts=1), a=bf(O_QN2, NH, parts=1),
               b=bf(O_KNG, NH, parts=1):
               e.tensor_mul(oo, a, b).then_inc(V.h, 1))
            v_c1 = V.bump()
            wait("scalar", V, v_c1)
            em("scalar", lambda e, oo=bf(O_CROW2, NH, parts=1), i=bf(O_CROW, NH, parts=1):
               e.activation(oo, i, AF.Sqrt).then_inc(SC.h, 1))
            sc_cs = SC.bump()
            wait("vector", SC, sc_cs)
            em("vector", lambda e, oo=bf(O_CROW, NH, parts=1), i=bf(O_CROW2, NH, parts=1):
               e.tensor_scalar_mul(oo, i, -1.05 * scale).then_inc(V.h, 1))
            v_cn = V.bump()
            wait("sync", V, v_cn)
            cst2 = dma("sync", s_cd1, cneg_dr[0:1, :], bf(O_CROW, NH, parts=1))
        for il in range(gs):
            if gi == 0 and il == 0:
                wait("tensor", V, QT_EV[DC - 1])
            wait("tensor", g_slab[0], lgld)
            prev_silu = C_SILU.get((gi, il - 1)) or (C_SILU.get((gi - 1, allgroups[gi - 1][2] - 1)) if gi > 0 else 0)
            wait("tensor", SC, prev_silu or 0)
            tiles = []
            for dc in range(DC):
                w = bb((W0_G if gi == 0 else C_WG) + 2 * (dc * gs * 128 + il * 128), 128)
                for f in range(NF):
                    tiles.append((P[0][:, f * NW:(f + 1) * NW], w,
                                  bb(O_XNT + 2 * (dc * T + f * NW), NW),
                                  dc == 0, dc == DC - 1))
            t_g = mm_group(None, tiles, TE)
            wait("tensor", g_slab[1], luld)
            prev_hm = C_HM.get((gi, il - 1)) or (C_HM.get((gi - 1, allgroups[gi - 1][2] - 1)) if gi > 0 else 0)
            wait("tensor", V, prev_hm or 0)
            tiles = []
            for dc in range(DC):
                w = bb((W0_U if gi == 0 else C_WU) + 2 * (dc * gs * 128 + il * 128), 128)
                for f in range(NF):
                    tiles.append((P[1][:, f * NW:(f + 1) * NW], w,
                                  bb(O_XNT + 2 * (dc * T + f * NW), NW),
                                  dc == 0, dc == DC - 1))
            t_u = mm_group(None, tiles, TE)
            C_GLAST[gi] = t_g
            C_ULAST[gi] = t_u
            wait("scalar", TE, t_g)
            wait("scalar", V, prev_hm or 0)
            em("scalar", lambda e, oo=bb(C_SG, T), i=P[0][:, 0:T]:
               e.activation(oo, i, AF.Silu).then_inc(SC.h, 1))
            C_SILU[gi, il] = SC.bump()
            wait("vector", SC, C_SILU[gi, il])
            wait("vector", TE, t_u)
            if gi >= 2:
                wait("vector", TE, C_TDOWN[gi - 2])
            if kind == "s":
                em("vector", lambda e, oo=bb(C_H[gi % 2] + 2 * il * T, T), a=bb(C_SG, T),
                   b=P[1][:, 0:T]: e.tensor_mul(oo, a, b).then_inc(V.h, 1))
                C_HM[gi, il] = V.bump()
            else:
                em("vector", lambda e, oo=bb(C_T2, T), a=bb(C_SG, T), b=P[1][:, 0:T]:
                   e.tensor_mul(oo, a, b).then_inc(V.h, 1))
                v_t2 = V.bump()
                wait("vector", V, v_t2)
                wait("vector", g_cb, cbld)
                em("vector", lambda e, oo=bb(C_H[gi % 2] + 2 * il * T, T), a=bb(C_T2, T),
                   b=bb(O_COMB, T): e.tensor_mul(oo, a, b).then_inc(V.h, 1))
                C_HM[gi, il] = V.bump()
        wait("tensor", g_slab[2], ldld)
        wait("tensor", V, C_HM[gi, gs - 1])
        for cc in range(DC):
            pp = P[2 + (cc % 2)]
            if cc >= 2:
                wait("tensor", V, C_YACC_V[gi, cc - 2])
            elif gi > 0:
                wait("tensor", V, C_YACC_V[gi - 1, DC - 2 + cc])
            else:
                wait("tensor", V, v_ctc)
            tiles = []
            for il in range(gs):
                w = bb((W0_D if gi == 0 else C_WD) + 2 * (il * D + cc * 128), 128)
                for f in range(NF):
                    tiles.append((pp[:, f * NW:(f + 1) * NW], w,
                                  bb(C_H[gi % 2] + 2 * (il * T + f * NW), NW),
                                  il == 0, il == gs - 1))
            t_d = mm_group(None, tiles, TE)
            wait("vector", TE, t_d)
            if gi == 0:
                if cc == 0:
                    wait("vector", s_kst, KST_ALL)
                    wait("vector", s_vst, VST_ALL)
                em("vector", lambda e, oo=bf(C_Y + 4 * cc * T, T), i=pp[:, 0:T]:
                   e.tensor_copy(oo, i).then_inc(V.h, 1))
            else:
                wait("vector", V, C_YACC_V[gi - 1, cc])
                em("vector", lambda e, oo=bf(C_Y + 4 * cc * T, T), i=pp[:, 0:T],
                   a=bf(C_Y + 4 * cc * T, T):
                   e.tensor_add(oo, a, i).then_inc(V.h, 1))
            C_YACC_V[gi, cc] = V.bump()
        C_TDOWN[gi] = TE.n

    FFN_LAST_V = C_YACC_V[len(allgroups) - 1, DC - 1]
    C_END_SC = SC.n

    wait("gpsimd", s_cd1, cst2)
    cngld = dma("gpsimd", g_kn, bf(O_CNEG, NH),
                bass.AP(cneg_dr, 0, [[0, 128], [1, NH]]))

    wait("gpsimd", TE, C_TDOWN[len(allgroups) - 1])
    load_wo(0)

    LG = len(allgroups) - 1
    wait("scalar", V, C_YACC_V[LG, 3])
    ysp_a = dma("scalar", s_ysp[0], yffn_dr[:, 0:4 * T], bf(C_Y, 4 * T))
    wait("scalar", V, C_YACC_V[LG, 9])
    ysp_b = dma("scalar", s_ysp[1], yffn_dr[:, 4 * T:10 * T], bf(C_Y + 16 * T, 6 * T))
    wait("scalar", V, C_YACC_V[LG, 15])
    ysp_c = dma("scalar", s_ysp[2], yffn_dr[:, 10 * T:16 * T], bf(C_Y + 40 * T, 6 * T))

    for h in range(NH):
        pr = h % 2
        vpr = (h + 1) % 2
        if h == 0:
            wait("sync", TE, C_GLAST[LG])
            wait("sync", g_cc, CC_V[0])
        if h % 4 == 0 and h > 0:
            wait("sync", g_cc, CC_V[h // 4])
        if h >= 2:
            wait("sync", pvsem, (h - 1) * KC)
        ktg = kT_gatQ[h // 4]
        vg = v_gatQ[h // 4]
        hh = h % 4
        if h == 1:
            wait("sync", TE, C_ULAST[LG])
        for r in range(C):
            dma("sync", s_kv[pr], bb(E_KH[pr] + 2 * r * T, T),
                ktg[r * (D // NQ) + hh * 128:r * (D // NQ) + (hh + 1) * 128, :])
        if h == 0:
            wait("sync", s_ysp[0], ysp_a)
        if h == 1:
            wait("sync", TE, C_TDOWN[LG])
        for r in range(C):
            dma("sync", s_kv[pr],
                bb(E_VH[vpr] + 2 * r * T, T).rearrange("p (kc d) -> p kc d", d=128),
                vg[r * T:(r + 1) * T, hh * 128:(hh + 1) * 128]
                .rearrange("(kc p) d -> p kc d", p=128))

    slot_wait = [(expsem, 0), (expsem, 0), (expsem, 0)]
    eslot = [(dacc, 0)] * 4
    pb_rd = [(dacc, 0), (pacc, 0)]
    FIN_V = {}
    COMB_V = {}
    PACC_AT = {}
    DACC_AT = {}

    def emit_combine(fh):
        wait("vector", pacc, PACC_AT[fh])
        wait("vector", dacc, DACC_AT[fh])
        em("vector", lambda e, oo=bf(E_EA, T), a=bf(E_EA, T), b=bf(E_EA2, T):
           e.tensor_tensor(oo, a, b, ALU.add).then_inc(V.h, 1))
        COMB_V[fh] = V.bump()

    DEN_T = {}
    DEN_SLOT = {}

    def emit_den(fh, slot):
        wsem, wval = slot_wait[slot]
        wait("tensor", wsem, wval)
        wait("tensor", V, COMB_V[fh])
        ppD = P[slot]
        tiles = [(ppD[:, f * NW:(f + 1) * NW], ones_f32,
                  bf(E_EA + 4 * f * NW, NW), True, True) for f in range(NF)]
        DEN_T[fh] = mm_group(None, tiles, TE)
        DEN_SLOT[fh] = slot

    def emit_rcp(fh):
        slot = DEN_SLOT[fh]
        wait("scalar", TE, DEN_T[fh])
        if fh >= 1:
            wait("scalar", V, FIN_V[fh - 1])
        em("scalar", lambda e, oo=bf(O_RCPA, T), i=P[slot][:, 0:T]:
           e.activation(oo, i, AF.Ln).then_inc(SC.h, 1))
        sc_ln = SC.bump()
        slot_wait[slot] = (SC, sc_ln)
        wait("scalar", SC, sc_ln)
        em("scalar", lambda e, oo=bf(O_RCPB, T), i=bf(O_RCPA, T):
           e.activation(oo, i, AF.Exp, scale=-1.0).then_inc(SC.h, 1))
        sc_rcp = SC.bump()
        wait("vector", SC, sc_rcp)
        wait("vector", vfin, fh + 1)
        em("vector", lambda e, oo=bb(E_AO + 2 * fh * T, T), a=bb(E_AO + 2 * fh * T, T),
           b=bf(O_RCPB, T): e.tensor_mul(oo, a, b).then_inc(V.h, 1))
        FIN_V[fh] = V.bump()

    for h in range(NH):
        pr = h % 2
        g_end_prev = h * KC
        kv_target = 16 * (2 * C) * (h // 2 + 1)
        wait("tensor", s_kv[pr], kv_target)
        if h == 0:
            wait("tensor", V, FFN_LAST_V)
            wait("tensor", SC, C_END_SC)
            wait("tensor", g_ms, ONESF_OK)
            wait("scalar", g_kn, cngld)
            wait("scalar", s_ysp[1], ysp_b)
            wait("vector", s_ysp[1], ysp_b)
            wait("gpsimd", s_ysp[1], ysp_b)
        for kc in range(KC):
            g = h * KC + kc
            wsem, wval = slot_wait[g % 3]
            wait("tensor", wsem, wval)
            pp = P[g % 3]
            tiles = [(pp[:, f * NW:(f + 1) * NW],
                      bb(E_KH[pr] + 2 * kc * 128, 128),
                      bb(O_QTT + 2 * (h * T + f * NW), NW), True, True)
                     for f in range(NF)]
            mm_group(None, tiles, scsem)
            slot_wait[g % 3] = (expsem, g + 1)
            wait("scalar", scsem, g + 1)
            wait("scalar", pvsem, g - 3)
            if g >= 4:
                esem, ev = eslot[g % 4]
                wait("scalar", esem, ev)
            em("scalar", lambda e, oo=bb(E_E[g % 4], T), i=pp[:, 0:T],
               b=bf(O_CNEG + 4 * h, 1), ssc=scale:
               e.activation(oo, i, AF.Exp, bias=b, scale=ssc).then_inc(expsem.h, 1))
            expsem.bump()
            if h >= 1 and kc == 1:
                emit_combine(h - 1)
            if h >= 1 and kc == 3:
                emit_den(h - 1, g_end_prev % 3)
            if h >= 1 and kc == 5:
                emit_rcp(h - 1)
            if kc % 2 == 1:
                j = kc // 2
                on_pool = j % 4 == 0
                wait("vector", expsem, g + 1)
                if on_pool and j >= 4:
                    wait("vector", pacc, pb_rd[0][1])
                if not on_pool and j >= 2:
                    wait("vector", dacc, pb_rd[1][1])
                em("vector", lambda e, oo=bb(E_PB[0 if on_pool else 1], T),
                   a=bb(E_E[(g - 1) % 4], T),
                   bbb=bb(E_E[g % 4], T), hh=dacc.h:
                   e.tensor_tensor(oo, a, bbb, ALU.add).then_inc(hh, 1))
                pj = dacc.bump()
                eslot[(g - 1) % 4] = (dacc, pj)
                eslot[g % 4] = (dacc, pj)
                if not on_pool:
                    wait("vector", dacc, pj)
                    if j == 1:
                        if h > 0:
                            wait("vector", TE, DEN_T[h - 1])
                        em("vector", lambda e, oo=bf(E_EA, T), i=bb(E_PB[1], T), hh=dacc.h:
                           e.tensor_copy(oo, i).then_inc(hh, 1))
                    else:
                        em("vector", lambda e, oo=bf(E_EA, T), a=bf(E_EA, T),
                           i=bb(E_PB[1], T), hh=dacc.h:
                           e.tensor_tensor(oo, a, i, ALU.add).then_inc(hh, 1))
                    pb_rd[1] = (dacc, dacc.bump())
                else:
                    wait("gpsimd", dacc, pj)
                    if j == 0:
                        if h > 0:
                            wait("gpsimd", V, COMB_V[h - 1])
                        em("gpsimd", lambda e, oo=bf(E_EA2, T), i=bb(E_PB[0], T), hh=pacc.h:
                           e.tensor_copy(oo, i).then_inc(hh, 1))
                    else:
                        em("gpsimd", lambda e, oo=bf(E_EA2, T), a=bf(E_EA2, T),
                           i=bb(E_PB[0], T), hh=pacc.h:
                           e.tensor_tensor(oo, a, i, ALU.add).then_inc(hh, 1))
                    pb_rd[0] = (pacc, pacc.bump())
            def emit_pv(gg, kk):
                wait("tensor", expsem, gg + 1)
                if kk == 0:
                    wait("tensor", vfin, h)
                tiles = [(P[3][:, f * NW:(f + 1) * NW],
                          bb(E_VH[(h + 1) % 2] + 2 * kk * 128, 128),
                          bb(E_E[gg % 4] + 2 * f * NW, NW),
                          kk == 0, kk == KC - 1) for f in range(NF)]
                mm_group(None, tiles, pvsem)
            if kc >= 2:
                emit_pv(g - 2, kc - 2)
        emit_pv(h * KC + KC - 2, KC - 2)
        emit_pv(h * KC + KC - 1, KC - 1)
        PACC_AT[h] = pacc.n
        DACC_AT[h] = dacc.n
        wait("vector", pvsem, (h + 1) * KC)
        if h == 0:
            wait("vector", s_ysp[2], ysp_c)
        em("vector", lambda e, oo=bb(E_AO + 2 * h * T, T), i=P[3][:, 0:T]:
           e.tensor_copy(oo, i).then_inc(vfin.h, 1))
        vfin.bump()
        if h == NH - 1:
            emit_combine(h)
            emit_den(h, ((h + 1) * KC) % 3)
            emit_rcp(h)
    ATT_DONE = NH * KC

    wait("gpsimd", V, FIN_V[NH - 1])
    load_wo(1)

    F_EV = {}
    XLD = {}
    PADD_V = {}
    for cc in range(DC):
        sidx = cc // 2
        pr = sidx % 2
        if cc == 0:
            wait("sync", pvsem, ATT_DONE)
            wait("sync", V, FIN_V[NH - 1])
            wait("sync", s_ysp[2], ysp_c)
        if cc >= 2:
            wait("sync", V, F_EV[cc - 2])
        ybl = dma("sync", s_yb[cc % 2], bf(F_YB[cc % 2], T), yffn_dr[:, cc * T:(cc + 1) * T])
        if cc >= 2:
            wait("scalar", V, PADD_V[cc - 2])
        if cc >= 4:
            wait("scalar", V, F_EV[cc - 4])
        XLD[cc] = dma("scalar", s_xr[cc % 2], bf(F_XR[cc % 2] + 4 * ((cc // 2) % 2) * T, T),
                      xT_d[cc * 128:(cc + 1) * 128, :])
        wait("tensor", g_wo[pr], 16 * (sidx // 2 + 1))
        if cc == 0:
            wait("tensor", V, FIN_V[NH - 1])
        if cc >= 2:
            wait("tensor", V, F_EV[cc - 2])
        pp = P[2 + (cc % 2)]
        tiles = []
        for dc in range(DC):
            w = bb(F_WO[pr] + 2 * (dc * 2 * 128 + (cc - sidx * 2) * 128), 128)
            for f in range(NF):
                tiles.append((pp[:, f * NW:(f + 1) * NW], w,
                              bb(E_AO + 2 * (dc * T + f * NW), NW),
                              dc == 0, dc == DC - 1))
        t_wo = mm_group(None, tiles, TE)
        if (cc % 2 == 1 or cc == DC - 1) and sidx + 2 < NSL:
            wait("gpsimd", TE, t_wo)
            load_wo(sidx + 2)
        wait("vector", s_yb[cc % 2], ybl)
        wait("vector", s_xr[cc % 2], XLD[cc])
        em("vector", lambda e, oo=bf(F_YB[cc % 2], T), a=bf(F_YB[cc % 2], T),
           b=bf(F_XR[cc % 2] + 4 * ((cc // 2) % 2) * T, T):
           e.tensor_add(oo, a, b).then_inc(V.h, 1))
        PADD_V[cc] = V.bump()
        wait("vector", TE, t_wo)
        wait("vector", V, PADD_V[cc])
        if cc >= 4:
            wait("vector", s_out[cc % 4], 16 * (cc // 4))
        em("vector", lambda e, oo=bf(F_OS[cc % 2] + 4 * ((cc // 2) % 2) * T, T),
           i=pp[:, 0:T], b=bf(O_BO + 4 * cc, 1), y=bf(F_YB[cc % 2], T):
           e.scalar_tensor_tensor(oo, i, b, y, ALU.add, ALU.add).then_inc(V.h, 1))
        F_EV[cc] = V.bump()
        wait("sync", V, F_EV[cc])
        dma("sync", s_out[cc % 4], out_d[cc * 128:(cc + 1) * 128, :],
            bf(F_OS[cc % 2] + 4 * ((cc // 2) % 2) * T, T))

    for so in s_out:
        wait("sync", so, so.n)

    for s, v in max_waited.items():
        assert v <= s.n, f"wait beyond final count on sem {s.name}: waited {v}, final {s.n}"

    with nc.Block() as block:
        @block.sync
        def _(e):
            for f in ops["sync"]:
                f(e)

        @block.gpsimd
        def _(e):
            for f in ops["gpsimd"]:
                f(e)

        @block.tensor
        def _(e):
            for f in ops["tensor"]:
                f(e)

        @block.scalar
        def _(e):
            for f in ops["scalar"]:
                f(e)

        @block.vector
        def _(e):
            for f in ops["vector"]:
                f(e)

    return nc


def _host_inputs(cfg, inputs):
    S, D, T, C = cfg.S, cfg.D, cfg.T, cfg.C
    f32 = np.float32

    def t2(a):
        return np.ascontiguousarray(np.asarray(a, dtype=f32).T.astype(BF))

    def b2(b, nch):
        return np.ascontiguousarray(np.asarray(b, dtype=f32).reshape(nch, 128).T)

    x = np.ascontiguousarray(np.asarray(inputs["L0_attn_hidden_states"], dtype=f32).reshape(S, D))

    shared = dict(
        wqdT=t2(inputs["L0_attn_Wq_down"]), bqd2=b2(inputs["L0_attn_bq_down"], cfg.LC),
        wkvdT=t2(inputs["L0_attn_Wkv_down"]), bkv2=b2(inputs["L0_attn_bkv_down"], cfg.LC),
        wquT=t2(inputs["L0_attn_Wq_up"]), bqu2=b2(inputs["L0_attn_bq_up"], cfg.DC),
        wknT=t2(inputs["L0_attn_Wk_nope"]), bkn2=b2(inputs["L0_attn_bk_nope"], cfg.NC2),
        wkrT=t2(inputs["L0_attn_Wk_rope"]), bkr2=b2(inputs["L0_attn_bk_rope"], 2 * cfg.JH),
        wvuT=t2(inputs["L0_attn_Wv_up"]), bvu2=b2(inputs["L0_attn_bv_up"], cfg.DC),
        woT=t2(inputs["L0_attn_Wo"]), bo2=b2(inputs["L0_attn_bo"], cfg.DC),
        routerT=t2(inputs["L0_ffn_router_weight"]),
        wgT=t2(inputs["L0_ffn_sh_gate"]), wuT=t2(inputs["L0_ffn_sh_up"]),
        wdT=t2(inputs["L0_ffn_sh_down"]),
        rgT=t2(inputs["L0_ffn_r_gate"]), ruT=t2(inputs["L0_ffn_r_up"]),
        rdT=t2(inputs["L0_ffn_r_down"]),
    )

    ROT = D // 2
    inv = (1.0 / (np.float32(cfg.base) ** (np.arange(0, ROT, 2, dtype=f32) / f32(ROT)))).astype(f32)
    tpos = np.arange(S, dtype=f32)
    freqs = np.outer(tpos, inv).astype(f32)
    cosT = np.ascontiguousarray(np.cos(freqs).astype(f32).T)
    sinT = np.ascontiguousarray(np.sin(freqs).astype(f32).T)

    in_maps = []
    for c in range(C):
        m = dict(shared)
        m["x"] = np.ascontiguousarray(x[c * T:(c + 1) * T])
        m["xT"] = np.ascontiguousarray(x[c * T:(c + 1) * T].T)
        m["cosT"] = np.ascontiguousarray(cosT[:, c * T:(c + 1) * T])
        m["sinT"] = np.ascontiguousarray(sinT[:, c * T:(c + 1) * T])
        in_maps.append(m)
    return in_maps


def kernel(**inputs):
    cfg = CFG
    nc = build(cfg)
    in_maps = _host_inputs(cfg, inputs)
    res = run_bass_kernel_spmd(nc, in_maps, core_ids=list(range(cfg.C)))
    out = np.concatenate(
        [np.ascontiguousarray(np.asarray(r["out"], dtype=np.float32).T) for r in res.results],
        axis=0)
    return out.reshape(1, cfg.S, cfg.D)
